# version 12
# baseline (speedup 1.0000x reference)
import numpy as np
import ml_dtypes
import concourse.bass as bass
import concourse.mybir as mybir
from concourse.bass_utils import run_bass_kernel_spmd

F32 = mybir.dt.float32
BF16 = mybir.dt.bfloat16
I32 = mybir.dt.int32
U16 = mybir.dt.uint16
AF = mybir.ActivationFunctionType
ALU = mybir.AluOpType
bfdt = ml_dtypes.bfloat16

NCORES = 8
D = 4096
SEQ = 16384
T = SEQ // NCORES
INTER = 11008
NG = 2 * INTER
NJ = INTER // 256
SW = 256
MT = 2 * NJ
KT = D // 128
HALF = T // 2
EPS = 1e-5
SBUF_BASE = 16512
SBUF_TOP = 229300


def _pos2dense(n):
    p = np.arange(n)
    j, rem = p // SW, p % SW
    i, r = rem // 64, rem % 64
    return j * SW + 4 * r + i


class Alloc:
    def __init__(self, nc, base, top=SBUF_TOP):
        self.nc, self.off, self.top = nc, base, top

    def take(self, name, shape, dtype, align=64):
        esz = {F32: 4, BF16: 2, I32: 4, U16: 2}[dtype]
        self.off = -(-self.off // align) * align
        h = self.nc.alloc_sbuf_tensor_at(name, list(shape), dtype, offset=self.off)
        self.off += shape[-1] * esz
        assert self.off <= self.top, f"SBUF overflow at {name}: {self.off}"
        return h


class Sem:
    __slots__ = ("h", "n")

    def __init__(self, h):
        self.h, self.n = h, 0

    def inc(self, by=1):
        self.n += by
        return self.n


def _pre(e, pre):
    for s, v in pre:
        if v > 0:
            e.wait_ge(s.h, v)


class B:
    ENGS = ("tensor", "vector", "scalar", "sync", "gpsimd")

    def __init__(self, nc):
        self.nc = nc
        self.ops = {e: [] for e in self.ENGS}
        self._si = 0
        self.rdy = [self.sem() for _ in range(8)]
        self.fre = [self.sem() for _ in range(8)]
        self.uses = [0] * 8
        self.v_ms = self.sem()
        self.a_ms = self.sem()
        self.g_ms = self.sem()
        self.t_ms = self.sem()
        self.bar = self.sem()
        self.qstreams = {"sync": [], "scalar": [], "gpsimd": []}

    def stream(self, queue):
        s = self.sem()
        self.qstreams[queue].append(s)
        return s

    def sem(self):
        h = self.nc.semaphore(f"sm{self._si}").__enter__()
        self._si += 1
        return Sem(h)

    def emit(self, eng, fn):
        self.ops[eng].append(fn)

    def wait(self, eng, sem, val):
        if val > 0:
            self.ops[eng].append(lambda e, s=sem, v=val: e.wait_ge(s.h, v))

    def group(self, bnk, mms, pre=()):
        wf = self.uses[bnk]
        self.uses[bnk] += 1
        self.rdy[bnk].n += 1
        rv = self.rdy[bnk].n

        def run(t, mms=mms, bnk=bnk, wf=wf, pre=tuple(pre)):
            _pre(t, pre)
            if wf > 0:
                t.wait_ge(self.fre[bnk].h, wf)
            n = len(mms)
            for i, m in enumerate(mms):
                inst = m(t, i == 0, i == n - 1)
                if i == n - 1:
                    inst.then_inc(self.rdy[bnk].h, 1)
        self.emit("tensor", run)
        return rv

    def consume(self, eng, bnk, rv, fn, pre=()):
        self.fre[bnk].n += 1

        def run(e, bnk=bnk, rv=rv, fn=fn, pre=tuple(pre)):
            _pre(e, pre)
            e.wait_ge(self.rdy[bnk].h, rv)
            fn(e).then_inc(self.fre[bnk].h, 1)
        self.emit(eng, run)
        return self.fre[bnk].n

    def dma(self, queue, out, in_, sem, pre=(), **kw):
        v = sem.inc(16)

        def run(e, out=out, in_=in_, kw=kw, sem=sem, pre=tuple(pre)):
            _pre(e, pre)
            e.dma_start(out=out, in_=in_, **kw).then_inc(sem.h, 16)
        self.emit(queue, run)
        return v

    def vop(self, fn, pre=()):
        self.v_ms.n += 1

        def run(v, fn=fn, pre=tuple(pre)):
            _pre(v, pre)
            fn(v).then_inc(self.v_ms.h, 1)
        self.emit("vector", run)
        return self.v_ms.n

    def gop(self, fn, pre=()):
        self.g_ms.n += 1

        def run(g, fn=fn, pre=tuple(pre)):
            _pre(g, pre)
            fn(g).then_inc(self.g_ms.h, 1)
        self.emit("gpsimd", run)
        return self.g_ms.n

    def aop(self, fn, pre=()):
        self.a_ms.n += 1

        def run(a, fn=fn, pre=tuple(pre)):
            _pre(a, pre)
            fn(a).then_inc(self.a_ms.h, 1)
        self.emit("scalar", run)
        return self.a_ms.n

    def tmark(self):
        self.t_ms.n += 1
        self.emit("tensor", lambda t: t.nop().then_inc(self.t_ms.h, 1))
        return self.t_ms.n

    def barrier(self):
        tgt = self.bar.n + 5
        self.bar.n = tgt
        snap = {q: [(s, s.n) for s in ss if s.n > 0]
                for q, ss in self.qstreams.items()}
        for eng in self.ENGS:
            def run(e, eng=eng, snap=snap):
                for s, v in snap.get(eng, ()):
                    e.wait_ge(s.h, v)
                if eng in ("vector", "gpsimd"):
                    e.engine_nop().then_inc(self.bar.h, 1)
                else:
                    e.nop().then_inc(self.bar.h, 1)
            self.emit(eng, run)
        for eng in self.ENGS:
            self.wait(eng, self.bar, tgt)


def build_program(nz_of_t, n_mask, upto="full", dbg=()):
    K_UPTO_GATE = [False]
    nc = bass.Bass("TRN2", target_bir_lowering=False, debug=False)
    nc.detect_race_conditions = False

    dx = nc.dram_tensor("x", [T, D], F32, kind="ExternalInput").ap()
    dw1 = nc.dram_tensor("w1", [D, D], F32, kind="ExternalInput").ap()
    dw2 = nc.dram_tensor("w2", [D, D], F32, kind="ExternalInput").ap()
    db2 = nc.dram_tensor("b2", [D], F32, kind="ExternalInput").ap()
    dmask = nc.dram_tensor("mask_nz", [max(n_mask, 1) * 128, 128], F32,
                           kind="ExternalInput").ap()
    dqw_gu = nc.dram_tensor("qw_gu", [D, NG // 8], I32, kind="ExternalInput").ap()
    dqz_gu = nc.dram_tensor("qz_gu", [KT, NG // 8], I32, kind="ExternalInput").ap()
    ds_gu = nc.dram_tensor("s_gu", [KT, NG], F32, kind="ExternalInput").ap()
    dqw_dn = nc.dram_tensor("qw_dn", [INTER, D // 8], I32, kind="ExternalInput").ap()
    dqz_dn = nc.dram_tensor("qz_dn", [MT, D // 8], I32, kind="ExternalInput").ap()
    ds_dn = nc.dram_tensor("s_dn", [MT, D], F32, kind="ExternalInput").ap()
    didf = nc.dram_tensor("identf", [128, 128], F32, kind="ExternalInput").ap()
    dones = nc.dram_tensor("ones_col", [128, 1], BF16, kind="ExternalInput").ap()
    dones1b = nc.dram_tensor("ones_row_b", [1, 128], BF16, kind="ExternalInput").ap()
    dones1f = nc.dram_tensor("ones_row_f", [1, 128], F32, kind="ExternalInput").ap()
    dind_gu = nc.dram_tensor("ind_gu", [128, KT * KT], BF16, kind="ExternalInput").ap()
    dind_dn = nc.dram_tensor("ind_dn", [128, NJ * MT], BF16, kind="ExternalInput").ap()
    dstair2 = nc.dram_tensor("stair2", [2, 128], BF16, kind="ExternalInput").ap()
    dout = nc.dram_tensor("out", [T, D], F32, kind="ExternalOutput").ap()

    dxT = [nc.dram_tensor(f"xT{i}", [D, T], F32, kind="Internal").ap()
           for i in range(4)]
    dxn = nc.dram_tensor("xn_sc", [D, T], BF16, kind="Internal").ap()
    dguc = nc.dram_tensor("gu_cache", [2 * NJ * 128, KT * SW], BF16, kind="Internal").ap()
    ddnc = nc.dram_tensor("dn_cache", [MT * 8 * 128, 512], BF16, kind="Internal").ap()
    dh = nc.dram_tensor("h_sc", [MT * 4 * 128, 512], BF16, kind="Internal").ap()
    dzs_gu = nc.dram_tensor("zs_gu_sc", [KT, NG], BF16, kind="Internal").ap()
    dzs_dn = nc.dram_tensor("zs_dn_sc", [MT, D], BF16, kind="Internal").ap()
    dw2bf = nc.dram_tensor("w2bf", [D, D], BF16, kind="Internal").ap()
    dw1c = nc.dram_tensor("w1c", [32 * 128, KT * 128], BF16, kind="Internal").ap()

    pb = [nc.psum_tensor(f"pbank{i}", [128, 512], F32).__enter__() for i in range(8)]
    b = B(nc)

    al = Alloc(nc, SBUF_BASE)
    sb_idf = al.take("idf", [128, 128], F32)
    sb_ones = al.take("onesc", [128, 1], BF16)
    sb_ones1b = al.take("ones1b", [1, 128], BF16)
    sb_ones1f = al.take("ones1f", [1, 128], F32)
    sb_indgu = al.take("indgu", [128, KT * KT], BF16)
    sb_stair2 = al.take("stair2", [2, 128], BF16)
    sb_b2 = al.take("b2t", [128, 32], F32)
    PERSIST = al.off

    s_const = b.stream("sync")
    s_w2bf = b.stream("gpsimd")
    for out, in_ in ((sb_idf[:], didf), (sb_ones[:], dones),
                     (sb_ones1b[:], dones1b), (sb_ones1f[:], dones1f),
                     (sb_indgu[:], dind_gu), (sb_stair2[:], dstair2)):
        b.dma("sync", out, in_, s_const)
    b.dma("sync", sb_b2[:], db2.rearrange("(j p) -> p j", p=128), s_const,
          allow_slow_non_contiguous=True)
    for r in range(4):
        b.dma("gpsimd",
              dw2bf[1024 * r:1024 * (r + 1), :].rearrange("(a p) c -> p a c", p=128),
              dw2[1024 * r:1024 * (r + 1), :].rearrange("(a p) c -> p a c", p=128),
              s_w2bf)
    w2bf_done = s_w2bf.n
    w1cv = dw1c.rearrange("(st p) (dt c) -> st p dt c", p=128, c=128)
    for st in range(32):
        b.dma("gpsimd", w1cv[st, :, :, :],
              dw1[:, 128 * st:128 * (st + 1)].rearrange("(dt p) c -> p dt c", p=128),
              s_w2bf)
    w1c_done = s_w2bf.n
    for eng in ("tensor", "vector", "scalar"):
        b.wait(eng, s_const, s_const.n)

    def emit_tap(dsrc_bytes_ap, nrows, rowlen_f32):
        lo = Alloc(nc, PERSIST)
        tbuf = lo.take("tapbuf", [128, 2048], F32)
        s_tl = b.stream("sync")
        s_tw = b.stream("gpsimd")
        dflat = dout.rearrange("a c -> (a c)")
        for i in range(0, nrows, 128):
            n = min(128, nrows - i)
            wv = b.dma("sync", tbuf[:n, :rowlen_f32], dsrc_bytes_ap[i:i + n, :],
                       s_tl, pre=[(s_tw, s_tw.n)])
            vv = b.vop(lambda v: v.engine_nop(), pre=[(s_tl, wv)])
            seg = dflat[i * rowlen_f32:(i + n) * rowlen_f32]
            b.dma("gpsimd", seg.rearrange("(p f) -> p f", f=rowlen_f32),
                  tbuf[:n, :rowlen_f32], s_tw, pre=[(b.v_ms, vv)])

    def finish():
        b.barrier()
        with nc.Block() as blk:
            for eng in B.ENGS:
                getattr(blk, eng)(lambda e, eng=eng: [f(e) for f in b.ops[eng]])
        return nc

    p0 = Alloc(nc, PERSIST)
    sb_xh = [p0.take(f"p0xh{i}", [128, D], F32) for i in range(4)]
    sb_xts = [p0.take(f"p0xts{i}", [128, 512], F32) for i in range(2)]
    sb_qzg = p0.take("p0qzg", [KT, NG // 8], I32)
    sb_zu = p0.take("p0zu", [KT, 4096], U16)
    sb_zb = p0.take("p0zb", [KT, 4096], BF16)
    sb_qzd = p0.take("p0qzd", [MT, D // 8], I32)
    sb_zud = p0.take("p0zud", [MT, D], U16)
    sb_zbd = p0.take("p0zbd", [MT, D], BF16)

    s_px = b.stream("gpsimd")
    s_pw = [b.stream("gpsimd") for _ in range(2)]
    xts_guard = [0, 0]
    for tg in range(4):
        wv = [b.dma("gpsimd", sb_xh[i][:],
                    dx[(4 * tg + i) * 128:(4 * tg + i + 1) * 128, :], s_px)
              for i in range(4)]
        for dt in range(KT):
            bnk = dt % 2
            wf = b.uses[bnk]
            b.uses[bnk] += 1
            b.rdy[bnk].n += 1
            rv = b.rdy[bnk].n

            def trun(t, dt=dt, bnk=bnk, wf=wf, w=wv[3]):
                t.wait_ge(s_px.h, w)
                if wf > 0:
                    t.wait_ge(b.fre[bnk].h, wf)
                for i in range(4):
                    inst = t.transpose(pb[bnk][:, 128 * i:128 * (i + 1)],
                                       sb_xh[i][:, 128 * dt:128 * (dt + 1)],
                                       sb_idf[:])
                    if i == 3:
                        inst.then_inc(b.rdy[bnk].h, 1)
            b.emit("tensor", trun)
            st = sb_xts[dt % 2]
            fv = b.consume("vector", bnk, rv,
                           lambda v, st=st, bnk=bnk: v.tensor_copy(st[:], pb[bnk][:]),
                           pre=[(s_pw[dt % 2], xts_guard[dt % 2])])
            b.dma("gpsimd", dxT[0][128 * dt:128 * (dt + 1), 512 * tg:512 * (tg + 1)],
                  st[:], s_pw[dt % 2], pre=[(b.fre[bnk], fv)])
            xts_guard[dt % 2] = s_pw[dt % 2].n

    s_zs = b.stream("sync")
    wq = b.dma("sync", sb_qzg[:], dqz_gu, s_zs)
    zb_guard = 0
    for ch in range(6):
        c0 = 4096 * ch
        w = min(4096, NG - c0)

        def zrun(v, c0=c0, w=w):
            qz = sb_qzg[:].bitcast(U16)
            for o in range(0, w, SW):
                u0 = ((c0 + o) // SW) * 64
                for i in range(4):
                    v.tensor_scalar(sb_zu[:, o + 64 * i:o + 64 * (i + 1)],
                                    qz[:, u0:u0 + 64], 4 * i, 15,
                                    ALU.logical_shift_right, ALU.bitwise_and)
            return v.tensor_copy(sb_zb[:, :w], sb_zu[:, :w])
        vv = b.vop(zrun, pre=[(s_zs, max(wq, zb_guard))])
        b.dma("sync", dzs_gu[:, c0:c0 + w], sb_zb[:, :w], s_zs, pre=[(b.v_ms, vv)])
        zb_guard = s_zs.n

    wq = b.dma("sync", sb_qzd[:], dqz_dn, s_zs)

    def zdrun(v):
        qz = sb_qzd[:].bitcast(U16)
        for o in range(0, D, SW):
            u0 = (o // SW) * 64
            for i in range(4):
                v.tensor_scalar(sb_zud[:, o + 64 * i:o + 64 * (i + 1)],
                                qz[:, u0:u0 + 64], 4 * i, 15,
                                ALU.logical_shift_right, ALU.bitwise_and)
        return v.tensor_copy(sb_zbd[:], sb_zud[:])
    vv = b.vop(zdrun, pre=[(s_zs, wq)])
    b.dma("sync", dzs_dn[:], sb_zbd[:], s_zs, pre=[(b.v_ms, vv)])

    b.barrier()

    if upto == "p0":
        emit_tap(dxT[0], D, T)
        return finish()

    p1 = Alloc(nc, PERSIST)
    sb_qw = [p1.take(f"dqqw{i}", [128, NG // 8], I32) for i in range(2)]
    sb_srow = [p1.take(f"dqsr{i}", [2, 8192], BF16) for i in range(2)]
    sb_sbc = [p1.take(f"dqsbc{i}", [128, 1024], BF16) for i in range(8)]
    sb_nib = [p1.take(f"dqnib{i}", [128, 512], U16) for i in range(2)]
    sb_nbb = p1.take("dqnbb", [128, 512], BF16)
    sb_wst = [p1.take(f"dqwst{i}", [128, 2048], BF16) for i in range(2)]

    s_qw = [b.stream("sync") for _ in range(2)]
    s_sr = [b.stream("gpsimd") for _ in range(2)]
    s_wc = [b.stream("sync") for _ in range(2)]
    qw_guard = [0, 0]
    srow_guard = [(0, 0), (0, 0)]
    sbc_guard = [0] * 8
    nib_guard = [0, 0]
    wst_guard = [0, 0]
    subctr = [0]

    def emit_dequant(dqw, dcache, n_i32, n_pos, dsrows, dzrows, n_tiles, dn_mode):
        qsel = (dqw.rearrange("(j r s) c -> j s r c", r=64, s=4) if dn_mode else None)
        for g in range(n_tiles):
            qs = sb_qw[g % 2]
            if dn_mode:
                j, q = g // 2, g % 2
                wqv = b.dma("sync", qs[:, :n_i32],
                            qsel[j, 2 * q:2 * q + 2, :, :],
                            s_qw[g % 2], pre=[(b.v_ms, qw_guard[g % 2])])
            else:
                wqv = b.dma("sync", qs[:, :n_i32], dqw[128 * g:128 * (g + 1), :],
                            s_qw[g % 2], pre=[(b.v_ms, qw_guard[g % 2])])
            for ch in range(0, n_pos, 2048):
                w = min(2048, n_pos - ch)
                chi = (ch // 2048) % 2
                sr = sb_srow[chi]
                gb, gv = srow_guard[chi]
                nrow = 2 if dn_mode else 1
                g0 = (g // 2) * 2 if dn_mode else g
                srv = sr[:nrow, :].rearrange("a (u two c) -> a u two c", two=2, c=512)
                wsv = b.dma("gpsimd", srv[:, :w // 512, 0, :],
                            dsrows[g0:g0 + nrow, ch:ch + w]
                            .rearrange("a (u c) -> a u c", c=512),
                            s_sr[chi], pre=[(b.rdy[gb], gv)])
                wzv = b.dma("gpsimd", srv[:, :w // 512, 1, :],
                            dzrows[g0:g0 + nrow, ch:ch + w]
                            .rearrange("a (u c) -> a u c", c=512),
                            s_sr[chi])
                wstg = sb_wst[chi]
                last_rv = (6, 0)
                for sub in range(0, w, 512):
                    si = subctr[0] % 8
                    bnk = (2 * subctr[0]) % 8
                    bnkz = bnk + 1
                    subctr[0] += 1
                    sbc = sb_sbc[si]
                    lhsb = sb_stair2[:] if dn_mode else sb_ones1b[:]
                    rv = b.group(bnk, [
                        (lambda t, s, e, sr=sr, sub=sub, bnk=bnk, lhsb=lhsb,
                         nrow=nrow:
                         t.matmul(pb[bnk][:], lhsb,
                                  sr[:nrow, 2 * sub:2 * sub + 512],
                                  start=s, stop=e))],
                        pre=[(s_sr[chi], wzv)])
                    rvz = b.group(bnkz, [
                        (lambda t, s, e, sr=sr, sub=sub, bnkz=bnkz, lhsb=lhsb,
                         nrow=nrow:
                         t.matmul(pb[bnkz][:], lhsb,
                                  sr[:nrow, 2 * sub + 512:2 * sub + 1024],
                                  start=s, stop=e))])
                    last_rv = (bnkz, rvz)
                    fv = b.consume("scalar", bnk, rv,
                                   lambda a, sbc=sbc, bnk=bnk:
                                   a.activation(sbc[:, :512], pb[bnk][:],
                                                AF.Identity),
                                   pre=[(b.v_ms, sbc_guard[si])])
                    fvz = b.consume("scalar", bnkz, rvz,
                                    lambda a, sbc=sbc, bnkz=bnkz:
                                    a.activation(sbc[:, 512:], pb[bnkz][:],
                                                 AF.Identity))

                    nslot = subctr[0] % 2
                    nib = sb_nib[nslot]

                    def drun(v, qs=qs, sbc=sbc, pos=ch + sub, sub=sub,
                             nib=nib, n_i32=n_i32, wstg=wstg):
                        qu = qs[:, :n_i32].bitcast(U16)
                        for jj in range(2):
                            u0 = ((pos + SW * jj) // SW) * 64
                            for i in range(4):
                                v.tensor_scalar(
                                    nib[:, SW * jj + 64 * i:SW * jj + 64 * (i + 1)],
                                    qu[:, u0:u0 + 64], 4 * i, 15,
                                    ALU.logical_shift_right, ALU.bitwise_and)
                        zbv = sbc[:].rearrange("p (u two c) -> p u two c",
                                               two=2, c=512)
                        v.tensor_tensor(sb_nbb[:], nib[:], zbv[:, 0, 1, :],
                                        ALU.subtract)
                        return v.tensor_tensor(wstg[:, sub:sub + 512],
                                               sb_nbb[:], zbv[:, 0, 0, :], ALU.mult)
                    vv = b.vop(drun, pre=[(s_qw[g % 2], wqv),
                                          (s_wc[chi], wst_guard[chi]),
                                          (b.fre[bnk], fv), (b.fre[bnkz], fvz)])
                    sbc_guard[si] = vv
                srow_guard[chi] = last_rv
                vvw = b.v_ms.n
                if not dn_mode:
                    g4 = dcache.rearrange("(s p) (kt c) -> s p kt c", p=128, kt=KT)
                    for o in range(0, w, SW):
                        s_idx = (ch + o) // SW
                        b.dma("sync", g4[s_idx, :, g, :],
                              wstg[:, o:o + SW], s_wc[chi],
                              pre=([(b.v_ms, vvw)] if o == 0 else ()))
                else:
                    d4 = dcache.rearrange("(mt oc p) c -> mt oc p c", oc=8, p=128)
                    for o in range(0, w, 512):
                        b.dma("sync", d4[g, (ch + o) // 512, :, :],
                              wstg[:, o:o + 512], s_wc[chi],
                              pre=([(b.v_ms, vvw)] if o == 0 else ()))
                wst_guard[chi] = s_wc[chi].n
            qw_guard[g % 2] = b.v_ms.n

    s_lx = b.stream("gpsimd")
    s_lw = b.stream("gpsimd")

    def emit_ln(dsrc, base=None):
        lo = Alloc(nc, base if base is not None else PERSIST)
        xh = lo.take("lnx", [128, KT * 512], F32)
        xball = lo.take("lnxb", [128, KT * 512], BF16)
        xqall = lo.take("lnxq", [128, KT * 512], BF16)
        xnall = lo.take("lnxn", [128, KT * 512], BF16)
        rows = lo.take("lnr1", [1, 512], F32)
        rows2 = lo.take("lnr2", [1, 512], F32)
        rowt = lo.take("lnrt", [1, 512], F32)
        rstdb = lo.take("lnrb", [128, 512], F32)
        mrb = lo.take("lnmb", [128, 512], F32)
        tmp = lo.take("lntmp", [128, 512], F32)
        xn_guard = 0

        for tc in range(4):
            wv = [b.dma("gpsimd", xh[:, 512 * dt:512 * (dt + 1)],
                        dsrc[128 * dt:128 * (dt + 1), 512 * tc:512 * (tc + 1)],
                        s_lx)
                  for dt in range(KT)]
            vv_cast = b.vop(lambda v: v.tensor_copy(xball[:], xh[:]),
                            pre=[(s_lx, wv[-1]), (s_lw, xn_guard)])
            av_sq = b.aop(lambda a: a.activation(xqall[:], xball[:], AF.Square),
                          pre=[(b.v_ms, vv_cast)])
            rv6 = b.group(6, [
                (lambda t, s, e, dt=dt: t.matmul(
                    pb[6][:1, :], sb_ones[:], xball[:, 512 * dt:512 * (dt + 1)],
                    start=s, stop=e)) for dt in range(KT)],
                pre=[(b.v_ms, vv_cast)])
            rv7 = b.group(7, [
                (lambda t, s, e, dt=dt: t.matmul(
                    pb[7][:1, :], sb_ones[:], xqall[:, 512 * dt:512 * (dt + 1)],
                    start=s, stop=e)) for dt in range(KT)],
                pre=[(b.a_ms, av_sq)])
            b.consume("vector", 6, rv6,
                      lambda v: v.tensor_scalar_mul(rows[:], pb[6][:1, :], 1.0 / D))
            b.consume("vector", 7, rv7,
                      lambda v: v.tensor_scalar_mul(rows2[:], pb[7][:1, :], 1.0 / D))

            def vstat(v):
                v.tensor_tensor(rowt[:], rows[:], rows[:], ALU.mult)
                v.tensor_tensor(rows2[:], rows2[:], rowt[:], ALU.subtract)
                return v.tensor_scalar_add(rows2[:], rows2[:], EPS)
            vv = b.vop(vstat)
            av = b.aop(lambda a: a.activation(rows2[:], rows2[:], AF.Sqrt),
                       pre=[(b.v_ms, vv)])

            def vrec(v):
                v.reciprocal(rows2[:], rows2[:])
                return v.tensor_tensor(rows[:], rows[:], rows2[:], ALU.mult)
            vv = b.vop(vrec, pre=[(b.a_ms, av)])

            for bnk, src, dst in ((6, rows2, rstdb), (7, rows, mrb)):
                rv = b.group(bnk, [
                    lambda t, s, e, bnk=bnk, src=src: t.matmul(
                        pb[bnk][:], sb_ones1f[:], src[:], start=s, stop=e)],
                    pre=[(b.v_ms, vv)])
                b.consume("vector", bnk, rv,
                          lambda v, dst=dst, bnk=bnk: v.tensor_copy(dst[:], pb[bnk][:]))

            def vnorm(v):
                for dt in range(KT - 1):
                    v.tensor_tensor(tmp[:], xh[:, 512 * dt:512 * (dt + 1)],
                                    rstdb[:], ALU.mult)
                    v.tensor_tensor(xnall[:, 512 * dt:512 * (dt + 1)],
                                    tmp[:], mrb[:], ALU.subtract)
                v.tensor_tensor(tmp[:], xh[:, 512 * (KT - 1):512 * KT],
                                rstdb[:], ALU.mult)
                return v.tensor_tensor(xnall[:, 512 * (KT - 1):512 * KT],
                                       tmp[:], mrb[:], ALU.subtract)
            vv = b.vop(vnorm)
            b.dma("gpsimd",
                  dxn[:, 512 * tc:512 * (tc + 1)].rearrange("(dt p) c -> p dt c", p=128),
                  xnall[:].rearrange("p (dt c) -> p dt c", dt=KT),
                  s_lw, pre=[(b.v_ms, vv)])
            xn_guard = s_lw.n


    emit_dequant(dqw_gu, dguc, NG // 8, NG, ds_gu, dzs_gu, KT, False)
    emit_dequant(dqw_dn, ddnc, D // 8, D, ds_dn, dzs_dn, MT, True)

    b.barrier()

    s_gx = b.stream("sync")
    s_rg = [b.stream("sync") for _ in range(4)]
    s_hw = [b.stream("gpsimd") for _ in range(4)]
    s_hh = b.stream("sync")
    s_wd = [b.stream("scalar") for _ in range(8)]
    s_xr = b.stream("gpsimd")
    s_os = [b.stream("gpsimd") for _ in range(8)]

    def emit_ffn(dsrc, ddst, skip_ln=False):
        if not skip_ln:
            emit_ln(dsrc)
            b.barrier()

        lo = Alloc(nc, PERSIST)
        xnh = lo.take("gxn", [128, KT * 1024], BF16)
        ring = [lo.take(f"grg{i}", [128, KT * SW], BF16) for i in range(4)]
        silub = [lo.take(f"gsl{i}", [128, 2048], BF16) for i in range(2)]
        hstg = [lo.take(f"ghs{i}", [128, 1024], BF16) for i in range(4)]

        ring_guard = [(0, 0)] * 4
        silu_guard = [(0, 0), (0, 0)]
        silu_fv = {}
        hmul_fv = {}
        hst_guard = [0, 0, 0, 0]
        rot = [0]

        for half in range(2):
            pre0 = ([(b.rdy[k], b.rdy[k].n) for k in range(8)] if half == 1 else [])
            xwv = [b.dma("sync", xnh[:, 1024 * k:1024 * (k + 1)],
                         dxn[128 * k:128 * (k + 1), HALF * half:HALF * (half + 1)],
                         s_gx, pre=(pre0 if k == 0 else ()))
                   for k in range(KT)]

            for jp in range(2 * NJ):
                j, side = jp // 2, jp % 2
                slot = ring[jp % 4]
                pos0 = side * INTER + SW * j
                gb, gv = ring_guard[jp % 4]
                s_idx = pos0 // SW
                wv = b.dma("sync", slot[:],
                           dguc[128 * s_idx:128 * (s_idx + 1), :],
                           s_rg[jp % 4], pre=[(b.rdy[gb], gv)])
                for q in range(2):
                    for tc in range(2):
                        bnk = rot[0]
                        rot[0] = (rot[0] + 1) % 6
                        mms = [(lambda t, s, e, k=k, q=q, tc=tc, slot=slot, bnk=bnk:
                                t.matmul(pb[bnk][:],
                                         slot[:, SW * k + 128 * q:SW * k + 128 * (q + 1)],
                                         xnh[:, 1024 * k + 512 * tc:1024 * k + 512 * (tc + 1)],
                                         start=s, stop=e))
                               for k in range(KT)]
                        rv = b.group(bnk, mms,
                                     pre=([(s_rg[jp % 4], wv), (s_gx, xwv[-1])]
                                          if (q == 0 and tc == 0) else ()))
                        ring_guard[jp % 4] = (bnk, rv)
                        if side == 0:
                            sl = silub[j % 2]
                            gb2, gv2 = silu_guard[j % 2]
                            sfv = b.consume("scalar", bnk, rv,
                                            lambda a, sl=sl, q=q, tc=tc, bnk=bnk:
                                            a.activation(
                                                sl[:, 1024 * q + 512 * tc:1024 * q + 512 * (tc + 1)],
                                                pb[bnk][:], AF.Silu),
                                            pre=([(b.fre[gb2], gv2)]
                                                 if (q == 0 and tc == 0) else ()))
                            silu_fv[(q, tc)] = (bnk, sfv)
                        else:
                            hs = hstg[2 * (j % 2) + q]
                            sl = silub[j % 2]

                            def hmul(v, hs=hs, sl=sl, q=q, tc=tc, bnk=bnk):
                                return v.tensor_tensor(
                                    hs[:, 512 * tc:512 * (tc + 1)], pb[bnk][:],
                                    sl[:, 1024 * q + 512 * tc:1024 * q + 512 * (tc + 1)],
                                    ALU.mult)
                            sb_, sv_ = silu_fv[(q, tc)]
                            fv = b.consume("vector", bnk, rv, hmul,
                                           pre=([(b.fre[sb_], sv_),
                                                 (s_hw[2 * (j % 2) + q],
                                                  hst_guard[2 * (j % 2) + q])]
                                                if tc == 0 else [(b.fre[sb_], sv_)]))
                            hmul_fv[(j % 2, q)] = (bnk, fv)
                            if tc == 1:
                                silu_guard[j % 2] = (bnk, fv)
                if side == 1:
                    dh4 = dh.rearrange("(mt tq p) c -> mt tq p c", tq=4, p=128)
                    for q in range(2):
                        hb, hv = hmul_fv[(j % 2, q)]
                        for tc in range(2):
                            b.dma("gpsimd",
                                  dh4[2 * j + q, 2 * half + tc, :, :],
                                  hstg[2 * (j % 2) + q][:, 512 * tc:512 * (tc + 1)],
                                  s_hw[2 * (j % 2) + q],
                                  pre=([(b.fre[hb], hv)] if tc == 0 else ()))
                        hst_guard[2 * (j % 2) + q] = s_hw[2 * (j % 2) + q].n

        b.barrier()
        if K_UPTO_GATE[0]:
            return

        lo = Alloc(nc, PERSIST)
        hh = lo.take("dnh", [128, MT * 512], BF16)
        wdn = [lo.take(f"dnw{i}", [128, 512], BF16) for i in range(8)]
        xres = [lo.take(f"dnxr{i}", [128, 512], F32) for i in range(8)]
        ost = [lo.take(f"dnos{i}", [128, 512], F32) for i in range(8)]

        wdn_guard = [0] * 8
        xres_guard = [0] * 8
        ost_guard = [0] * 8
        last_og_marks = []
        dsel = ddnc.rearrange("(mt oc p) c -> mt oc p c", oc=8, p=128)
        dh4 = dh.rearrange("(mt tq p) c -> mt tq p c", tq=4, p=128)
        xsel = dsrc.rearrange("(c r s) t -> c s r t", r=64, s=4)
        osel = ddst.rearrange("(c r s) t -> c s r t", r=64, s=4)

        for tq in range(4):
            hwv = [b.dma("sync", hh[:, 512 * mt:512 * (mt + 1)],
                         dh4[mt, tq, :, :],
                         s_hh,
                         pre=([(b.t_ms, last_og_marks[mt])]
                              if last_og_marks else
                              ([(b.rdy[k], b.rdy[k].n) for k in range(8)]
                               if mt == 0 else ())))
                   for mt in range(MT)]
            last_og_marks = []

            for og in range(8):
                base = 4 * (og % 2)
                banks = [base + i for i in range(4)]
                for ot in range(4):
                    bnk = banks[ot]
                    jj, q = 2 * og + ot // 2, ot % 2
                    b.dma("gpsimd", xres[bnk][:],
                          xsel[jj, 2 * q:2 * q + 2, :, 512 * tq:512 * (tq + 1)],
                          s_xr, pre=[(b.fre[bnk], xres_guard[bnk])])
                xr_done = s_xr.n
                open_wf = []
                rvs = {}
                for bnk in banks:
                    open_wf.append((bnk, b.uses[bnk]))
                    b.uses[bnk] += 1
                    b.rdy[bnk].n += 1
                    rvs[bnk] = b.rdy[bnk].n
                for mt in range(MT):
                    ws = wdn[mt % 8]
                    wv = b.dma("scalar", ws[:], dsel[mt, og, :, :],
                               s_wd[mt % 8], pre=[(b.t_ms, wdn_guard[mt % 8])])

                    def dmm(t, mt=mt, ws=ws, wv=wv, banks=tuple(banks),
                            open_wf=tuple(open_wf), hv=hwv[-1]):
                        t.wait_ge(s_wd[mt % 8].h, wv)
                        if mt == 0:
                            t.wait_ge(s_hh.h, hv)
                            for bnk, wf in open_wf:
                                if wf > 0:
                                    t.wait_ge(b.fre[bnk].h, wf)
                        for ot in range(4):
                            inst = t.matmul(
                                pb[banks[ot]][:], ws[:, 128 * ot:128 * (ot + 1)],
                                hh[:, 512 * mt:512 * (mt + 1)],
                                start=(mt == 0), stop=(mt == MT - 1))
                            if mt == MT - 1:
                                inst.then_inc(b.rdy[banks[ot]].h, 1)
                        if mt != MT - 1:
                            inst.then_inc(b.t_ms.h, 1)
                        else:
                            t.nop().then_inc(b.t_ms.h, 1)
                    b.t_ms.n += 1
                    b.emit("tensor", dmm)
                    wdn_guard[mt % 8] = b.t_ms.n
                    if og == 7:
                        last_og_marks.append(b.t_ms.n)
                for ot in range(4):
                    bnk = banks[ot]
                    jj, q = 2 * og + ot // 2, ot % 2
                    if "noxres" in dbg:
                        fv = b.consume("vector", bnk, rvs[bnk],
                                       lambda v, bnk=bnk: v.tensor_scalar_mul(
                                           ost[bnk][:], pb[bnk][:], 0.5),
                                       pre=[(s_xr, xr_done),
                                            (s_os[bnk], ost_guard[bnk])])
                    else:
                        fv = b.consume("vector", bnk, rvs[bnk],
                                       lambda v, bnk=bnk: v.scalar_tensor_tensor(
                                           ost[bnk][:], pb[bnk][:], 0.5, xres[bnk][:],
                                           ALU.mult, ALU.add),
                                       pre=[(s_xr, xr_done),
                                            (s_os[bnk], ost_guard[bnk])])
                    xres_guard[bnk] = fv
                    b.dma("gpsimd",
                          osel[jj, 2 * q:2 * q + 2, :, 512 * tq:512 * (tq + 1)],
                          ost[bnk][:], s_os[bnk], pre=[(b.fre[bnk], fv)])
                    ost_guard[bnk] = s_os[bnk].n
        b.barrier()

    def emit_attn(dsrc, ddst):
        emit_ln(dsrc)
        b.barrier()
        lo = Alloc(nc, PERSIST)
        abuf = lo.take("abuf", [128, max(n_mask, 1) * 128], BF16)
        A_END = lo.off
        w1s = [lo.take(f"aw1{i}", [128, KT * 128], BF16) for i in range(4)]
        w2t = [lo.take(f"aw2{i}", [128, KT * 512], BF16) for i in range(2)]
        amask = [lo.take(f"amk{i}", [128, 128], F32) for i in range(2)]

        nz_of_chunk = []
        for tcb in range(8):
            ss = {}
            for tt in range(4 * tcb, 4 * tcb + 4):
                for (s_t, slot) in nz_of_t[tt]:
                    ss.setdefault(s_t, []).append((tt, slot))
            nz_of_chunk.append(sorted(ss.items()))

        s_w2 = [b.stream("sync") for _ in range(2)]
        s_w1 = [b.stream("sync") for _ in range(4)]
        s_mk = [b.stream("sync") for _ in range(2)]
        w1_guard = [0] * 4
        w2_guard = [0, 0]
        mk_guard = [("v", 0, 0), ("v", 0, 0)]
        arot = [0]
        mki = [0]
        for tcb in range(8):
            w2s = w2t[tcb % 2]
            wv2 = [b.dma("sync", w2s[:, 512 * dt:512 * (dt + 1)],
                         dw2bf[512 * tcb:512 * (tcb + 1), 128 * dt:128 * (dt + 1)],
                         s_w2[tcb % 2], transpose=True,
                         pre=([(b.t_ms, w2_guard[tcb % 2]), (s_w2bf, w2bf_done)]
                              if dt == 0 else ()))
                   for dt in range(KT)]
            si = 0
            for (s_t, uses) in nz_of_chunk[tcb]:
                w1b = w1s[si % 4]
                wv1 = b.dma("sync", w1b[:],
                            dw1c[128 * s_t:128 * (s_t + 1), :],
                            s_w1[si % 4],
                            pre=[(b.t_ms, w1_guard[si % 4]),
                                 (s_w2bf, w1c_done)])
                bnk = arot[0]
                arot[0] = (arot[0] + 1) % 6
                mms = [(lambda t, s, e, k=k, w1b=w1b, w2s=w2s, bnk=bnk:
                        t.matmul(pb[bnk][:], w1b[:, 128 * k:128 * (k + 1)],
                                 w2s[:, 512 * k:512 * (k + 1)],
                                 start=s, stop=e)) for k in range(KT)]
                rv = b.group(bnk, mms, pre=[(s_w2[tcb % 2], wv2[-1]),
                                            (s_w1[si % 4], wv1)])
                tm = b.tmark()
                w1_guard[si % 4] = tm
                w2_guard[tcb % 2] = tm
                nuses = len(uses)
                for ui, (tt, slot) in enumerate(uses):
                    mi = mki[0] % 2
                    mk = amask[mi]
                    kind, gb3, gv3 = mk_guard[mi]
                    wmk = b.dma("sync", mk[:], dmask[128 * slot:128 * (slot + 1), :],
                                s_mk[mi],
                                pre=[(b.v_ms, gv3) if kind == "v"
                                     else (b.fre[gb3], gv3)])
                    mki[0] += 1
                    qq = tt - 4 * tcb

                    def amul(v, mk=mk, qq=qq, slot=slot, bnk=bnk):
                        return v.tensor_tensor(
                            abuf[:, 128 * slot:128 * (slot + 1)],
                            pb[bnk][:, 128 * qq:128 * (qq + 1)], mk[:], ALU.mult)
                    if ui == nuses - 1:
                        fv = b.consume("vector", bnk, rv, amul,
                                       pre=[(s_mk[mi], wmk)])
                        mk_guard[mi] = ("f", bnk, fv)
                    else:
                        vv = b.vop(amul, pre=[(s_mk[mi], wmk),
                                              (b.rdy[bnk], rv)])
                        mk_guard[mi] = ("v", 0, vv)
                si += 1
        b.barrier()

        lo = Alloc(nc, A_END)
        xn2 = lo.take("yxn", [128, KT * T], BF16)
        gsb = [lo.take(f"ygs{i}", [128, 512], F32) for i in range(4)]
        x1t = [lo.take(f"yx1{i}", [128, 512], F32) for i in range(4)]
        x2t = [lo.take(f"yx2{i}", [128, 512], F32) for i in range(4)]

        s_x2l = b.stream("sync")
        s_x1l = [b.stream("gpsimd") for _ in range(4)]
        s_x2w = [b.stream("gpsimd") for _ in range(4)]
        xw = [b.dma("sync", xn2[:, T * k:T * (k + 1)],
                    dxn[128 * k:128 * (k + 1), :], s_x2l) for k in range(KT)]
        yrot = [0]
        gs_guard = [0] * 4
        x1_guard = [0] * 4
        x2_guard = [0] * 4
        for J in range(32):
            pairs = nz_of_t[J]
            assert pairs, "empty attn row-tile unsupported"
            for tk in range(4):
                bnk = yrot[0]
                yrot[0] = (yrot[0] + 1) % 6
                mms = [(lambda t, s, e, st=st, slot=slot, tk=tk, bnk=bnk:
                        t.matmul(pb[bnk][:],
                                 abuf[:, 128 * slot:128 * (slot + 1)],
                                 xn2[:, T * st + 512 * tk:T * st + 512 * (tk + 1)],
                                 start=s, stop=e))
                       for (st, slot) in pairs]
                rv = b.group(bnk, mms, pre=[(s_x2l, xw[-1])])
                gi = (J * 4 + tk) % 4
                gs = gsb[gi]
                fvg = b.consume("scalar", bnk, rv,
                                lambda a, gs=gs, J=J, bnk=bnk: a.activation(
                                    gs[:], pb[bnk][:], AF.Gelu,
                                    bias=sb_b2[:, J:J + 1]),
                                pre=[(b.v_ms, gs_guard[gi])])
                wx1 = b.dma("gpsimd", x1t[gi][:],
                            dsrc[128 * J:128 * (J + 1), 512 * tk:512 * (tk + 1)],
                            s_x1l[gi], pre=[(b.v_ms, x1_guard[gi])])

                def yadd(v, gs=gs, gi=gi):
                    return v.tensor_tensor(x2t[gi][:], x1t[gi][:], gs[:], ALU.add)
                vv = b.vop(yadd, pre=[(b.fre[bnk], fvg), (s_x1l[gi], wx1),
                                      (s_x2w[gi], x2_guard[gi])])
                gs_guard[gi] = vv
                x1_guard[gi] = vv
                b.dma("gpsimd", ddst[128 * J:128 * (J + 1), 512 * tk:512 * (tk + 1)],
                      x2t[gi][:], s_x2w[gi], pre=[(b.v_ms, vv)])
                x2_guard[gi] = s_x2w[gi].n
        b.barrier()

    def emit_final():
        lo = Alloc(nc, PERSIST)
        fst = [lo.take(f"ffs{i}", [128, KT * 128], F32) for i in range(2)]
        ostg = [lo.take(f"fos{i}", [128, D], F32) for i in range(2)]
        s_fl = [b.stream("sync") for _ in range(2)]
        s_fo = [b.stream("gpsimd") for _ in range(2)]
        f_guard = [0, 0]
        o_guard = [0, 0]
        for tt in range(16):
            fs = fst[tt % 2]
            wv = b.dma("sync", fs[:].rearrange("p (dt c) -> p dt c", dt=KT),
                       dxT[3][:, 128 * tt:128 * (tt + 1)]
                       .rearrange("(dt p) c -> p dt c", p=128),
                       s_fl[tt % 2], pre=[(b.t_ms, f_guard[tt % 2])])
            og = ostg[tt % 2]
            fvs = {}
            for bg in range(8):
                bnk = bg % 2 + 2
                wf = b.uses[bnk]
                b.uses[bnk] += 1
                b.rdy[bnk].n += 1
                rv = b.rdy[bnk].n

                def trun(t, bg=bg, bnk=bnk, wf=wf, wv=wv, fs=fs, tt=tt):
                    t.wait_ge(s_fl[tt % 2].h, wv)
                    if wf > 0:
                        t.wait_ge(b.fre[bnk].h, wf)
                    for i in range(4):
                        inst = t.transpose(
                            pb[bnk][:, 128 * i:128 * (i + 1)],
                            fs[:, 128 * (4 * bg + i):128 * (4 * bg + i + 1)],
                            sb_idf[:])
                        if i == 3:
                            inst.then_inc(b.rdy[bnk].h, 1)
                b.emit("tensor", trun)
                fvs[bnk] = b.consume(
                    "vector", bnk, rv,
                    lambda v, og=og, bg=bg, bnk=bnk:
                    v.tensor_copy(og[:, 512 * bg:512 * (bg + 1)], pb[bnk][:]),
                    pre=([(s_fo[tt % 2], o_guard[tt % 2])] if bg == 0 else ()))
            f_guard[tt % 2] = b.tmark()
            b.dma("gpsimd", dout[128 * tt:128 * (tt + 1), :], og[:],
                  s_fo[tt % 2], pre=[(b.fre[2], fvs[2]), (b.fre[3], fvs[3])])
            o_guard[tt % 2] = s_fo[tt % 2].n

    if upto == "p1":
        emit_tap(ddnc.bitcast(F32), 2048, 2048)
        return finish()
    emit_ln(dxT[0])
    b.barrier()
    if upto == "ln1":
        emit_tap(dxn.bitcast(F32), D, T // 2)
        return finish()
    if upto == "gate":
        K_UPTO_GATE[0] = True
    emit_ffn(dxT[0], dxT[1], skip_ln=True)
    if upto == "gate":
        emit_tap(dh.bitcast(F32), 4096, 1024)
        return finish()
    if upto == "ffn1":
        emit_tap(dxT[1], D, T)
        return finish()
    emit_attn(dxT[1], dxT[2])
    if upto == "attn":
        emit_tap(dxT[2], D, T)
        return finish()
    emit_ffn(dxT[2], dxT[3])
    emit_final()
    b.barrier()

    with nc.Block() as blk:
        for eng in B.ENGS:
            getattr(blk, eng)(lambda e, eng=eng: [f(e) for f in b.ops[eng]])
    return nc



def _host_prep(inputs):
    mask = np.asarray(inputs["L0_attn_sparse_mask"], dtype=np.float32)
    tiles = mask.reshape(32, 128, 32, 128)
    nzmap = (tiles != 0).any(axis=(1, 3))
    nz_of_t = [[] for _ in range(32)]
    packed = []
    slot = 0
    for tj in range(32):
        for si in range(32):
            if nzmap[si, tj]:
                packed.append(np.ascontiguousarray(tiles[si, :, tj, :]))
                nz_of_t[tj].append((si, slot))
                slot += 1
    mask_nz = (np.concatenate(packed, axis=0) if packed
               else np.zeros((128, 128), np.float32))
    n_mask = max(slot, 1)

    perm_gu = np.concatenate([_pos2dense(INTER), INTER + _pos2dense(INTER)])
    perm_dn = _pos2dense(D)
    s_gu = np.ascontiguousarray(
        np.asarray(inputs["L0_ffn_scales_gu"], np.float32)[:, perm_gu])
    s_dn = np.ascontiguousarray(
        np.asarray(inputs["L0_ffn_scales_dn"], np.float32)[:, perm_dn])

    ind_gu = np.zeros((128, KT * KT), dtype=bfdt)
    for g in range(KT):
        ind_gu[:, KT * g + g] = 1.0
    ind_dn = np.zeros((128, NJ * MT), dtype=bfdt)
    p = np.arange(128)
    for j in range(NJ):
        ind_dn[p, MT * j + 2 * j + (p % 64) // 32] = 1.0

    common = dict(
        w1=np.ascontiguousarray(np.asarray(inputs["L0_attn_w1"], np.float32)),
        w2=np.ascontiguousarray(np.asarray(inputs["L0_attn_w2"], np.float32)),
        b2=np.ascontiguousarray(np.asarray(inputs["L0_attn_b2"], np.float32)),
        mask_nz=np.ascontiguousarray(mask_nz),
        qw_gu=np.ascontiguousarray(np.asarray(inputs["L0_ffn_qweight_gu"], np.int32)),
        qz_gu=np.ascontiguousarray(np.asarray(inputs["L0_ffn_qzeros_gu"], np.int32)),
        s_gu=s_gu,
        qw_dn=np.ascontiguousarray(np.asarray(inputs["L0_ffn_qweight_dn"], np.int32)),
        qz_dn=np.ascontiguousarray(np.asarray(inputs["L0_ffn_qzeros_dn"], np.int32)),
        s_dn=s_dn,
        identf=np.eye(128, dtype=np.float32),
        stair2=np.stack([((np.arange(128) % 64) // 32 == k).astype(bfdt)
                         for k in range(2)]),
        ones_col=np.ones((128, 1), dtype=bfdt),
        ones_row_b=np.ones((1, 128), dtype=bfdt),
        ones_row_f=np.ones((1, 128), dtype=np.float32),
        ind_gu=ind_gu, ind_dn=ind_dn,
    )
    return nz_of_t, n_mask, common


def kernel(**inputs):
    nz_of_t, n_mask, common = _host_prep(inputs)
    x = np.asarray(inputs["L0_attn_x"], np.float32).reshape(SEQ, D)
    nc = build_program(nz_of_t, n_mask)
    in_maps = []
    for c in range(NCORES):
        m = dict(common)
        m["x"] = np.ascontiguousarray(x[T * c:T * (c + 1), :])
        in_maps.append(m)
    res = run_bass_kernel_spmd(nc, in_maps, core_ids=list(range(NCORES)))
    out = np.concatenate([res.results[c]["out"] for c in range(NCORES)], axis=0)
    return out.reshape(1, SEQ, D).astype(np.float32)


# version 13
# speedup vs baseline: 1.0157x; 1.0157x over previous
import numpy as np
import ml_dtypes
import concourse.bass as bass
import concourse.mybir as mybir
from concourse.bass_utils import run_bass_kernel_spmd

F32 = mybir.dt.float32
BF16 = mybir.dt.bfloat16
I32 = mybir.dt.int32
U16 = mybir.dt.uint16
AF = mybir.ActivationFunctionType
ALU = mybir.AluOpType
bfdt = ml_dtypes.bfloat16

NCORES = 8
D = 4096
SEQ = 16384
T = SEQ // NCORES
INTER = 11008
NG = 2 * INTER
NJ = INTER // 256
SW = 256
MT = 2 * NJ
KT = D // 128
HALF = T // 2
EPS = 1e-5
SBUF_BASE = 16512
SBUF_TOP = 229300


def _pos2dense(n):
    p = np.arange(n)
    j, rem = p // SW, p % SW
    i, r = rem // 64, rem % 64
    return j * SW + 4 * r + i


class Alloc:
    def __init__(self, nc, base, top=SBUF_TOP):
        self.nc, self.off, self.top = nc, base, top

    def take(self, name, shape, dtype, align=64):
        esz = {F32: 4, BF16: 2, I32: 4, U16: 2}[dtype]
        self.off = -(-self.off // align) * align
        h = self.nc.alloc_sbuf_tensor_at(name, list(shape), dtype, offset=self.off)
        self.off += shape[-1] * esz
        assert self.off <= self.top, f"SBUF overflow at {name}: {self.off}"
        return h


class Sem:
    __slots__ = ("h", "n")

    def __init__(self, h):
        self.h, self.n = h, 0

    def inc(self, by=1):
        self.n += by
        return self.n


def _pre(e, pre):
    for s, v in pre:
        if v > 0:
            e.wait_ge(s.h, v)


class B:
    ENGS = ("tensor", "vector", "scalar", "sync", "gpsimd")

    def __init__(self, nc):
        self.nc = nc
        self.ops = {e: [] for e in self.ENGS}
        self._si = 0
        self.rdy = [self.sem() for _ in range(8)]
        self.fre = [self.sem() for _ in range(8)]
        self.uses = [0] * 8
        self.v_ms = self.sem()
        self.a_ms = self.sem()
        self.g_ms = self.sem()
        self.t_ms = self.sem()
        self.bar = self.sem()
        self.qstreams = {"sync": [], "scalar": [], "gpsimd": []}

    def stream(self, queue):
        s = self.sem()
        self.qstreams[queue].append(s)
        return s

    def sem(self):
        h = self.nc.semaphore(f"sm{self._si}").__enter__()
        self._si += 1
        return Sem(h)

    def emit(self, eng, fn):
        self.ops[eng].append(fn)

    def wait(self, eng, sem, val):
        if val > 0:
            self.ops[eng].append(lambda e, s=sem, v=val: e.wait_ge(s.h, v))

    def group(self, bnk, mms, pre=()):
        wf = self.uses[bnk]
        self.uses[bnk] += 1
        self.rdy[bnk].n += 1
        rv = self.rdy[bnk].n

        def run(t, mms=mms, bnk=bnk, wf=wf, pre=tuple(pre)):
            _pre(t, pre)
            if wf > 0:
                t.wait_ge(self.fre[bnk].h, wf)
            n = len(mms)
            for i, m in enumerate(mms):
                inst = m(t, i == 0, i == n - 1)
                if i == n - 1:
                    inst.then_inc(self.rdy[bnk].h, 1)
        self.emit("tensor", run)
        return rv

    def consume(self, eng, bnk, rv, fn, pre=()):
        self.fre[bnk].n += 1

        def run(e, bnk=bnk, rv=rv, fn=fn, pre=tuple(pre)):
            _pre(e, pre)
            e.wait_ge(self.rdy[bnk].h, rv)
            fn(e).then_inc(self.fre[bnk].h, 1)
        self.emit(eng, run)
        return self.fre[bnk].n

    def dma(self, queue, out, in_, sem, pre=(), **kw):
        v = sem.inc(16)

        def run(e, out=out, in_=in_, kw=kw, sem=sem, pre=tuple(pre)):
            _pre(e, pre)
            e.dma_start(out=out, in_=in_, **kw).then_inc(sem.h, 16)
        self.emit(queue, run)
        return v

    def vop(self, fn, pre=()):
        self.v_ms.n += 1

        def run(v, fn=fn, pre=tuple(pre)):
            _pre(v, pre)
            fn(v).then_inc(self.v_ms.h, 1)
        self.emit("vector", run)
        return self.v_ms.n

    def gop(self, fn, pre=()):
        self.g_ms.n += 1

        def run(g, fn=fn, pre=tuple(pre)):
            _pre(g, pre)
            fn(g).then_inc(self.g_ms.h, 1)
        self.emit("gpsimd", run)
        return self.g_ms.n

    def aop(self, fn, pre=()):
        self.a_ms.n += 1

        def run(a, fn=fn, pre=tuple(pre)):
            _pre(a, pre)
            fn(a).then_inc(self.a_ms.h, 1)
        self.emit("scalar", run)
        return self.a_ms.n

    def tmark(self):
        self.t_ms.n += 1
        self.emit("tensor", lambda t: t.nop().then_inc(self.t_ms.h, 1))
        return self.t_ms.n

    def barrier(self):
        tgt = self.bar.n + 5
        self.bar.n = tgt
        snap = {q: [(s, s.n) for s in ss if s.n > 0]
                for q, ss in self.qstreams.items()}
        for eng in self.ENGS:
            def run(e, eng=eng, snap=snap):
                for s, v in snap.get(eng, ()):
                    e.wait_ge(s.h, v)
                if eng in ("vector", "gpsimd"):
                    e.engine_nop().then_inc(self.bar.h, 1)
                else:
                    e.nop().then_inc(self.bar.h, 1)
            self.emit(eng, run)
        for eng in self.ENGS:
            self.wait(eng, self.bar, tgt)


def build_program(nz_of_t, n_mask, upto="full", dbg=()):
    K_UPTO_GATE = [False]
    nc = bass.Bass("TRN2", target_bir_lowering=False, debug=False)
    nc.detect_race_conditions = False

    dx = nc.dram_tensor("x", [T, D], F32, kind="ExternalInput").ap()
    dw1 = nc.dram_tensor("w1", [D, D], F32, kind="ExternalInput").ap()
    dw2 = nc.dram_tensor("w2", [D, D], F32, kind="ExternalInput").ap()
    db2 = nc.dram_tensor("b2", [D], F32, kind="ExternalInput").ap()
    dmask = nc.dram_tensor("mask_nz", [max(n_mask, 1) * 128, 128], F32,
                           kind="ExternalInput").ap()
    dqw_gu = nc.dram_tensor("qw_gu", [D, NG // 8], I32, kind="ExternalInput").ap()
    dqz_gu = nc.dram_tensor("qz_gu", [KT, NG // 8], I32, kind="ExternalInput").ap()
    ds_gu = nc.dram_tensor("s_gu", [KT, NG], F32, kind="ExternalInput").ap()
    dqw_dn = nc.dram_tensor("qw_dn", [INTER, D // 8], I32, kind="ExternalInput").ap()
    dqz_dn = nc.dram_tensor("qz_dn", [MT, D // 8], I32, kind="ExternalInput").ap()
    ds_dn = nc.dram_tensor("s_dn", [MT, D], F32, kind="ExternalInput").ap()
    didf = nc.dram_tensor("identf", [128, 128], F32, kind="ExternalInput").ap()
    dones = nc.dram_tensor("ones_col", [128, 1], BF16, kind="ExternalInput").ap()
    dones1b = nc.dram_tensor("ones_row_b", [1, 128], BF16, kind="ExternalInput").ap()
    dones1f = nc.dram_tensor("ones_row_f", [1, 128], F32, kind="ExternalInput").ap()
    dind_gu = nc.dram_tensor("ind_gu", [128, KT * KT], BF16, kind="ExternalInput").ap()
    dind_dn = nc.dram_tensor("ind_dn", [128, NJ * MT], BF16, kind="ExternalInput").ap()
    dstair2 = nc.dram_tensor("stair2", [2, 128], BF16, kind="ExternalInput").ap()
    dout = nc.dram_tensor("out", [T, D], F32, kind="ExternalOutput").ap()

    dxT = [nc.dram_tensor(f"xT{i}", [D, T], F32, kind="Internal").ap()
           for i in range(4)]
    dxn = nc.dram_tensor("xn_sc", [D, T], BF16, kind="Internal").ap()
    dguc = nc.dram_tensor("gu_cache", [2 * NJ * 128, KT * SW], BF16, kind="Internal").ap()
    ddnc = nc.dram_tensor("dn_cache", [MT * 8 * 128, 512], BF16, kind="Internal").ap()
    dh = nc.dram_tensor("h_sc", [MT * 4 * 128, 512], BF16, kind="Internal").ap()
    dzs_gu = nc.dram_tensor("zs_gu_sc", [KT, NG], BF16, kind="Internal").ap()
    dzs_dn = nc.dram_tensor("zs_dn_sc", [MT, D], BF16, kind="Internal").ap()
    dw2bf = nc.dram_tensor("w2bf", [D, D], BF16, kind="Internal").ap()
    dw1c = nc.dram_tensor("w1c", [32 * 128, KT * 128], BF16, kind="Internal").ap()

    pb = [nc.psum_tensor(f"pbank{i}", [128, 512], F32).__enter__() for i in range(8)]
    b = B(nc)

    al = Alloc(nc, SBUF_BASE)
    sb_idf = al.take("idf", [128, 128], F32)
    sb_ones = al.take("onesc", [128, 1], BF16)
    sb_ones1b = al.take("ones1b", [1, 128], BF16)
    sb_ones1f = al.take("ones1f", [1, 128], F32)
    sb_indgu = al.take("indgu", [128, KT * KT], BF16)
    sb_stair2 = al.take("stair2", [2, 128], BF16)
    sb_b2 = al.take("b2t", [128, 32], F32)
    PERSIST = al.off

    s_const = b.stream("sync")
    s_w2bf = b.stream("gpsimd")
    for out, in_ in ((sb_idf[:], didf), (sb_ones[:], dones),
                     (sb_ones1b[:], dones1b), (sb_ones1f[:], dones1f),
                     (sb_indgu[:], dind_gu), (sb_stair2[:], dstair2)):
        b.dma("sync", out, in_, s_const)
    b.dma("sync", sb_b2[:], db2.rearrange("(j p) -> p j", p=128), s_const,
          allow_slow_non_contiguous=True)
    for r in range(4):
        b.dma("gpsimd",
              dw2bf[1024 * r:1024 * (r + 1), :].rearrange("(a p) c -> p a c", p=128),
              dw2[1024 * r:1024 * (r + 1), :].rearrange("(a p) c -> p a c", p=128),
              s_w2bf)
    w2bf_done = s_w2bf.n
    w1cv = dw1c.rearrange("(st p) (dt c) -> st p dt c", p=128, c=128)
    for st in range(32):
        b.dma("gpsimd", w1cv[st, :, :, :],
              dw1[:, 128 * st:128 * (st + 1)].rearrange("(dt p) c -> p dt c", p=128),
              s_w2bf)
    w1c_done = s_w2bf.n
    for eng in ("tensor", "vector", "scalar"):
        b.wait(eng, s_const, s_const.n)

    def emit_tap(dsrc_bytes_ap, nrows, rowlen_f32):
        lo = Alloc(nc, PERSIST)
        tbuf = lo.take("tapbuf", [128, 2048], F32)
        s_tl = b.stream("sync")
        s_tw = b.stream("gpsimd")
        dflat = dout.rearrange("a c -> (a c)")
        for i in range(0, nrows, 128):
            n = min(128, nrows - i)
            wv = b.dma("sync", tbuf[:n, :rowlen_f32], dsrc_bytes_ap[i:i + n, :],
                       s_tl, pre=[(s_tw, s_tw.n)])
            vv = b.vop(lambda v: v.engine_nop(), pre=[(s_tl, wv)])
            seg = dflat[i * rowlen_f32:(i + n) * rowlen_f32]
            b.dma("gpsimd", seg.rearrange("(p f) -> p f", f=rowlen_f32),
                  tbuf[:n, :rowlen_f32], s_tw, pre=[(b.v_ms, vv)])

    def finish():
        b.barrier()
        with nc.Block() as blk:
            for eng in B.ENGS:
                getattr(blk, eng)(lambda e, eng=eng: [f(e) for f in b.ops[eng]])
        return nc

    lnal = Alloc(nc, SBUF_TOP - 66000)
    ln_xb = lnal.take("lnxb", [128, KT * 512], BF16)
    ln_xq = [lnal.take(f"lnxq{i}", [128, 8 * 512], BF16) for i in range(2)]
    ln_r1 = lnal.take("lnr1", [1, 512], F32)
    ln_r2 = lnal.take("lnr2", [1, 512], F32)
    ln_rt = lnal.take("lnrt", [1, 512], F32)
    ln_rb = lnal.take("lnrb", [128, 512], F32)
    ln_mb = lnal.take("lnmb", [128, 512], F32)
    ln_tmp = lnal.take("lntmp", [128, 512], F32)
    ln_xn = [lnal.take(f"lnxn{i}", [128, 512], BF16) for i in range(4)]
    s_lnl = b.stream("gpsimd")
    s_lw = [b.stream("gpsimd") for _ in range(4)]
    ln_pace = b.sem()
    ln_st = {"xbf": 0, "sq0": 0, "sq1": 0, "xn": [0] * 4}

    def ln_load(dsrc, tc, seg, pre):
        p = list(pre)
        if seg is None or seg == 0:
            p.append((b.v_ms, ln_st["xbf"]))
        if seg is None:
            b.dma("gpsimd",
                  ln_xb[:].rearrange("p (dt c) -> p dt c", c=512),
                  dsrc[:, 512 * tc:512 * (tc + 1)]
                  .rearrange("(dt p) c -> p dt c", p=128),
                  s_lnl, pre=p)
        else:
            b.dma("gpsimd",
                  ln_xb[:, 2048 * seg:2048 * (seg + 1)]
                  .rearrange("p (dt c) -> p dt c", c=512),
                  dsrc[512 * seg:512 * (seg + 1), 512 * tc:512 * (tc + 1)]
                  .rearrange("(dt p) c -> p dt c", p=128),
                  s_lnl, pre=p)

    def ln_compute(tc):
        ldv = s_lnl.n
        p0p = ln_pace.n
        sqv = []
        for c in range(4):
            if c == 0:
                pre = [(s_lnl, ldv)]
                if ln_st["sq0"]:
                    pre.append((ln_pace, ln_st["sq0"]))
            elif c == 1:
                pre = [(b.rdy[7], ln_st["sq1"])] if ln_st["sq1"] else []
            elif c == 2:
                pre = [(ln_pace, p0p + 1)]
            else:
                pre = [(ln_pace, p0p + 2)]
            sqv.append(b.aop(lambda a, c=c: a.activation(
                ln_xq[c % 2][:], ln_xb[:, 4096 * c:4096 * (c + 1)], AF.Square),
                pre=pre))
        rv6 = b.group(6, [
            (lambda t, s, e, dt=dt: t.matmul(
                pb[6][:1, :], sb_ones[:], ln_xb[:, 512 * dt:512 * (dt + 1)],
                start=s, stop=e)) for dt in range(KT)],
            pre=[(s_lnl, ldv)])
        wf7 = b.uses[7]
        b.uses[7] += 1
        b.rdy[7].n += 1
        rv7 = b.rdy[7].n

        def sq_run(t, sqv=tuple(sqv), wf7=wf7):
            if wf7 > 0:
                t.wait_ge(b.fre[7].h, wf7)
            for c in range(4):
                t.wait_ge(b.a_ms.h, sqv[c])
                for i in range(8):
                    dt = 8 * c + i
                    inst = t.matmul(pb[7][:1, :], sb_ones[:],
                                    ln_xq[c % 2][:, 512 * i:512 * (i + 1)],
                                    start=(dt == 0), stop=(dt == KT - 1))
                    if dt == KT - 1:
                        inst.then_inc(b.rdy[7].h, 1)
                    elif i == 7:
                        inst.then_inc(ln_pace.h, 1)
        b.emit("tensor", sq_run)
        ln_pace.n += 3
        ln_st["sq0"] = p0p + 3
        ln_st["sq1"] = rv7
        b.consume("vector", 6, rv6,
                  lambda v: v.tensor_scalar_mul(ln_r1[:], pb[6][:1, :], 1.0 / D))
        b.consume("vector", 7, rv7,
                  lambda v: v.tensor_scalar_mul(ln_r2[:], pb[7][:1, :], 1.0 / D))

        def vstat(v):
            v.tensor_tensor(ln_rt[:], ln_r1[:], ln_r1[:], ALU.mult)
            v.tensor_tensor(ln_r2[:], ln_r2[:], ln_rt[:], ALU.subtract)
            return v.tensor_scalar_add(ln_r2[:], ln_r2[:], EPS)
        vv = b.vop(vstat)
        av = b.aop(lambda a: a.activation(ln_r2[:], ln_r2[:], AF.Sqrt),
                   pre=[(b.v_ms, vv)])

        def vrec(v):
            v.reciprocal(ln_r2[:], ln_r2[:])
            return v.tensor_tensor(ln_r1[:], ln_r1[:], ln_r2[:], ALU.mult)
        vv = b.vop(vrec, pre=[(b.a_ms, av)])
        for bnk, srcr, dst in ((6, ln_r2, ln_rb), (7, ln_r1, ln_mb)):
            rvb = b.group(bnk, [
                lambda t, s, e, bnk=bnk, srcr=srcr: t.matmul(
                    pb[bnk][:], sb_ones1f[:], srcr[:], start=s, stop=e)],
                pre=[(b.v_ms, vv)])
            b.consume("vector", bnk, rvb,
                      lambda v, dst=dst, bnk=bnk: v.tensor_copy(dst[:], pb[bnk][:]))
        lastv = 0
        for dt in range(KT):
            sl = dt % 4

            def vnorm(v, dt=dt, sl=sl):
                v.tensor_tensor(ln_tmp[:], ln_xb[:, 512 * dt:512 * (dt + 1)],
                                ln_rb[:], ALU.mult)
                return v.tensor_tensor(ln_xn[sl][:], ln_tmp[:], ln_mb[:],
                                       ALU.subtract)
            pre = [(s_lw[sl], ln_st["xn"][sl])] if ln_st["xn"][sl] else []
            lastv = b.vop(vnorm, pre=pre)
            b.dma("gpsimd",
                  dxn[128 * dt:128 * (dt + 1), 512 * tc:512 * (tc + 1)],
                  ln_xn[sl][:], s_lw[sl], pre=[(b.v_ms, lastv)])
            ln_st["xn"][sl] = s_lw[sl].n
        ln_st["xbf"] = lastv

    p0 = Alloc(nc, PERSIST)
    sb_xh = [p0.take(f"p0xh{i}", [128, D], F32) for i in range(4)]
    sb_xts = [p0.take(f"p0xts{i}", [128, 512], F32) for i in range(2)]
    sb_qzg = p0.take("p0qzg", [KT, NG // 8], I32)
    sb_zu = p0.take("p0zu", [KT, 4096], U16)
    sb_zb = p0.take("p0zb", [KT, 4096], BF16)
    sb_qzd = p0.take("p0qzd", [MT, D // 8], I32)
    sb_zud = p0.take("p0zud", [MT, D], U16)
    sb_zbd = p0.take("p0zbd", [MT, D], BF16)

    s_px = b.stream("sync")
    s_pw = [b.stream("gpsimd") for _ in range(2)]
    xts_guard = [0, 0]
    xh_guard = [0]
    for tg in range(4):
        wv = [b.dma("sync", sb_xh[i][:],
                    dx[(4 * tg + i) * 128:(4 * tg + i + 1) * 128, :], s_px,
                    pre=([(b.t_ms, xh_guard[0])] if i == 0 else ()))
              for i in range(4)]
        for dt in range(KT):
            bnk = dt % 2
            wf = b.uses[bnk]
            b.uses[bnk] += 1
            b.rdy[bnk].n += 1
            rv = b.rdy[bnk].n

            def trun(t, dt=dt, bnk=bnk, wf=wf, w=wv[3]):
                t.wait_ge(s_px.h, w)
                if wf > 0:
                    t.wait_ge(b.fre[bnk].h, wf)
                for i in range(4):
                    inst = t.transpose(pb[bnk][:, 128 * i:128 * (i + 1)],
                                       sb_xh[i][:, 128 * dt:128 * (dt + 1)],
                                       sb_idf[:])
                    if i == 3:
                        inst.then_inc(b.rdy[bnk].h, 1)
            b.emit("tensor", trun)
            st = sb_xts[dt % 2]
            fv = b.consume("vector", bnk, rv,
                           lambda v, st=st, bnk=bnk: v.tensor_copy(st[:], pb[bnk][:]),
                           pre=[(s_pw[dt % 2], xts_guard[dt % 2])])
            b.dma("gpsimd", dxT[0][128 * dt:128 * (dt + 1), 512 * tg:512 * (tg + 1)],
                  st[:], s_pw[dt % 2], pre=[(b.fre[bnk], fv)])
            xts_guard[dt % 2] = s_pw[dt % 2].n
        xh_guard[0] = b.tmark()
        ln_load(dxT[0], tg, None, [(s_pw[0], s_pw[0].n), (s_pw[1], s_pw[1].n)])
        ln_compute(tg)

    s_zs = b.stream("sync")
    wq = b.dma("sync", sb_qzg[:], dqz_gu, s_zs)
    zb_guard = 0
    for ch in range(6):
        c0 = 4096 * ch
        w = min(4096, NG - c0)

        def zrun(v, c0=c0, w=w):
            qz = sb_qzg[:].bitcast(U16)
            for o in range(0, w, SW):
                u0 = ((c0 + o) // SW) * 64
                for i in range(4):
                    v.tensor_scalar(sb_zu[:, o + 64 * i:o + 64 * (i + 1)],
                                    qz[:, u0:u0 + 64], 4 * i, 15,
                                    ALU.logical_shift_right, ALU.bitwise_and)
            return v.tensor_copy(sb_zb[:, :w], sb_zu[:, :w])
        vv = b.vop(zrun, pre=[(s_zs, max(wq, zb_guard))])
        b.dma("sync", dzs_gu[:, c0:c0 + w], sb_zb[:, :w], s_zs, pre=[(b.v_ms, vv)])
        zb_guard = s_zs.n

    wq = b.dma("sync", sb_qzd[:], dqz_dn, s_zs)

    def zdrun(v):
        qz = sb_qzd[:].bitcast(U16)
        for o in range(0, D, SW):
            u0 = (o // SW) * 64
            for i in range(4):
                v.tensor_scalar(sb_zud[:, o + 64 * i:o + 64 * (i + 1)],
                                qz[:, u0:u0 + 64], 4 * i, 15,
                                ALU.logical_shift_right, ALU.bitwise_and)
        return v.tensor_copy(sb_zbd[:], sb_zud[:])
    vv = b.vop(zdrun, pre=[(s_zs, wq)])
    b.dma("sync", dzs_dn[:], sb_zbd[:], s_zs, pre=[(b.v_ms, vv)])

    b.barrier()

    if upto == "p0":
        emit_tap(dxT[0], D, T)
        return finish()
    if upto == "ln1":
        emit_tap(dxn.bitcast(F32), D, T // 2)
        return finish()

    p1 = Alloc(nc, PERSIST)
    sb_qw = [p1.take(f"dqqw{i}", [128, NG // 8], I32) for i in range(2)]
    sb_srow = [p1.take(f"dqsr{i}", [2, 8192], BF16) for i in range(2)]
    sb_sbc = [p1.take(f"dqsbc{i}", [128, 1024], BF16) for i in range(8)]
    sb_nib = [p1.take(f"dqnib{i}", [128, 512], U16) for i in range(2)]
    sb_nbb = p1.take("dqnbb", [128, 512], BF16)
    sb_wst = [p1.take(f"dqwst{i}", [128, 2048], BF16) for i in range(2)]

    s_qw = [b.stream("sync") for _ in range(2)]
    s_sr = [b.stream("gpsimd") for _ in range(2)]
    s_wc = [b.stream("sync") for _ in range(2)]
    qw_guard = [0, 0]
    srow_guard = [(0, 0), (0, 0)]
    sbc_guard = [0] * 8
    nib_guard = [0, 0]
    wst_guard = [0, 0]
    subctr = [0]

    def emit_dequant(dqw, dcache, n_i32, n_pos, dsrows, dzrows, n_tiles, dn_mode):
        qsel = (dqw.rearrange("(j r s) c -> j s r c", r=64, s=4) if dn_mode else None)
        for g in range(n_tiles):
            qs = sb_qw[g % 2]
            if dn_mode:
                j, q = g // 2, g % 2
                wqv = b.dma("sync", qs[:, :n_i32],
                            qsel[j, 2 * q:2 * q + 2, :, :],
                            s_qw[g % 2], pre=[(b.v_ms, qw_guard[g % 2])])
            else:
                wqv = b.dma("sync", qs[:, :n_i32], dqw[128 * g:128 * (g + 1), :],
                            s_qw[g % 2], pre=[(b.v_ms, qw_guard[g % 2])])
            for ch in range(0, n_pos, 2048):
                w = min(2048, n_pos - ch)
                chi = (ch // 2048) % 2
                sr = sb_srow[chi]
                gb, gv = srow_guard[chi]
                nrow = 2 if dn_mode else 1
                g0 = (g // 2) * 2 if dn_mode else g
                srv = sr[:nrow, :].rearrange("a (u two c) -> a u two c", two=2, c=512)
                wsv = b.dma("gpsimd", srv[:, :w // 512, 0, :],
                            dsrows[g0:g0 + nrow, ch:ch + w]
                            .rearrange("a (u c) -> a u c", c=512),
                            s_sr[chi], pre=[(b.rdy[gb], gv)])
                wzv = b.dma("gpsimd", srv[:, :w // 512, 1, :],
                            dzrows[g0:g0 + nrow, ch:ch + w]
                            .rearrange("a (u c) -> a u c", c=512),
                            s_sr[chi])
                wstg = sb_wst[chi]
                last_rv = (6, 0)
                for sub in range(0, w, 512):
                    si = subctr[0] % 8
                    bnk = (2 * subctr[0]) % 8
                    bnkz = bnk + 1
                    subctr[0] += 1
                    sbc = sb_sbc[si]
                    lhsb = sb_stair2[:] if dn_mode else sb_ones1b[:]
                    rv = b.group(bnk, [
                        (lambda t, s, e, sr=sr, sub=sub, bnk=bnk, lhsb=lhsb,
                         nrow=nrow:
                         t.matmul(pb[bnk][:], lhsb,
                                  sr[:nrow, 2 * sub:2 * sub + 512],
                                  start=s, stop=e))],
                        pre=[(s_sr[chi], wzv)])
                    rvz = b.group(bnkz, [
                        (lambda t, s, e, sr=sr, sub=sub, bnkz=bnkz, lhsb=lhsb,
                         nrow=nrow:
                         t.matmul(pb[bnkz][:], lhsb,
                                  sr[:nrow, 2 * sub + 512:2 * sub + 1024],
                                  start=s, stop=e))])
                    last_rv = (bnkz, rvz)
                    fv = b.consume("scalar", bnk, rv,
                                   lambda a, sbc=sbc, bnk=bnk:
                                   a.activation(sbc[:, :512], pb[bnk][:],
                                                AF.Identity),
                                   pre=[(b.v_ms, sbc_guard[si])])
                    fvz = b.consume("scalar", bnkz, rvz,
                                    lambda a, sbc=sbc, bnkz=bnkz:
                                    a.activation(sbc[:, 512:], pb[bnkz][:],
                                                 AF.Identity))

                    nslot = subctr[0] % 2
                    nib = sb_nib[nslot]

                    def drun(v, qs=qs, sbc=sbc, pos=ch + sub, sub=sub,
                             nib=nib, n_i32=n_i32, wstg=wstg):
                        qu = qs[:, :n_i32].bitcast(U16)
                        for jj in range(2):
                            u0 = ((pos + SW * jj) // SW) * 64
                            for i in range(4):
                                v.tensor_scalar(
                                    nib[:, SW * jj + 64 * i:SW * jj + 64 * (i + 1)],
                                    qu[:, u0:u0 + 64], 4 * i, 15,
                                    ALU.logical_shift_right, ALU.bitwise_and)
                        zbv = sbc[:].rearrange("p (u two c) -> p u two c",
                                               two=2, c=512)
                        v.tensor_tensor(sb_nbb[:], nib[:], zbv[:, 0, 1, :],
                                        ALU.subtract)
                        return v.tensor_tensor(wstg[:, sub:sub + 512],
                                               sb_nbb[:], zbv[:, 0, 0, :], ALU.mult)
                    vv = b.vop(drun, pre=[(s_qw[g % 2], wqv),
                                          (s_wc[chi], wst_guard[chi]),
                                          (b.fre[bnk], fv), (b.fre[bnkz], fvz)])
                    sbc_guard[si] = vv
                srow_guard[chi] = last_rv
                vvw = b.v_ms.n
                if not dn_mode:
                    g4 = dcache.rearrange("(s p) (kt c) -> s p kt c", p=128, kt=KT)
                    for o in range(0, w, SW):
                        s_idx = (ch + o) // SW
                        b.dma("sync", g4[s_idx, :, g, :],
                              wstg[:, o:o + SW], s_wc[chi],
                              pre=([(b.v_ms, vvw)] if o == 0 else ()))
                else:
                    d4 = dcache.rearrange("(mt oc p) c -> mt oc p c", oc=8, p=128)
                    for o in range(0, w, 512):
                        b.dma("sync", d4[g, (ch + o) // 512, :, :],
                              wstg[:, o:o + 512], s_wc[chi],
                              pre=([(b.v_ms, vvw)] if o == 0 else ()))
                wst_guard[chi] = s_wc[chi].n
            qw_guard[g % 2] = b.v_ms.n

    emit_dequant(dqw_gu, dguc, NG // 8, NG, ds_gu, dzs_gu, KT, False)
    emit_dequant(dqw_dn, ddnc, D // 8, D, ds_dn, dzs_dn, MT, True)

    b.barrier()

    s_gx = b.stream("sync")
    s_rg = [b.stream("sync") for _ in range(4)]
    s_hw = [b.stream("gpsimd") for _ in range(4)]
    s_hh = b.stream("sync")
    s_wd = [b.stream("scalar") for _ in range(8)]
    s_xr = b.stream("gpsimd")
    s_os = [b.stream("gpsimd") for _ in range(8)]

    def emit_ffn(dsrc, ddst, ln_dst=False):
        lo = Alloc(nc, PERSIST)
        xnh = lo.take("gxn", [128, KT * 1024], BF16)
        ring = [lo.take(f"grg{i}", [128, KT * SW], BF16) for i in range(4)]
        silub = [lo.take(f"gsl{i}", [128, 2048], BF16) for i in range(2)]
        hstg = [lo.take(f"ghs{i}", [128, 1024], BF16) for i in range(4)]

        ring_guard = [(0, 0)] * 4
        silu_guard = [(0, 0), (0, 0)]
        silu_fv = {}
        hmul_fv = {}
        hst_guard = [0, 0, 0, 0]
        rot = [0]

        for half in range(2):
            pre0 = ([(b.rdy[k], b.rdy[k].n) for k in range(8)] if half == 1 else [])
            xwv = [b.dma("sync", xnh[:, 1024 * k:1024 * (k + 1)],
                         dxn[128 * k:128 * (k + 1), HALF * half:HALF * (half + 1)],
                         s_gx, pre=(pre0 if k == 0 else ()))
                   for k in range(KT)]

            for jp in range(2 * NJ):
                j, side = jp // 2, jp % 2
                slot = ring[jp % 4]
                pos0 = side * INTER + SW * j
                gb, gv = ring_guard[jp % 4]
                s_idx = pos0 // SW
                wv = b.dma("sync", slot[:],
                           dguc[128 * s_idx:128 * (s_idx + 1), :],
                           s_rg[jp % 4], pre=[(b.rdy[gb], gv)])
                for q in range(2):
                    for tc in range(2):
                        bnk = rot[0]
                        rot[0] = (rot[0] + 1) % 6
                        mms = [(lambda t, s, e, k=k, q=q, tc=tc, slot=slot, bnk=bnk:
                                t.matmul(pb[bnk][:],
                                         slot[:, SW * k + 128 * q:SW * k + 128 * (q + 1)],
                                         xnh[:, 1024 * k + 512 * tc:1024 * k + 512 * (tc + 1)],
                                         start=s, stop=e))
                               for k in range(KT)]
                        rv = b.group(bnk, mms,
                                     pre=([(s_rg[jp % 4], wv), (s_gx, xwv[-1])]
                                          if (q == 0 and tc == 0) else ()))
                        ring_guard[jp % 4] = (bnk, rv)
                        if side == 0:
                            sl = silub[j % 2]
                            gb2, gv2 = silu_guard[j % 2]
                            sfv = b.consume("scalar", bnk, rv,
                                            lambda a, sl=sl, q=q, tc=tc, bnk=bnk:
                                            a.activation(
                                                sl[:, 1024 * q + 512 * tc:1024 * q + 512 * (tc + 1)],
                                                pb[bnk][:], AF.Silu),
                                            pre=([(b.fre[gb2], gv2)]
                                                 if (q == 0 and tc == 0) else ()))
                            silu_fv[(q, tc)] = (bnk, sfv)
                        else:
                            hs = hstg[2 * (j % 2) + q]
                            sl = silub[j % 2]

                            def hmul(v, hs=hs, sl=sl, q=q, tc=tc, bnk=bnk):
                                return v.tensor_tensor(
                                    hs[:, 512 * tc:512 * (tc + 1)], pb[bnk][:],
                                    sl[:, 1024 * q + 512 * tc:1024 * q + 512 * (tc + 1)],
                                    ALU.mult)
                            sb_, sv_ = silu_fv[(q, tc)]
                            fv = b.consume("vector", bnk, rv, hmul,
                                           pre=([(b.fre[sb_], sv_),
                                                 (s_hw[2 * (j % 2) + q],
                                                  hst_guard[2 * (j % 2) + q])]
                                                if tc == 0 else [(b.fre[sb_], sv_)]))
                            hmul_fv[(j % 2, q)] = (bnk, fv)
                            if tc == 1:
                                silu_guard[j % 2] = (bnk, fv)
                if side == 1:
                    dh4 = dh.rearrange("(mt tq p) c -> mt tq p c", tq=4, p=128)
                    for q in range(2):
                        hb, hv = hmul_fv[(j % 2, q)]
                        for tc in range(2):
                            b.dma("gpsimd",
                                  dh4[2 * j + q, 2 * half + tc, :, :],
                                  hstg[2 * (j % 2) + q][:, 512 * tc:512 * (tc + 1)],
                                  s_hw[2 * (j % 2) + q],
                                  pre=([(b.fre[hb], hv)] if tc == 0 else ()))
                        hst_guard[2 * (j % 2) + q] = s_hw[2 * (j % 2) + q].n

        b.barrier()
        if K_UPTO_GATE[0]:
            return

        lo = Alloc(nc, PERSIST)
        hh = lo.take("dnh", [128, MT * 512], BF16)
        wdn = [lo.take(f"dnw{i}", [128, 512], BF16) for i in range(8)]
        xres = [lo.take(f"dnxr{i}", [128, 512], F32) for i in range(8)]
        ost = [lo.take(f"dnos{i}", [128, 512], F32) for i in range(8)]

        wdn_guard = [0] * 8
        xres_guard = [0] * 8
        ost_guard = [0] * 8
        dsel = ddnc.rearrange("(mt oc p) c -> mt oc p c", oc=8, p=128)
        dh4 = dh.rearrange("(mt tq p) c -> mt tq p c", tq=4, p=128)
        xsel = dsrc.rearrange("(c r s) t -> c s r t", r=64, s=4)
        osel = ddst.rearrange("(c r s) t -> c s r t", r=64, s=4)

        for tq in range(4):
            pre0 = [(b.rdy[k], b.rdy[k].n) for k in range(8)]
            hwv = [b.dma("sync", hh[:, 512 * mt:512 * (mt + 1)],
                         dh4[mt, tq, :, :],
                         s_hh, pre=(pre0 if mt == 0 else ()))
                   for mt in range(MT)]

            for og in range(8):
                base = 4 * (og % 2)
                banks = [base + i for i in range(4)]
                for ot in range(4):
                    bnk = banks[ot]
                    jj, q = 2 * og + ot // 2, ot % 2
                    b.dma("gpsimd", xres[bnk][:],
                          xsel[jj, 2 * q:2 * q + 2, :, 512 * tq:512 * (tq + 1)],
                          s_xr, pre=[(b.fre[bnk], xres_guard[bnk])])
                xr_done = s_xr.n
                open_wf = []
                rvs = {}
                for bnk in banks:
                    open_wf.append((bnk, b.uses[bnk]))
                    b.uses[bnk] += 1
                    b.rdy[bnk].n += 1
                    rvs[bnk] = b.rdy[bnk].n
                for mt in range(MT):
                    ws = wdn[mt % 8]
                    wv = b.dma("scalar", ws[:], dsel[mt, og, :, :],
                               s_wd[mt % 8], pre=[(b.t_ms, wdn_guard[mt % 8])])

                    def dmm(t, mt=mt, ws=ws, wv=wv, banks=tuple(banks),
                            open_wf=tuple(open_wf), hv=hwv[-1]):
                        t.wait_ge(s_wd[mt % 8].h, wv)
                        if mt == 0:
                            t.wait_ge(s_hh.h, hv)
                            for bnk, wf in open_wf:
                                if wf > 0:
                                    t.wait_ge(b.fre[bnk].h, wf)
                        for ot in range(4):
                            inst = t.matmul(
                                pb[banks[ot]][:], ws[:, 128 * ot:128 * (ot + 1)],
                                hh[:, 512 * mt:512 * (mt + 1)],
                                start=(mt == 0), stop=(mt == MT - 1))
                            if mt == MT - 1:
                                inst.then_inc(b.rdy[banks[ot]].h, 1)
                        if mt != MT - 1:
                            inst.then_inc(b.t_ms.h, 1)
                        else:
                            t.nop().then_inc(b.t_ms.h, 1)
                    b.t_ms.n += 1
                    b.emit("tensor", dmm)
                    wdn_guard[mt % 8] = b.t_ms.n
                for ot in range(4):
                    bnk = banks[ot]
                    jj, q = 2 * og + ot // 2, ot % 2
                    if "noxres" in dbg:
                        fv = b.consume("vector", bnk, rvs[bnk],
                                       lambda v, bnk=bnk: v.tensor_scalar_mul(
                                           ost[bnk][:], pb[bnk][:], 0.5),
                                       pre=[(s_xr, xr_done),
                                            (s_os[bnk], ost_guard[bnk])])
                    else:
                        fv = b.consume("vector", bnk, rvs[bnk],
                                       lambda v, bnk=bnk: v.scalar_tensor_tensor(
                                           ost[bnk][:], pb[bnk][:], 0.5, xres[bnk][:],
                                           ALU.mult, ALU.add),
                                       pre=[(s_xr, xr_done),
                                            (s_os[bnk], ost_guard[bnk])])
                    xres_guard[bnk] = fv
                    b.dma("gpsimd",
                          osel[jj, 2 * q:2 * q + 2, :, 512 * tq:512 * (tq + 1)],
                          ost[bnk][:], s_os[bnk], pre=[(b.fre[bnk], fv)])
                    ost_guard[bnk] = s_os[bnk].n
                if ln_dst:
                    ln_load(ddst, tq, og,
                            [(s_os[bk], ost_guard[bk]) for bk in banks])
            if ln_dst:
                ln_compute(tq)
        b.barrier()

    def emit_attn(dsrc, ddst):
        lo = Alloc(nc, PERSIST)
        abuf = lo.take("abuf", [128, max(n_mask, 1) * 128], BF16)
        A_END = lo.off
        w1s = [lo.take(f"aw1{i}", [128, KT * 128], BF16) for i in range(4)]
        w2t = [lo.take(f"aw2{i}", [128, KT * 512], BF16) for i in range(2)]
        amask = [lo.take(f"amk{i}", [128, 128], F32) for i in range(2)]

        nz_of_chunk = []
        for tcb in range(8):
            ss = {}
            for tt in range(4 * tcb, 4 * tcb + 4):
                for (s_t, slot) in nz_of_t[tt]:
                    ss.setdefault(s_t, []).append((tt, slot))
            nz_of_chunk.append(sorted(ss.items()))

        s_w2 = [b.stream("sync") for _ in range(2)]
        s_w1 = [b.stream("sync") for _ in range(4)]
        s_mk = [b.stream("sync") for _ in range(2)]
        w1_guard = [0] * 4
        w2_guard = [0, 0]
        mk_guard = [("v", 0, 0), ("v", 0, 0)]
        arot = [0]
        mki = [0]
        for tcb in range(8):
            w2s = w2t[tcb % 2]
            wv2 = [b.dma("sync", w2s[:, 512 * dt:512 * (dt + 1)],
                         dw2bf[512 * tcb:512 * (tcb + 1), 128 * dt:128 * (dt + 1)],
                         s_w2[tcb % 2], transpose=True,
                         pre=([(b.t_ms, w2_guard[tcb % 2]), (s_w2bf, w2bf_done)]
                              if dt == 0 else ()))
                   for dt in range(KT)]
            si = 0
            for (s_t, uses) in nz_of_chunk[tcb]:
                w1b = w1s[si % 4]
                wv1 = b.dma("sync", w1b[:],
                            dw1c[128 * s_t:128 * (s_t + 1), :],
                            s_w1[si % 4],
                            pre=[(b.t_ms, w1_guard[si % 4]),
                                 (s_w2bf, w1c_done)])
                bnk = arot[0]
                arot[0] = (arot[0] + 1) % 6
                mms = [(lambda t, s, e, k=k, w1b=w1b, w2s=w2s, bnk=bnk:
                        t.matmul(pb[bnk][:], w1b[:, 128 * k:128 * (k + 1)],
                                 w2s[:, 512 * k:512 * (k + 1)],
                                 start=s, stop=e)) for k in range(KT)]
                rv = b.group(bnk, mms, pre=[(s_w2[tcb % 2], wv2[-1]),
                                            (s_w1[si % 4], wv1)])
                tm = b.tmark()
                w1_guard[si % 4] = tm
                w2_guard[tcb % 2] = tm
                nuses = len(uses)
                for ui, (tt, slot) in enumerate(uses):
                    mi = mki[0] % 2
                    mk = amask[mi]
                    kind, gb3, gv3 = mk_guard[mi]
                    wmk = b.dma("sync", mk[:], dmask[128 * slot:128 * (slot + 1), :],
                                s_mk[mi],
                                pre=[(b.v_ms, gv3) if kind == "v"
                                     else (b.fre[gb3], gv3)])
                    mki[0] += 1
                    qq = tt - 4 * tcb

                    def amul(v, mk=mk, qq=qq, slot=slot, bnk=bnk):
                        return v.tensor_tensor(
                            abuf[:, 128 * slot:128 * (slot + 1)],
                            pb[bnk][:, 128 * qq:128 * (qq + 1)], mk[:], ALU.mult)
                    if ui == nuses - 1:
                        fv = b.consume("vector", bnk, rv, amul,
                                       pre=[(s_mk[mi], wmk)])
                        mk_guard[mi] = ("f", bnk, fv)
                    else:
                        vv = b.vop(amul, pre=[(s_mk[mi], wmk),
                                              (b.rdy[bnk], rv)])
                        mk_guard[mi] = ("v", 0, vv)
                si += 1
        b.barrier()

        lo = Alloc(nc, A_END)
        xn2 = [lo.take(f"yxn{i}", [128, KT * 512], BF16) for i in range(2)]
        gsb = [lo.take(f"ygs{i}", [128, 512], F32) for i in range(4)]
        x1t = [lo.take(f"yx1{i}", [128, 512], F32) for i in range(4)]
        x2t = [lo.take(f"yx2{i}", [128, 512], F32) for i in range(4)]

        s_x2l = b.stream("sync")
        s_x1l = [b.stream("gpsimd") for _ in range(4)]
        s_x2w = [b.stream("gpsimd") for _ in range(4)]
        yrot = [0]
        gs_guard = [0] * 4
        x1_guard = [0] * 4
        x2_guard = [0] * 4
        xn2_guard = [0, 0]
        for tk in range(4):
            sl2 = tk % 2
            xw = b.dma("sync", xn2[sl2][:].rearrange("p (k c) -> p k c", c=512),
                       dxn[:, 512 * tk:512 * (tk + 1)]
                       .rearrange("(k p) c -> p k c", p=128),
                       s_x2l, pre=[(b.t_ms, xn2_guard[sl2])])
            for J in range(32):
                pairs = nz_of_t[J]
                assert pairs, "empty attn row-tile unsupported"
                bnk = yrot[0]
                yrot[0] = (yrot[0] + 1) % 6
                mms = [(lambda t, s, e, st=st, slot=slot, bnk=bnk, sl2=sl2:
                        t.matmul(pb[bnk][:],
                                 abuf[:, 128 * slot:128 * (slot + 1)],
                                 xn2[sl2][:, 512 * st:512 * (st + 1)],
                                 start=s, stop=e))
                       for (st, slot) in pairs]
                rv = b.group(bnk, mms, pre=([(s_x2l, xw)] if J == 0 else []))
                gi = J % 4
                gs = gsb[gi]
                fvg = b.consume("scalar", bnk, rv,
                                lambda a, gs=gs, J=J, bnk=bnk: a.activation(
                                    gs[:], pb[bnk][:], AF.Gelu,
                                    bias=sb_b2[:, J:J + 1]),
                                pre=[(b.v_ms, gs_guard[gi])])
                wx1 = b.dma("gpsimd", x1t[gi][:],
                            dsrc[128 * J:128 * (J + 1), 512 * tk:512 * (tk + 1)],
                            s_x1l[gi], pre=[(b.v_ms, x1_guard[gi])])

                def yadd(v, gs=gs, gi=gi):
                    return v.tensor_tensor(x2t[gi][:], x1t[gi][:], gs[:], ALU.add)
                vv = b.vop(yadd, pre=[(b.fre[bnk], fvg), (s_x1l[gi], wx1),
                                      (s_x2w[gi], x2_guard[gi])])
                gs_guard[gi] = vv
                x1_guard[gi] = vv
                b.dma("gpsimd",
                      ddst[128 * J:128 * (J + 1), 512 * tk:512 * (tk + 1)],
                      x2t[gi][:], s_x2w[gi], pre=[(b.v_ms, vv)])
                x2_guard[gi] = s_x2w[gi].n
                if J % 4 == 3:
                    ln_load(ddst, tk, J // 4,
                            [(s_x2w[g2], x2_guard[g2]) for g2 in range(4)])
            xn2_guard[sl2] = b.tmark()
            ln_compute(tk)
        b.barrier()

    def emit_final():
        lo = Alloc(nc, PERSIST)
        fst = [lo.take(f"ffs{i}", [128, KT * 128], F32) for i in range(2)]
        ostg = [lo.take(f"fos{i}", [128, D], F32) for i in range(2)]
        s_fl = [b.stream("sync") for _ in range(2)]
        s_fo = [b.stream("gpsimd") for _ in range(2)]
        f_guard = [0, 0]
        o_guard = [0, 0]
        for tt in range(16):
            fs = fst[tt % 2]
            wv = b.dma("sync", fs[:].rearrange("p (dt c) -> p dt c", dt=KT),
                       dxT[3][:, 128 * tt:128 * (tt + 1)]
                       .rearrange("(dt p) c -> p dt c", p=128),
                       s_fl[tt % 2], pre=[(b.t_ms, f_guard[tt % 2])])
            og = ostg[tt % 2]
            fvs = {}
            for bg in range(8):
                bnk = bg % 2 + 2
                wf = b.uses[bnk]
                b.uses[bnk] += 1
                b.rdy[bnk].n += 1
                rv = b.rdy[bnk].n

                def trun(t, bg=bg, bnk=bnk, wf=wf, wv=wv, fs=fs, tt=tt):
                    t.wait_ge(s_fl[tt % 2].h, wv)
                    if wf > 0:
                        t.wait_ge(b.fre[bnk].h, wf)
                    for i in range(4):
                        inst = t.transpose(
                            pb[bnk][:, 128 * i:128 * (i + 1)],
                            fs[:, 128 * (4 * bg + i):128 * (4 * bg + i + 1)],
                            sb_idf[:])
                        if i == 3:
                            inst.then_inc(b.rdy[bnk].h, 1)
                b.emit("tensor", trun)
                fvs[bnk] = b.consume(
                    "vector", bnk, rv,
                    lambda v, og=og, bg=bg, bnk=bnk:
                    v.tensor_copy(og[:, 512 * bg:512 * (bg + 1)], pb[bnk][:]),
                    pre=([(s_fo[tt % 2], o_guard[tt % 2])] if bg == 0 else ()))
            f_guard[tt % 2] = b.tmark()
            b.dma("gpsimd", dout[128 * tt:128 * (tt + 1), :], og[:],
                  s_fo[tt % 2], pre=[(b.fre[2], fvs[2]), (b.fre[3], fvs[3])])
            o_guard[tt % 2] = s_fo[tt % 2].n

    if upto == "p1":
        emit_tap(ddnc.bitcast(F32), 2048, 2048)
        return finish()
    if upto == "gate":
        K_UPTO_GATE[0] = True
    emit_ffn(dxT[0], dxT[1], ln_dst=True)
    if upto == "gate":
        emit_tap(dh.bitcast(F32), 4096, 1024)
        return finish()
    if upto == "ffn1":
        emit_tap(dxT[1], D, T)
        return finish()
    emit_attn(dxT[1], dxT[2])
    if upto == "attn":
        emit_tap(dxT[2], D, T)
        return finish()
    emit_ffn(dxT[2], dxT[3])
    emit_final()
    b.barrier()

    with nc.Block() as blk:
        for eng in B.ENGS:
            getattr(blk, eng)(lambda e, eng=eng: [f(e) for f in b.ops[eng]])
    return nc



def _host_prep(inputs):
    mask = np.asarray(inputs["L0_attn_sparse_mask"], dtype=np.float32)
    tiles = mask.reshape(32, 128, 32, 128)
    nzmap = (tiles != 0).any(axis=(1, 3))
    nz_of_t = [[] for _ in range(32)]
    packed = []
    slot = 0
    for tj in range(32):
        for si in range(32):
            if nzmap[si, tj]:
                packed.append(np.ascontiguousarray(tiles[si, :, tj, :]))
                nz_of_t[tj].append((si, slot))
                slot += 1
    mask_nz = (np.concatenate(packed, axis=0) if packed
               else np.zeros((128, 128), np.float32))
    n_mask = max(slot, 1)

    perm_gu = np.concatenate([_pos2dense(INTER), INTER + _pos2dense(INTER)])
    perm_dn = _pos2dense(D)
    s_gu = np.ascontiguousarray(
        np.asarray(inputs["L0_ffn_scales_gu"], np.float32)[:, perm_gu])
    s_dn = np.ascontiguousarray(
        np.asarray(inputs["L0_ffn_scales_dn"], np.float32)[:, perm_dn])

    ind_gu = np.zeros((128, KT * KT), dtype=bfdt)
    for g in range(KT):
        ind_gu[:, KT * g + g] = 1.0
    ind_dn = np.zeros((128, NJ * MT), dtype=bfdt)
    p = np.arange(128)
    for j in range(NJ):
        ind_dn[p, MT * j + 2 * j + (p % 64) // 32] = 1.0

    common = dict(
        w1=np.ascontiguousarray(np.asarray(inputs["L0_attn_w1"], np.float32)),
        w2=np.ascontiguousarray(np.asarray(inputs["L0_attn_w2"], np.float32)),
        b2=np.ascontiguousarray(np.asarray(inputs["L0_attn_b2"], np.float32)),
        mask_nz=np.ascontiguousarray(mask_nz),
        qw_gu=np.ascontiguousarray(np.asarray(inputs["L0_ffn_qweight_gu"], np.int32)),
        qz_gu=np.ascontiguousarray(np.asarray(inputs["L0_ffn_qzeros_gu"], np.int32)),
        s_gu=s_gu,
        qw_dn=np.ascontiguousarray(np.asarray(inputs["L0_ffn_qweight_dn"], np.int32)),
        qz_dn=np.ascontiguousarray(np.asarray(inputs["L0_ffn_qzeros_dn"], np.int32)),
        s_dn=s_dn,
        identf=np.eye(128, dtype=np.float32),
        stair2=np.stack([((np.arange(128) % 64) // 32 == k).astype(bfdt)
                         for k in range(2)]),
        ones_col=np.ones((128, 1), dtype=bfdt),
        ones_row_b=np.ones((1, 128), dtype=bfdt),
        ones_row_f=np.ones((1, 128), dtype=np.float32),
        ind_gu=ind_gu, ind_dn=ind_dn,
    )
    return nz_of_t, n_mask, common


def kernel(**inputs):
    nz_of_t, n_mask, common = _host_prep(inputs)
    x = np.asarray(inputs["L0_attn_x"], np.float32).reshape(SEQ, D)
    nc = build_program(nz_of_t, n_mask)
    in_maps = []
    for c in range(NCORES):
        m = dict(common)
        m["x"] = np.ascontiguousarray(x[T * c:T * (c + 1), :])
        in_maps.append(m)
    res = run_bass_kernel_spmd(nc, in_maps, core_ids=list(range(NCORES)))
    out = np.concatenate([res.results[c]["out"] for c in range(NCORES)], axis=0)
    return out.reshape(1, SEQ, D).astype(np.float32)


# version 14
# speedup vs baseline: 1.0199x; 1.0041x over previous
import numpy as np
import ml_dtypes
import concourse.bass as bass
import concourse.mybir as mybir
from concourse.bass_utils import run_bass_kernel_spmd

F32 = mybir.dt.float32
BF16 = mybir.dt.bfloat16
I32 = mybir.dt.int32
U16 = mybir.dt.uint16
AF = mybir.ActivationFunctionType
ALU = mybir.AluOpType
bfdt = ml_dtypes.bfloat16

NCORES = 8
D = 4096
SEQ = 16384
T = SEQ // NCORES
INTER = 11008
NG = 2 * INTER
NJ = INTER // 256
SW = 256
MT = 2 * NJ
KT = D // 128
HALF = T // 2
EPS = 1e-5
SBUF_BASE = 16512
SBUF_TOP = 229300


def _pos2dense(n):
    p = np.arange(n)
    j, rem = p // SW, p % SW
    i, r = rem // 64, rem % 64
    return j * SW + 4 * r + i


class Alloc:
    def __init__(self, nc, base, top=SBUF_TOP):
        self.nc, self.off, self.top = nc, base, top

    def take(self, name, shape, dtype, align=64):
        esz = {F32: 4, BF16: 2, I32: 4, U16: 2}[dtype]
        self.off = -(-self.off // align) * align
        h = self.nc.alloc_sbuf_tensor_at(name, list(shape), dtype, offset=self.off)
        self.off += shape[-1] * esz
        assert self.off <= self.top, f"SBUF overflow at {name}: {self.off}"
        return h


class Sem:
    __slots__ = ("h", "n")

    def __init__(self, h):
        self.h, self.n = h, 0

    def inc(self, by=1):
        self.n += by
        return self.n


def _pre(e, pre):
    for s, v in pre:
        if v > 0:
            e.wait_ge(s.h, v)


class B:
    ENGS = ("tensor", "vector", "scalar", "sync", "gpsimd")

    def __init__(self, nc):
        self.nc = nc
        self.ops = {e: [] for e in self.ENGS}
        self._si = 0
        self.rdy = [self.sem() for _ in range(8)]
        self.fre = [self.sem() for _ in range(8)]
        self.uses = [0] * 8
        self.v_ms = self.sem()
        self.a_ms = self.sem()
        self.g_ms = self.sem()
        self.t_ms = self.sem()
        self.bar = self.sem()
        self.qstreams = {"sync": [], "scalar": [], "gpsimd": []}

    def stream(self, queue):
        s = self.sem()
        self.qstreams[queue].append(s)
        return s

    def sem(self):
        h = self.nc.semaphore(f"sm{self._si}").__enter__()
        self._si += 1
        return Sem(h)

    def emit(self, eng, fn):
        self.ops[eng].append(fn)

    def wait(self, eng, sem, val):
        if val > 0:
            self.ops[eng].append(lambda e, s=sem, v=val: e.wait_ge(s.h, v))

    def group(self, bnk, mms, pre=()):
        wf = self.uses[bnk]
        self.uses[bnk] += 1
        self.rdy[bnk].n += 1
        rv = self.rdy[bnk].n

        def run(t, mms=mms, bnk=bnk, wf=wf, pre=tuple(pre)):
            _pre(t, pre)
            if wf > 0:
                t.wait_ge(self.fre[bnk].h, wf)
            n = len(mms)
            for i, m in enumerate(mms):
                inst = m(t, i == 0, i == n - 1)
                if i == n - 1:
                    inst.then_inc(self.rdy[bnk].h, 1)
        self.emit("tensor", run)
        return rv

    def consume(self, eng, bnk, rv, fn, pre=()):
        self.fre[bnk].n += 1

        def run(e, bnk=bnk, rv=rv, fn=fn, pre=tuple(pre)):
            _pre(e, pre)
            e.wait_ge(self.rdy[bnk].h, rv)
            fn(e).then_inc(self.fre[bnk].h, 1)
        self.emit(eng, run)
        return self.fre[bnk].n

    def dma(self, queue, out, in_, sem, pre=(), **kw):
        v = sem.inc(16)

        def run(e, out=out, in_=in_, kw=kw, sem=sem, pre=tuple(pre)):
            _pre(e, pre)
            e.dma_start(out=out, in_=in_, **kw).then_inc(sem.h, 16)
        self.emit(queue, run)
        return v

    def vop(self, fn, pre=()):
        self.v_ms.n += 1

        def run(v, fn=fn, pre=tuple(pre)):
            _pre(v, pre)
            fn(v).then_inc(self.v_ms.h, 1)
        self.emit("vector", run)
        return self.v_ms.n

    def gop(self, fn, pre=()):
        self.g_ms.n += 1

        def run(g, fn=fn, pre=tuple(pre)):
            _pre(g, pre)
            fn(g).then_inc(self.g_ms.h, 1)
        self.emit("gpsimd", run)
        return self.g_ms.n

    def aop(self, fn, pre=()):
        self.a_ms.n += 1

        def run(a, fn=fn, pre=tuple(pre)):
            _pre(a, pre)
            fn(a).then_inc(self.a_ms.h, 1)
        self.emit("scalar", run)
        return self.a_ms.n

    def tmark(self):
        self.t_ms.n += 1
        self.emit("tensor", lambda t: t.nop().then_inc(self.t_ms.h, 1))
        return self.t_ms.n

    def barrier(self):
        tgt = self.bar.n + 5
        self.bar.n = tgt
        snap = {q: [(s, s.n) for s in ss if s.n > 0]
                for q, ss in self.qstreams.items()}
        for eng in self.ENGS:
            def run(e, eng=eng, snap=snap):
                for s, v in snap.get(eng, ()):
                    e.wait_ge(s.h, v)
                if eng in ("vector", "gpsimd"):
                    e.engine_nop().then_inc(self.bar.h, 1)
                else:
                    e.nop().then_inc(self.bar.h, 1)
            self.emit(eng, run)
        for eng in self.ENGS:
            self.wait(eng, self.bar, tgt)


def build_program(nz_of_t, n_mask, upto="full", dbg=()):
    K_UPTO_GATE = [False]
    nc = bass.Bass("TRN2", target_bir_lowering=False, debug=False)
    nc.detect_race_conditions = False

    dx = nc.dram_tensor("x", [T, D], F32, kind="ExternalInput").ap()
    dw1 = nc.dram_tensor("w1", [D, D], F32, kind="ExternalInput").ap()
    dw2 = nc.dram_tensor("w2", [D, D], F32, kind="ExternalInput").ap()
    db2 = nc.dram_tensor("b2", [D], F32, kind="ExternalInput").ap()
    dmask = nc.dram_tensor("mask_nz", [max(n_mask, 1) * 128, 128], F32,
                           kind="ExternalInput").ap()
    dqw_gu = nc.dram_tensor("qw_gu", [D, NG // 8], I32, kind="ExternalInput").ap()
    dqz_gu = nc.dram_tensor("qz_gu", [KT, NG // 8], I32, kind="ExternalInput").ap()
    ds_gu = nc.dram_tensor("s_gu", [KT, NG], F32, kind="ExternalInput").ap()
    dqw_dn = nc.dram_tensor("qw_dn", [INTER, D // 8], I32, kind="ExternalInput").ap()
    dqz_dn = nc.dram_tensor("qz_dn", [MT, D // 8], I32, kind="ExternalInput").ap()
    ds_dn = nc.dram_tensor("s_dn", [MT, D], F32, kind="ExternalInput").ap()
    didf = nc.dram_tensor("identf", [128, 128], F32, kind="ExternalInput").ap()
    dones = nc.dram_tensor("ones_col", [128, 1], BF16, kind="ExternalInput").ap()
    dones1b = nc.dram_tensor("ones_row_b", [1, 128], BF16, kind="ExternalInput").ap()
    dones1f = nc.dram_tensor("ones_row_f", [1, 128], F32, kind="ExternalInput").ap()
    dind_gu = nc.dram_tensor("ind_gu", [128, KT * KT], BF16, kind="ExternalInput").ap()
    dind_dn = nc.dram_tensor("ind_dn", [128, NJ * MT], BF16, kind="ExternalInput").ap()
    dstair2 = nc.dram_tensor("stair2", [2, 128], BF16, kind="ExternalInput").ap()
    dout = nc.dram_tensor("out", [T, D], F32, kind="ExternalOutput").ap()

    dxT = [nc.dram_tensor(f"xT{i}", [D, T], F32, kind="Internal").ap()
           for i in range(4)]
    dxn = nc.dram_tensor("xn_sc", [D, T], BF16, kind="Internal").ap()
    dguc = nc.dram_tensor("gu_cache", [2 * NJ * 128, KT * SW], BF16, kind="Internal").ap()
    ddnc = nc.dram_tensor("dn_cache", [MT * 8 * 128, 512], BF16, kind="Internal").ap()
    dh = nc.dram_tensor("h_sc", [MT * 4 * 128, 512], BF16, kind="Internal").ap()
    dzs_gu = nc.dram_tensor("zs_gu_sc", [KT, NG], BF16, kind="Internal").ap()
    dzs_dn = nc.dram_tensor("zs_dn_sc", [MT, D], BF16, kind="Internal").ap()
    dw2bf = nc.dram_tensor("w2bf", [D, D], BF16, kind="Internal").ap()
    dw1c = nc.dram_tensor("w1c", [32 * 128, KT * 128], BF16, kind="Internal").ap()
    dw2tc = nc.dram_tensor("w2tc", [8 * KT * 128, 512], BF16, kind="Internal").ap()

    pb = [nc.psum_tensor(f"pbank{i}", [128, 512], F32).__enter__() for i in range(8)]
    b = B(nc)

    al = Alloc(nc, SBUF_BASE)
    sb_idf = al.take("idf", [128, 128], F32)
    sb_ones = al.take("onesc", [128, 1], BF16)
    sb_ones1b = al.take("ones1b", [1, 128], BF16)
    sb_ones1f = al.take("ones1f", [1, 128], F32)
    sb_indgu = al.take("indgu", [128, KT * KT], BF16)
    sb_stair2 = al.take("stair2", [2, 128], BF16)
    sb_b2 = al.take("b2t", [128, 32], F32)
    PERSIST = al.off

    s_const = b.stream("sync")
    s_w2bf = b.stream("gpsimd")
    for out, in_ in ((sb_idf[:], didf), (sb_ones[:], dones),
                     (sb_ones1b[:], dones1b), (sb_ones1f[:], dones1f),
                     (sb_indgu[:], dind_gu), (sb_stair2[:], dstair2)):
        b.dma("sync", out, in_, s_const)
    b.dma("sync", sb_b2[:], db2.rearrange("(j p) -> p j", p=128), s_const,
          allow_slow_non_contiguous=True)
    for r in range(4):
        b.dma("gpsimd",
              dw2bf[1024 * r:1024 * (r + 1), :].rearrange("(a p) c -> p a c", p=128),
              dw2[1024 * r:1024 * (r + 1), :].rearrange("(a p) c -> p a c", p=128),
              s_w2bf)
    w2bf_done = s_w2bf.n
    w1cv = dw1c.rearrange("(st p) (dt c) -> st p dt c", p=128, c=128)
    for st in range(32):
        b.dma("gpsimd", w1cv[st, :, :, :],
              dw1[:, 128 * st:128 * (st + 1)].rearrange("(dt p) c -> p dt c", p=128),
              s_w2bf)
    w1c_done = s_w2bf.n
    for eng in ("tensor", "vector", "scalar"):
        b.wait(eng, s_const, s_const.n)

    def emit_tap(dsrc_bytes_ap, nrows, rowlen_f32):
        lo = Alloc(nc, PERSIST)
        tbuf = lo.take("tapbuf", [128, 2048], F32)
        s_tl = b.stream("sync")
        s_tw = b.stream("gpsimd")
        dflat = dout.rearrange("a c -> (a c)")
        for i in range(0, nrows, 128):
            n = min(128, nrows - i)
            wv = b.dma("sync", tbuf[:n, :rowlen_f32], dsrc_bytes_ap[i:i + n, :],
                       s_tl, pre=[(s_tw, s_tw.n)])
            vv = b.vop(lambda v: v.engine_nop(), pre=[(s_tl, wv)])
            seg = dflat[i * rowlen_f32:(i + n) * rowlen_f32]
            b.dma("gpsimd", seg.rearrange("(p f) -> p f", f=rowlen_f32),
                  tbuf[:n, :rowlen_f32], s_tw, pre=[(b.v_ms, vv)])

    def finish():
        b.barrier()
        with nc.Block() as blk:
            for eng in B.ENGS:
                getattr(blk, eng)(lambda e, eng=eng: [f(e) for f in b.ops[eng]])
        return nc

    lnal = Alloc(nc, SBUF_TOP - 66000)
    ln_xb = lnal.take("lnxb", [128, KT * 512], BF16)
    ln_xq = [lnal.take(f"lnxq{i}", [128, 8 * 512], BF16) for i in range(2)]
    ln_r1 = lnal.take("lnr1", [1, 512], F32)
    ln_r2 = lnal.take("lnr2", [1, 512], F32)
    ln_rt = lnal.take("lnrt", [1, 512], F32)
    ln_rb = lnal.take("lnrb", [128, 512], F32)
    ln_mb = lnal.take("lnmb", [128, 512], F32)
    ln_tmp = lnal.take("lntmp", [128, 512], F32)
    ln_xn = [lnal.take(f"lnxn{i}", [128, 512], BF16) for i in range(4)]
    s_lnl = b.stream("gpsimd")
    s_lw = [b.stream("gpsimd") for _ in range(4)]
    ln_pace = b.sem()
    ln_st = {"xbf": 0, "sq0": 0, "sq1": 0, "xn": [0] * 4}

    def ln_load(dsrc, tc, seg, pre):
        p = list(pre)
        if seg is None or seg == 0:
            p.append((b.v_ms, ln_st["xbf"]))
        if seg is None:
            b.dma("gpsimd",
                  ln_xb[:].rearrange("p (dt c) -> p dt c", c=512),
                  dsrc[:, 512 * tc:512 * (tc + 1)]
                  .rearrange("(dt p) c -> p dt c", p=128),
                  s_lnl, pre=p)
        else:
            b.dma("gpsimd",
                  ln_xb[:, 2048 * seg:2048 * (seg + 1)]
                  .rearrange("p (dt c) -> p dt c", c=512),
                  dsrc[512 * seg:512 * (seg + 1), 512 * tc:512 * (tc + 1)]
                  .rearrange("(dt p) c -> p dt c", p=128),
                  s_lnl, pre=p)

    def ln_compute(tc):
        ldv = s_lnl.n
        p0p = ln_pace.n
        sqv = []
        for c in range(4):
            if c == 0:
                pre = [(s_lnl, ldv)]
                if ln_st["sq0"]:
                    pre.append((ln_pace, ln_st["sq0"]))
            elif c == 1:
                pre = [(b.rdy[7], ln_st["sq1"])] if ln_st["sq1"] else []
            elif c == 2:
                pre = [(ln_pace, p0p + 1)]
            else:
                pre = [(ln_pace, p0p + 2)]
            sqv.append(b.aop(lambda a, c=c: a.activation(
                ln_xq[c % 2][:], ln_xb[:, 4096 * c:4096 * (c + 1)], AF.Square),
                pre=pre))
        rv6 = b.group(6, [
            (lambda t, s, e, dt=dt: t.matmul(
                pb[6][:1, :], sb_ones[:], ln_xb[:, 512 * dt:512 * (dt + 1)],
                start=s, stop=e)) for dt in range(KT)],
            pre=[(s_lnl, ldv)])
        wf7 = b.uses[7]
        b.uses[7] += 1
        b.rdy[7].n += 1
        rv7 = b.rdy[7].n

        def sq_run(t, sqv=tuple(sqv), wf7=wf7):
            if wf7 > 0:
                t.wait_ge(b.fre[7].h, wf7)
            for c in range(4):
                t.wait_ge(b.a_ms.h, sqv[c])
                for i in range(8):
                    dt = 8 * c + i
                    inst = t.matmul(pb[7][:1, :], sb_ones[:],
                                    ln_xq[c % 2][:, 512 * i:512 * (i + 1)],
                                    start=(dt == 0), stop=(dt == KT - 1))
                    if dt == KT - 1:
                        inst.then_inc(b.rdy[7].h, 1)
                    elif i == 7:
                        inst.then_inc(ln_pace.h, 1)
        b.emit("tensor", sq_run)
        ln_pace.n += 3
        ln_st["sq0"] = p0p + 3
        ln_st["sq1"] = rv7
        b.consume("vector", 6, rv6,
                  lambda v: v.tensor_scalar_mul(ln_r1[:], pb[6][:1, :], 1.0 / D))
        b.consume("vector", 7, rv7,
                  lambda v: v.tensor_scalar_mul(ln_r2[:], pb[7][:1, :], 1.0 / D))

        def vstat(v):
            v.tensor_tensor(ln_rt[:], ln_r1[:], ln_r1[:], ALU.mult)
            v.tensor_tensor(ln_r2[:], ln_r2[:], ln_rt[:], ALU.subtract)
            return v.tensor_scalar_add(ln_r2[:], ln_r2[:], EPS)
        vv = b.vop(vstat)
        av = b.aop(lambda a: a.activation(ln_r2[:], ln_r2[:], AF.Sqrt),
                   pre=[(b.v_ms, vv)])

        def vrec(v):
            v.reciprocal(ln_r2[:], ln_r2[:])
            return v.tensor_tensor(ln_r1[:], ln_r1[:], ln_r2[:], ALU.mult)
        vv = b.vop(vrec, pre=[(b.a_ms, av)])
        for bnk, srcr, dst in ((6, ln_r2, ln_rb), (7, ln_r1, ln_mb)):
            rvb = b.group(bnk, [
                lambda t, s, e, bnk=bnk, srcr=srcr: t.matmul(
                    pb[bnk][:], sb_ones1f[:], srcr[:], start=s, stop=e)],
                pre=[(b.v_ms, vv)])
            b.consume("vector", bnk, rvb,
                      lambda v, dst=dst, bnk=bnk: v.tensor_copy(dst[:], pb[bnk][:]))
        lastv = 0
        for dt in range(KT):
            sl = dt % 4

            def vnorm(v, dt=dt, sl=sl):
                v.tensor_tensor(ln_tmp[:], ln_xb[:, 512 * dt:512 * (dt + 1)],
                                ln_rb[:], ALU.mult)
                return v.tensor_tensor(ln_xn[sl][:], ln_tmp[:], ln_mb[:],
                                       ALU.subtract)
            pre = [(s_lw[sl], ln_st["xn"][sl])] if ln_st["xn"][sl] else []
            lastv = b.vop(vnorm, pre=pre)
            b.dma("gpsimd",
                  dxn[128 * dt:128 * (dt + 1), 512 * tc:512 * (tc + 1)],
                  ln_xn[sl][:], s_lw[sl], pre=[(b.v_ms, lastv)])
            ln_st["xn"][sl] = s_lw[sl].n
        ln_st["xbf"] = lastv

    p0 = Alloc(nc, PERSIST)
    sb_xh = [p0.take(f"p0xh{i}", [128, D], F32) for i in range(4)]
    sb_xts = [p0.take(f"p0xts{i}", [128, 512], F32) for i in range(2)]
    sb_qzg = p0.take("p0qzg", [KT, NG // 8], I32)
    sb_zu = p0.take("p0zu", [KT, 4096], U16)
    sb_zb = p0.take("p0zb", [KT, 4096], BF16)
    sb_w2ts = [p0.take(f"w2ts{i}", [128, 512], BF16) for i in range(4)]
    sb_qzd = p0.take("p0qzd", [MT, D // 8], I32)
    sb_zud = p0.take("p0zud", [MT, D], U16)
    sb_zbd = p0.take("p0zbd", [MT, D], BF16)

    s_px = b.stream("sync")
    s_pw = [b.stream("gpsimd") for _ in range(2)]
    xts_guard = [0, 0]
    xh_guard = [0]
    for tg in range(4):
        wv = [b.dma("sync", sb_xh[i][:],
                    dx[(4 * tg + i) * 128:(4 * tg + i + 1) * 128, :], s_px,
                    pre=([(b.t_ms, xh_guard[0])] if i == 0 else ()))
              for i in range(4)]
        for dt in range(KT):
            bnk = dt % 2
            wf = b.uses[bnk]
            b.uses[bnk] += 1
            b.rdy[bnk].n += 1
            rv = b.rdy[bnk].n

            def trun(t, dt=dt, bnk=bnk, wf=wf, w=wv[3]):
                t.wait_ge(s_px.h, w)
                if wf > 0:
                    t.wait_ge(b.fre[bnk].h, wf)
                for i in range(4):
                    inst = t.transpose(pb[bnk][:, 128 * i:128 * (i + 1)],
                                       sb_xh[i][:, 128 * dt:128 * (dt + 1)],
                                       sb_idf[:])
                    if i == 3:
                        inst.then_inc(b.rdy[bnk].h, 1)
            b.emit("tensor", trun)
            st = sb_xts[dt % 2]
            fv = b.consume("vector", bnk, rv,
                           lambda v, st=st, bnk=bnk: v.tensor_copy(st[:], pb[bnk][:]),
                           pre=[(s_pw[dt % 2], xts_guard[dt % 2])])
            b.dma("gpsimd", dxT[0][128 * dt:128 * (dt + 1), 512 * tg:512 * (tg + 1)],
                  st[:], s_pw[dt % 2], pre=[(b.fre[bnk], fv)])
            xts_guard[dt % 2] = s_pw[dt % 2].n
        xh_guard[0] = b.tmark()
        ln_load(dxT[0], tg, None, [(s_pw[0], s_pw[0].n), (s_pw[1], s_pw[1].n)])
        ln_compute(tg)

    s_w2l = [b.stream("scalar") for _ in range(4)]
    s_w2w = [b.stream("scalar") for _ in range(4)]
    w2w_guard = [0] * 4
    w2tc_v = dw2tc.rearrange("(tcb st p) c -> tcb st p c", st=KT, p=128)
    for st in range(KT):
        for dtb in range(8):
            sl = (st * 8 + dtb) % 4
            p_ = ([(s_w2bf, w2bf_done)] if st == 0 and dtb == 0
                  else [(s_w2w[sl], w2w_guard[sl])])
            wv2t = b.dma("scalar", sb_w2ts[sl][:],
                         dw2bf[512 * dtb:512 * (dtb + 1), 128 * st:128 * (st + 1)],
                         s_w2l[sl], transpose=True, pre=p_)
            b.dma("scalar", w2tc_v[dtb, st, :, :], sb_w2ts[sl][:],
                  s_w2w[sl], pre=[(s_w2l[sl], wv2t)])
            w2w_guard[sl] = s_w2w[sl].n
    w2tc_done = [(s, s.n) for s in s_w2w]

    s_zs = b.stream("sync")
    wq = b.dma("sync", sb_qzg[:], dqz_gu, s_zs)
    zb_guard = 0
    for ch in range(6):
        c0 = 4096 * ch
        w = min(4096, NG - c0)

        def zrun(v, c0=c0, w=w):
            qz = sb_qzg[:].bitcast(U16)
            for o in range(0, w, SW):
                u0 = ((c0 + o) // SW) * 64
                for i in range(4):
                    v.tensor_scalar(sb_zu[:, o + 64 * i:o + 64 * (i + 1)],
                                    qz[:, u0:u0 + 64], 4 * i, 15,
                                    ALU.logical_shift_right, ALU.bitwise_and)
            return v.tensor_copy(sb_zb[:, :w], sb_zu[:, :w])
        vv = b.vop(zrun, pre=[(s_zs, max(wq, zb_guard))])
        b.dma("sync", dzs_gu[:, c0:c0 + w], sb_zb[:, :w], s_zs, pre=[(b.v_ms, vv)])
        zb_guard = s_zs.n

    wq = b.dma("sync", sb_qzd[:], dqz_dn, s_zs)

    def zdrun(v):
        qz = sb_qzd[:].bitcast(U16)
        for o in range(0, D, SW):
            u0 = (o // SW) * 64
            for i in range(4):
                v.tensor_scalar(sb_zud[:, o + 64 * i:o + 64 * (i + 1)],
                                qz[:, u0:u0 + 64], 4 * i, 15,
                                ALU.logical_shift_right, ALU.bitwise_and)
        return v.tensor_copy(sb_zbd[:], sb_zud[:])
    vv = b.vop(zdrun, pre=[(s_zs, wq)])
    b.dma("sync", dzs_dn[:], sb_zbd[:], s_zs, pre=[(b.v_ms, vv)])

    b.barrier()

    if upto == "p0":
        emit_tap(dxT[0], D, T)
        return finish()
    if upto == "ln1":
        emit_tap(dxn.bitcast(F32), D, T // 2)
        return finish()

    p1 = Alloc(nc, PERSIST)
    sb_qw = [p1.take(f"dqqw{i}", [128, NG // 8], I32) for i in range(2)]
    sb_srow = [p1.take(f"dqsr{i}", [2, 8192], BF16) for i in range(2)]
    sb_sbc = [p1.take(f"dqsbc{i}", [128, 1024], BF16) for i in range(8)]
    sb_nib = [p1.take(f"dqnib{i}", [128, 512], U16) for i in range(2)]
    sb_nbb = p1.take("dqnbb", [128, 512], BF16)
    sb_wst = [p1.take(f"dqwst{i}", [128, 2048], BF16) for i in range(2)]

    s_qw = [b.stream("sync") for _ in range(2)]
    s_sr = [b.stream("gpsimd") for _ in range(2)]
    s_wc = [b.stream("sync") for _ in range(2)]
    qw_guard = [0, 0]
    srow_guard = [(0, 0), (0, 0)]
    sbc_guard = [0] * 8
    nib_guard = [0, 0]
    wst_guard = [0, 0]
    subctr = [0]

    def emit_dequant(dqw, dcache, n_i32, n_pos, dsrows, dzrows, n_tiles, dn_mode):
        qsel = (dqw.rearrange("(j r s) c -> j s r c", r=64, s=4) if dn_mode else None)
        for g in range(n_tiles):
            qs = sb_qw[g % 2]
            if dn_mode:
                j, q = g // 2, g % 2
                wqv = b.dma("sync", qs[:, :n_i32],
                            qsel[j, 2 * q:2 * q + 2, :, :],
                            s_qw[g % 2], pre=[(b.v_ms, qw_guard[g % 2])])
            else:
                wqv = b.dma("sync", qs[:, :n_i32], dqw[128 * g:128 * (g + 1), :],
                            s_qw[g % 2], pre=[(b.v_ms, qw_guard[g % 2])])
            for ch in range(0, n_pos, 2048):
                w = min(2048, n_pos - ch)
                chi = (ch // 2048) % 2
                sr = sb_srow[chi]
                gb, gv = srow_guard[chi]
                nrow = 2 if dn_mode else 1
                g0 = (g // 2) * 2 if dn_mode else g
                srv = sr[:nrow, :].rearrange("a (u two c) -> a u two c", two=2, c=512)
                wsv = b.dma("gpsimd", srv[:, :w // 512, 0, :],
                            dsrows[g0:g0 + nrow, ch:ch + w]
                            .rearrange("a (u c) -> a u c", c=512),
                            s_sr[chi], pre=[(b.rdy[gb], gv)])
                wzv = b.dma("gpsimd", srv[:, :w // 512, 1, :],
                            dzrows[g0:g0 + nrow, ch:ch + w]
                            .rearrange("a (u c) -> a u c", c=512),
                            s_sr[chi])
                wstg = sb_wst[chi]
                last_rv = (6, 0)
                for sub in range(0, w, 512):
                    si = subctr[0] % 8
                    bnk = (2 * subctr[0]) % 8
                    bnkz = bnk + 1
                    subctr[0] += 1
                    sbc = sb_sbc[si]
                    lhsb = sb_stair2[:] if dn_mode else sb_ones1b[:]
                    rv = b.group(bnk, [
                        (lambda t, s, e, sr=sr, sub=sub, bnk=bnk, lhsb=lhsb,
                         nrow=nrow:
                         t.matmul(pb[bnk][:], lhsb,
                                  sr[:nrow, 2 * sub:2 * sub + 512],
                                  start=s, stop=e))],
                        pre=[(s_sr[chi], wzv)])
                    rvz = b.group(bnkz, [
                        (lambda t, s, e, sr=sr, sub=sub, bnkz=bnkz, lhsb=lhsb,
                         nrow=nrow:
                         t.matmul(pb[bnkz][:], lhsb,
                                  sr[:nrow, 2 * sub + 512:2 * sub + 1024],
                                  start=s, stop=e))])
                    last_rv = (bnkz, rvz)
                    fv = b.consume("scalar", bnk, rv,
                                   lambda a, sbc=sbc, bnk=bnk:
                                   a.activation(sbc[:, :512], pb[bnk][:],
                                                AF.Identity),
                                   pre=[(b.v_ms, sbc_guard[si])])
                    fvz = b.consume("scalar", bnkz, rvz,
                                    lambda a, sbc=sbc, bnkz=bnkz:
                                    a.activation(sbc[:, 512:], pb[bnkz][:],
                                                 AF.Identity))

                    nslot = subctr[0] % 2
                    nib = sb_nib[nslot]

                    def drun(v, qs=qs, sbc=sbc, pos=ch + sub, sub=sub,
                             nib=nib, n_i32=n_i32, wstg=wstg):
                        qu = qs[:, :n_i32].bitcast(U16)
                        for jj in range(2):
                            u0 = ((pos + SW * jj) // SW) * 64
                            for i in range(4):
                                v.tensor_scalar(
                                    nib[:, SW * jj + 64 * i:SW * jj + 64 * (i + 1)],
                                    qu[:, u0:u0 + 64], 4 * i, 15,
                                    ALU.logical_shift_right, ALU.bitwise_and)
                        zbv = sbc[:].rearrange("p (u two c) -> p u two c",
                                               two=2, c=512)
                        v.tensor_tensor(sb_nbb[:], nib[:], zbv[:, 0, 1, :],
                                        ALU.subtract)
                        return v.tensor_tensor(wstg[:, sub:sub + 512],
                                               sb_nbb[:], zbv[:, 0, 0, :], ALU.mult)
                    vv = b.vop(drun, pre=[(s_qw[g % 2], wqv),
                                          (s_wc[chi], wst_guard[chi]),
                                          (b.fre[bnk], fv), (b.fre[bnkz], fvz)])
                    sbc_guard[si] = vv
                srow_guard[chi] = last_rv
                vvw = b.v_ms.n
                if not dn_mode:
                    g4 = dcache.rearrange("(s p) (kt c) -> s p kt c", p=128, kt=KT)
                    for o in range(0, w, SW):
                        s_idx = (ch + o) // SW
                        b.dma("sync", g4[s_idx, :, g, :],
                              wstg[:, o:o + SW], s_wc[chi],
                              pre=([(b.v_ms, vvw)] if o == 0 else ()))
                else:
                    d4 = dcache.rearrange("(mt oc p) c -> mt oc p c", oc=8, p=128)
                    for o in range(0, w, 512):
                        b.dma("sync", d4[g, (ch + o) // 512, :, :],
                              wstg[:, o:o + 512], s_wc[chi],
                              pre=([(b.v_ms, vvw)] if o == 0 else ()))
                wst_guard[chi] = s_wc[chi].n
            qw_guard[g % 2] = b.v_ms.n

    emit_dequant(dqw_gu, dguc, NG // 8, NG, ds_gu, dzs_gu, KT, False)
    emit_dequant(dqw_dn, ddnc, D // 8, D, ds_dn, dzs_dn, MT, True)

    b.barrier()

    s_gx = b.stream("sync")
    s_rg = [b.stream("sync") for _ in range(4)]
    s_hw = [b.stream("gpsimd") for _ in range(4)]
    s_hh = b.stream("sync")
    s_wd = [b.stream("scalar") for _ in range(8)]
    s_xr = b.stream("gpsimd")
    s_os = [b.stream("gpsimd") for _ in range(8)]

    def emit_ffn(dsrc, ddst, ln_dst=False):
        lo = Alloc(nc, PERSIST)
        xnh = lo.take("gxn", [128, KT * 1024], BF16)
        ring = [lo.take(f"grg{i}", [128, KT * SW], BF16) for i in range(4)]
        silub = [lo.take(f"gsl{i}", [128, 2048], BF16) for i in range(2)]
        hstg = [lo.take(f"ghs{i}", [128, 1024], BF16) for i in range(4)]

        ring_guard = [(0, 0)] * 4
        silu_guard = [(0, 0), (0, 0)]
        silu_fv = {}
        hmul_fv = {}
        hst_guard = [0, 0, 0, 0]
        rot = [0]

        for half in range(2):
            pre0 = ([(b.rdy[k], b.rdy[k].n) for k in range(8)] if half == 1 else [])
            xwv = [b.dma("sync", xnh[:, 1024 * k:1024 * (k + 1)],
                         dxn[128 * k:128 * (k + 1), HALF * half:HALF * (half + 1)],
                         s_gx, pre=(pre0 if k == 0 else ()))
                   for k in range(KT)]

            for jp in range(2 * NJ):
                j, side = jp // 2, jp % 2
                slot = ring[jp % 4]
                pos0 = side * INTER + SW * j
                gb, gv = ring_guard[jp % 4]
                s_idx = pos0 // SW
                wv = b.dma("sync", slot[:],
                           dguc[128 * s_idx:128 * (s_idx + 1), :],
                           s_rg[jp % 4], pre=[(b.rdy[gb], gv)])
                for q in range(2):
                    for tc in range(2):
                        bnk = rot[0]
                        rot[0] = (rot[0] + 1) % 6
                        mms = [(lambda t, s, e, k=k, q=q, tc=tc, slot=slot, bnk=bnk:
                                t.matmul(pb[bnk][:],
                                         slot[:, SW * k + 128 * q:SW * k + 128 * (q + 1)],
                                         xnh[:, 1024 * k + 512 * tc:1024 * k + 512 * (tc + 1)],
                                         start=s, stop=e))
                               for k in range(KT)]
                        rv = b.group(bnk, mms,
                                     pre=([(s_rg[jp % 4], wv), (s_gx, xwv[-1])]
                                          if (q == 0 and tc == 0) else ()))
                        ring_guard[jp % 4] = (bnk, rv)
                        if side == 0:
                            sl = silub[j % 2]
                            gb2, gv2 = silu_guard[j % 2]
                            sfv = b.consume("scalar", bnk, rv,
                                            lambda a, sl=sl, q=q, tc=tc, bnk=bnk:
                                            a.activation(
                                                sl[:, 1024 * q + 512 * tc:1024 * q + 512 * (tc + 1)],
                                                pb[bnk][:], AF.Silu),
                                            pre=([(b.fre[gb2], gv2)]
                                                 if (q == 0 and tc == 0) else ()))
                            silu_fv[(q, tc)] = (bnk, sfv)
                        else:
                            hs = hstg[2 * (j % 2) + q]
                            sl = silub[j % 2]

                            def hmul(v, hs=hs, sl=sl, q=q, tc=tc, bnk=bnk):
                                return v.tensor_tensor(
                                    hs[:, 512 * tc:512 * (tc + 1)], pb[bnk][:],
                                    sl[:, 1024 * q + 512 * tc:1024 * q + 512 * (tc + 1)],
                                    ALU.mult)
                            sb_, sv_ = silu_fv[(q, tc)]
                            fv = b.consume("vector", bnk, rv, hmul,
                                           pre=([(b.fre[sb_], sv_),
                                                 (s_hw[2 * (j % 2) + q],
                                                  hst_guard[2 * (j % 2) + q])]
                                                if tc == 0 else [(b.fre[sb_], sv_)]))
                            hmul_fv[(j % 2, q)] = (bnk, fv)
                            if tc == 1:
                                silu_guard[j % 2] = (bnk, fv)
                if side == 1:
                    dh4 = dh.rearrange("(mt tq p) c -> mt tq p c", tq=4, p=128)
                    for q in range(2):
                        hb, hv = hmul_fv[(j % 2, q)]
                        for tc in range(2):
                            b.dma("gpsimd",
                                  dh4[2 * j + q, 2 * half + tc, :, :],
                                  hstg[2 * (j % 2) + q][:, 512 * tc:512 * (tc + 1)],
                                  s_hw[2 * (j % 2) + q],
                                  pre=([(b.fre[hb], hv)] if tc == 0 else ()))
                        hst_guard[2 * (j % 2) + q] = s_hw[2 * (j % 2) + q].n

        b.barrier()
        if K_UPTO_GATE[0]:
            return

        lo = Alloc(nc, PERSIST)
        hh = lo.take("dnh", [128, MT * 512], BF16)
        wdn = [lo.take(f"dnw{i}", [128, 512], BF16) for i in range(8)]
        xres = [lo.take(f"dnxr{i}", [128, 512], F32) for i in range(8)]
        ost = [lo.take(f"dnos{i}", [128, 512], F32) for i in range(8)]

        wdn_guard = [0] * 8
        xres_guard = [0] * 8
        ost_guard = [0] * 8
        dsel = ddnc.rearrange("(mt oc p) c -> mt oc p c", oc=8, p=128)
        dh4 = dh.rearrange("(mt tq p) c -> mt tq p c", tq=4, p=128)
        xsel = dsrc.rearrange("(c r s) t -> c s r t", r=64, s=4)
        osel = ddst.rearrange("(c r s) t -> c s r t", r=64, s=4)

        for tq in range(4):
            pre0 = [(b.rdy[k], b.rdy[k].n) for k in range(8)]
            hwv = [b.dma("sync", hh[:, 512 * mt:512 * (mt + 1)],
                         dh4[mt, tq, :, :],
                         s_hh, pre=(pre0 if mt == 0 else ()))
                   for mt in range(MT)]

            for og in range(8):
                base = 4 * (og % 2)
                banks = [base + i for i in range(4)]
                for ot in range(4):
                    bnk = banks[ot]
                    jj, q = 2 * og + ot // 2, ot % 2
                    b.dma("gpsimd", xres[bnk][:],
                          xsel[jj, 2 * q:2 * q + 2, :, 512 * tq:512 * (tq + 1)],
                          s_xr, pre=[(b.fre[bnk], xres_guard[bnk])])
                xr_done = s_xr.n
                open_wf = []
                rvs = {}
                for bnk in banks:
                    open_wf.append((bnk, b.uses[bnk]))
                    b.uses[bnk] += 1
                    b.rdy[bnk].n += 1
                    rvs[bnk] = b.rdy[bnk].n
                for mt in range(MT):
                    ws = wdn[mt % 8]
                    wv = b.dma("scalar", ws[:], dsel[mt, og, :, :],
                               s_wd[mt % 8], pre=[(b.t_ms, wdn_guard[mt % 8])])

                    def dmm(t, mt=mt, ws=ws, wv=wv, banks=tuple(banks),
                            open_wf=tuple(open_wf), hv=hwv[-1]):
                        t.wait_ge(s_wd[mt % 8].h, wv)
                        if mt == 0:
                            t.wait_ge(s_hh.h, hv)
                            for bnk, wf in open_wf:
                                if wf > 0:
                                    t.wait_ge(b.fre[bnk].h, wf)
                        for ot in range(4):
                            inst = t.matmul(
                                pb[banks[ot]][:], ws[:, 128 * ot:128 * (ot + 1)],
                                hh[:, 512 * mt:512 * (mt + 1)],
                                start=(mt == 0), stop=(mt == MT - 1))
                            if mt == MT - 1:
                                inst.then_inc(b.rdy[banks[ot]].h, 1)
                        if mt != MT - 1:
                            inst.then_inc(b.t_ms.h, 1)
                        else:
                            t.nop().then_inc(b.t_ms.h, 1)
                    b.t_ms.n += 1
                    b.emit("tensor", dmm)
                    wdn_guard[mt % 8] = b.t_ms.n
                for ot in range(4):
                    bnk = banks[ot]
                    jj, q = 2 * og + ot // 2, ot % 2
                    if "noxres" in dbg:
                        fv = b.consume("vector", bnk, rvs[bnk],
                                       lambda v, bnk=bnk: v.tensor_scalar_mul(
                                           ost[bnk][:], pb[bnk][:], 0.5),
                                       pre=[(s_xr, xr_done),
                                            (s_os[bnk], ost_guard[bnk])])
                    else:
                        fv = b.consume("vector", bnk, rvs[bnk],
                                       lambda v, bnk=bnk: v.scalar_tensor_tensor(
                                           ost[bnk][:], pb[bnk][:], 0.5, xres[bnk][:],
                                           ALU.mult, ALU.add),
                                       pre=[(s_xr, xr_done),
                                            (s_os[bnk], ost_guard[bnk])])
                    xres_guard[bnk] = fv
                    b.dma("gpsimd",
                          osel[jj, 2 * q:2 * q + 2, :, 512 * tq:512 * (tq + 1)],
                          ost[bnk][:], s_os[bnk], pre=[(b.fre[bnk], fv)])
                    ost_guard[bnk] = s_os[bnk].n
                if ln_dst:
                    ln_load(ddst, tq, og,
                            [(s_os[bk], ost_guard[bk]) for bk in banks])
            if ln_dst:
                ln_compute(tq)
        b.barrier()

    def emit_attn(dsrc, ddst):
        lo = Alloc(nc, PERSIST)
        abuf = lo.take("abuf", [128, max(n_mask, 1) * 128], BF16)
        A_END = lo.off
        w1s = [lo.take(f"aw1{i}", [128, KT * 128], BF16) for i in range(4)]
        w2t = [lo.take(f"aw2{i}", [128, KT * 512], BF16) for i in range(2)]
        amask = [lo.take(f"amk{i}", [128, 128], F32) for i in range(2)]

        nz_of_chunk = []
        for tcb in range(8):
            ss = {}
            for tt in range(4 * tcb, 4 * tcb + 4):
                for (s_t, slot) in nz_of_t[tt]:
                    ss.setdefault(s_t, []).append((tt, slot))
            nz_of_chunk.append(sorted(ss.items()))

        s_w2 = [b.stream("sync") for _ in range(2)]
        s_w1 = [b.stream("sync") for _ in range(4)]
        s_mk = [b.stream("sync") for _ in range(2)]
        w1_guard = [0] * 4
        w2_guard = [0, 0]
        mk_guard = [("v", 0, 0), ("v", 0, 0)]
        arot = [0]
        mki = [0]
        for tcb in range(8):
            w2s = w2t[tcb % 2]
            wv2 = b.dma("sync", w2s[:].rearrange("p (dt c) -> p dt c", c=512),
                        dw2tc[4096 * tcb:4096 * (tcb + 1), :]
                        .rearrange("(dt p) c -> p dt c", p=128),
                        s_w2[tcb % 2],
                        pre=([(b.t_ms, w2_guard[tcb % 2])] +
                             (w2tc_done if tcb == 0 else [])))
            si = 0
            for (s_t, uses) in nz_of_chunk[tcb]:
                w1b = w1s[si % 4]
                wv1 = b.dma("sync", w1b[:],
                            dw1c[128 * s_t:128 * (s_t + 1), :],
                            s_w1[si % 4],
                            pre=[(b.t_ms, w1_guard[si % 4]),
                                 (s_w2bf, w1c_done)])
                bnk = arot[0]
                arot[0] = (arot[0] + 1) % 6
                mms = [(lambda t, s, e, k=k, w1b=w1b, w2s=w2s, bnk=bnk:
                        t.matmul(pb[bnk][:], w1b[:, 128 * k:128 * (k + 1)],
                                 w2s[:, 512 * k:512 * (k + 1)],
                                 start=s, stop=e)) for k in range(KT)]
                rv = b.group(bnk, mms, pre=[(s_w2[tcb % 2], wv2),
                                            (s_w1[si % 4], wv1)])
                tm = b.tmark()
                w1_guard[si % 4] = tm
                w2_guard[tcb % 2] = tm
                nuses = len(uses)
                for ui, (tt, slot) in enumerate(uses):
                    mi = mki[0] % 2
                    mk = amask[mi]
                    kind, gb3, gv3 = mk_guard[mi]
                    wmk = b.dma("sync", mk[:], dmask[128 * slot:128 * (slot + 1), :],
                                s_mk[mi],
                                pre=[(b.v_ms, gv3) if kind == "v"
                                     else (b.fre[gb3], gv3)])
                    mki[0] += 1
                    qq = tt - 4 * tcb

                    def amul(v, mk=mk, qq=qq, slot=slot, bnk=bnk):
                        return v.tensor_tensor(
                            abuf[:, 128 * slot:128 * (slot + 1)],
                            pb[bnk][:, 128 * qq:128 * (qq + 1)], mk[:], ALU.mult)
                    if ui == nuses - 1:
                        fv = b.consume("vector", bnk, rv, amul,
                                       pre=[(s_mk[mi], wmk)])
                        mk_guard[mi] = ("f", bnk, fv)
                    else:
                        vv = b.vop(amul, pre=[(s_mk[mi], wmk),
                                              (b.rdy[bnk], rv)])
                        mk_guard[mi] = ("v", 0, vv)
                si += 1
        b.barrier()

        lo = Alloc(nc, A_END)
        xn2 = [lo.take(f"yxn{i}", [128, KT * 512], BF16) for i in range(2)]
        gsb = [lo.take(f"ygs{i}", [128, 512], F32) for i in range(4)]
        x1t = [lo.take(f"yx1{i}", [128, 512], F32) for i in range(4)]
        x2t = [lo.take(f"yx2{i}", [128, 512], F32) for i in range(4)]

        s_x2l = b.stream("sync")
        s_x1l = [b.stream("gpsimd") for _ in range(4)]
        s_x2w = [b.stream("gpsimd") for _ in range(4)]
        yrot = [0]
        gs_guard = [0] * 4
        x1_guard = [0] * 4
        x2_guard = [0] * 4
        xn2_guard = [0, 0]
        for tk in range(4):
            sl2 = tk % 2
            xw = b.dma("sync", xn2[sl2][:].rearrange("p (k c) -> p k c", c=512),
                       dxn[:, 512 * tk:512 * (tk + 1)]
                       .rearrange("(k p) c -> p k c", p=128),
                       s_x2l, pre=[(b.t_ms, xn2_guard[sl2])])
            for J in range(32):
                pairs = nz_of_t[J]
                assert pairs, "empty attn row-tile unsupported"
                bnk = yrot[0]
                yrot[0] = (yrot[0] + 1) % 6
                mms = [(lambda t, s, e, st=st, slot=slot, bnk=bnk, sl2=sl2:
                        t.matmul(pb[bnk][:],
                                 abuf[:, 128 * slot:128 * (slot + 1)],
                                 xn2[sl2][:, 512 * st:512 * (st + 1)],
                                 start=s, stop=e))
                       for (st, slot) in pairs]
                rv = b.group(bnk, mms, pre=([(s_x2l, xw)] if J == 0 else []))
                gi = J % 4
                gs = gsb[gi]
                fvg = b.consume("scalar", bnk, rv,
                                lambda a, gs=gs, J=J, bnk=bnk: a.activation(
                                    gs[:], pb[bnk][:], AF.Gelu,
                                    bias=sb_b2[:, J:J + 1]),
                                pre=[(b.v_ms, gs_guard[gi])])
                wx1 = b.dma("gpsimd", x1t[gi][:],
                            dsrc[128 * J:128 * (J + 1), 512 * tk:512 * (tk + 1)],
                            s_x1l[gi], pre=[(b.v_ms, x1_guard[gi])])

                def yadd(v, gs=gs, gi=gi):
                    return v.tensor_tensor(x2t[gi][:], x1t[gi][:], gs[:], ALU.add)
                vv = b.vop(yadd, pre=[(b.fre[bnk], fvg), (s_x1l[gi], wx1),
                                      (s_x2w[gi], x2_guard[gi])])
                gs_guard[gi] = vv
                x1_guard[gi] = vv
                b.dma("gpsimd",
                      ddst[128 * J:128 * (J + 1), 512 * tk:512 * (tk + 1)],
                      x2t[gi][:], s_x2w[gi], pre=[(b.v_ms, vv)])
                x2_guard[gi] = s_x2w[gi].n
                if J % 4 == 3:
                    ln_load(ddst, tk, J // 4,
                            [(s_x2w[g2], x2_guard[g2]) for g2 in range(4)])
            xn2_guard[sl2] = b.tmark()
            ln_compute(tk)
        b.barrier()

    def emit_final():
        lo = Alloc(nc, PERSIST)
        fst = [lo.take(f"ffs{i}", [128, KT * 128], F32) for i in range(2)]
        ostg = [lo.take(f"fos{i}", [128, D], F32) for i in range(2)]
        s_fl = [b.stream("sync") for _ in range(2)]
        s_fo = [b.stream("gpsimd") for _ in range(2)]
        f_guard = [0, 0]
        o_guard = [0, 0]
        for tt in range(16):
            fs = fst[tt % 2]
            wv = b.dma("sync", fs[:].rearrange("p (dt c) -> p dt c", dt=KT),
                       dxT[3][:, 128 * tt:128 * (tt + 1)]
                       .rearrange("(dt p) c -> p dt c", p=128),
                       s_fl[tt % 2], pre=[(b.t_ms, f_guard[tt % 2])])
            og = ostg[tt % 2]
            fvs = {}
            for bg in range(8):
                bnk = bg % 2 + 2
                wf = b.uses[bnk]
                b.uses[bnk] += 1
                b.rdy[bnk].n += 1
                rv = b.rdy[bnk].n

                def trun(t, bg=bg, bnk=bnk, wf=wf, wv=wv, fs=fs, tt=tt):
                    t.wait_ge(s_fl[tt % 2].h, wv)
                    if wf > 0:
                        t.wait_ge(b.fre[bnk].h, wf)
                    for i in range(4):
                        inst = t.transpose(
                            pb[bnk][:, 128 * i:128 * (i + 1)],
                            fs[:, 128 * (4 * bg + i):128 * (4 * bg + i + 1)],
                            sb_idf[:])
                        if i == 3:
                            inst.then_inc(b.rdy[bnk].h, 1)
                b.emit("tensor", trun)
                fvs[bnk] = b.consume(
                    "vector", bnk, rv,
                    lambda v, og=og, bg=bg, bnk=bnk:
                    v.tensor_copy(og[:, 512 * bg:512 * (bg + 1)], pb[bnk][:]),
                    pre=([(s_fo[tt % 2], o_guard[tt % 2])] if bg == 0 else ()))
            f_guard[tt % 2] = b.tmark()
            b.dma("gpsimd", dout[128 * tt:128 * (tt + 1), :], og[:],
                  s_fo[tt % 2], pre=[(b.fre[2], fvs[2]), (b.fre[3], fvs[3])])
            o_guard[tt % 2] = s_fo[tt % 2].n

    if upto == "p1":
        emit_tap(ddnc.bitcast(F32), 2048, 2048)
        return finish()
    if upto == "gate":
        K_UPTO_GATE[0] = True
    emit_ffn(dxT[0], dxT[1], ln_dst=True)
    if upto == "gate":
        emit_tap(dh.bitcast(F32), 4096, 1024)
        return finish()
    if upto == "ffn1":
        emit_tap(dxT[1], D, T)
        return finish()
    emit_attn(dxT[1], dxT[2])
    if upto == "attn":
        emit_tap(dxT[2], D, T)
        return finish()
    emit_ffn(dxT[2], dxT[3])
    emit_final()
    b.barrier()

    with nc.Block() as blk:
        for eng in B.ENGS:
            getattr(blk, eng)(lambda e, eng=eng: [f(e) for f in b.ops[eng]])
    return nc



def _host_prep(inputs):
    mask = np.asarray(inputs["L0_attn_sparse_mask"], dtype=np.float32)
    tiles = mask.reshape(32, 128, 32, 128)
    nzmap = (tiles != 0).any(axis=(1, 3))
    nz_of_t = [[] for _ in range(32)]
    packed = []
    slot = 0
    for tj in range(32):
        for si in range(32):
            if nzmap[si, tj]:
                packed.append(np.ascontiguousarray(tiles[si, :, tj, :]))
                nz_of_t[tj].append((si, slot))
                slot += 1
    mask_nz = (np.concatenate(packed, axis=0) if packed
               else np.zeros((128, 128), np.float32))
    n_mask = max(slot, 1)

    perm_gu = np.concatenate([_pos2dense(INTER), INTER + _pos2dense(INTER)])
    perm_dn = _pos2dense(D)
    s_gu = np.ascontiguousarray(
        np.asarray(inputs["L0_ffn_scales_gu"], np.float32)[:, perm_gu])
    s_dn = np.ascontiguousarray(
        np.asarray(inputs["L0_ffn_scales_dn"], np.float32)[:, perm_dn])

    ind_gu = np.zeros((128, KT * KT), dtype=bfdt)
    for g in range(KT):
        ind_gu[:, KT * g + g] = 1.0
    ind_dn = np.zeros((128, NJ * MT), dtype=bfdt)
    p = np.arange(128)
    for j in range(NJ):
        ind_dn[p, MT * j + 2 * j + (p % 64) // 32] = 1.0

    common = dict(
        w1=np.ascontiguousarray(np.asarray(inputs["L0_attn_w1"], np.float32)),
        w2=np.ascontiguousarray(np.asarray(inputs["L0_attn_w2"], np.float32)),
        b2=np.ascontiguousarray(np.asarray(inputs["L0_attn_b2"], np.float32)),
        mask_nz=np.ascontiguousarray(mask_nz),
        qw_gu=np.ascontiguousarray(np.asarray(inputs["L0_ffn_qweight_gu"], np.int32)),
        qz_gu=np.ascontiguousarray(np.asarray(inputs["L0_ffn_qzeros_gu"], np.int32)),
        s_gu=s_gu,
        qw_dn=np.ascontiguousarray(np.asarray(inputs["L0_ffn_qweight_dn"], np.int32)),
        qz_dn=np.ascontiguousarray(np.asarray(inputs["L0_ffn_qzeros_dn"], np.int32)),
        s_dn=s_dn,
        identf=np.eye(128, dtype=np.float32),
        stair2=np.stack([((np.arange(128) % 64) // 32 == k).astype(bfdt)
                         for k in range(2)]),
        ones_col=np.ones((128, 1), dtype=bfdt),
        ones_row_b=np.ones((1, 128), dtype=bfdt),
        ones_row_f=np.ones((1, 128), dtype=np.float32),
        ind_gu=ind_gu, ind_dn=ind_dn,
    )
    return nz_of_t, n_mask, common


def kernel(**inputs):
    nz_of_t, n_mask, common = _host_prep(inputs)
    x = np.asarray(inputs["L0_attn_x"], np.float32).reshape(SEQ, D)
    nc = build_program(nz_of_t, n_mask)
    in_maps = []
    for c in range(NCORES):
        m = dict(common)
        m["x"] = np.ascontiguousarray(x[T * c:T * (c + 1), :])
        in_maps.append(m)
    res = run_bass_kernel_spmd(nc, in_maps, core_ids=list(range(NCORES)))
    out = np.concatenate([res.results[c]["out"] for c in range(NCORES)], axis=0)
    return out.reshape(1, SEQ, D).astype(np.float32)


# version 15
# speedup vs baseline: 1.0309x; 1.0108x over previous
import numpy as np
import ml_dtypes
import concourse.bass as bass
import concourse.mybir as mybir
from concourse.bass_utils import run_bass_kernel_spmd

F32 = mybir.dt.float32
BF16 = mybir.dt.bfloat16
I32 = mybir.dt.int32
U16 = mybir.dt.uint16
AF = mybir.ActivationFunctionType
ALU = mybir.AluOpType
bfdt = ml_dtypes.bfloat16

NCORES = 8
D = 4096
SEQ = 16384
T = SEQ // NCORES
INTER = 11008
NG = 2 * INTER
NJ = INTER // 256
SW = 256
MT = 2 * NJ
KT = D // 128
HALF = T // 2
EPS = 1e-5
SBUF_BASE = 16512
SBUF_TOP = 229300


def _pos2dense(n):
    p = np.arange(n)
    j, rem = p // SW, p % SW
    i, r = rem // 64, rem % 64
    return j * SW + 4 * r + i


class Alloc:
    def __init__(self, nc, base, top=SBUF_TOP):
        self.nc, self.off, self.top = nc, base, top

    def take(self, name, shape, dtype, align=64):
        esz = {F32: 4, BF16: 2, I32: 4, U16: 2}[dtype]
        self.off = -(-self.off // align) * align
        h = self.nc.alloc_sbuf_tensor_at(name, list(shape), dtype, offset=self.off)
        self.off += shape[-1] * esz
        assert self.off <= self.top, f"SBUF overflow at {name}: {self.off}"
        return h


class Sem:
    __slots__ = ("h", "n")

    def __init__(self, h):
        self.h, self.n = h, 0

    def inc(self, by=1):
        self.n += by
        return self.n


def _pre(e, pre):
    for s, v in pre:
        if v > 0:
            e.wait_ge(s.h, v)


class B:
    ENGS = ("tensor", "vector", "scalar", "sync", "gpsimd")

    def __init__(self, nc):
        self.nc = nc
        self.ops = {e: [] for e in self.ENGS}
        self._si = 0
        self.rdy = [self.sem() for _ in range(8)]
        self.fre = [self.sem() for _ in range(8)]
        self.uses = [0] * 8
        self.v_ms = self.sem()
        self.a_ms = self.sem()
        self.g_ms = self.sem()
        self.t_ms = self.sem()
        self.bar = self.sem()
        self.qstreams = {"sync": [], "scalar": [], "gpsimd": []}

    def stream(self, queue):
        s = self.sem()
        self.qstreams[queue].append(s)
        return s

    def sem(self):
        h = self.nc.semaphore(f"sm{self._si}").__enter__()
        self._si += 1
        return Sem(h)

    def emit(self, eng, fn):
        self.ops[eng].append(fn)

    def wait(self, eng, sem, val):
        if val > 0:
            self.ops[eng].append(lambda e, s=sem, v=val: e.wait_ge(s.h, v))

    def group(self, bnk, mms, pre=()):
        wf = self.uses[bnk]
        self.uses[bnk] += 1
        self.rdy[bnk].n += 1
        rv = self.rdy[bnk].n

        def run(t, mms=mms, bnk=bnk, wf=wf, pre=tuple(pre)):
            _pre(t, pre)
            if wf > 0:
                t.wait_ge(self.fre[bnk].h, wf)
            n = len(mms)
            for i, m in enumerate(mms):
                inst = m(t, i == 0, i == n - 1)
                if i == n - 1:
                    inst.then_inc(self.rdy[bnk].h, 1)
        self.emit("tensor", run)
        return rv

    def consume(self, eng, bnk, rv, fn, pre=()):
        self.fre[bnk].n += 1

        def run(e, bnk=bnk, rv=rv, fn=fn, pre=tuple(pre)):
            _pre(e, pre)
            e.wait_ge(self.rdy[bnk].h, rv)
            fn(e).then_inc(self.fre[bnk].h, 1)
        self.emit(eng, run)
        return self.fre[bnk].n

    def dma(self, queue, out, in_, sem, pre=(), **kw):
        v = sem.inc(16)

        def run(e, out=out, in_=in_, kw=kw, sem=sem, pre=tuple(pre)):
            _pre(e, pre)
            e.dma_start(out=out, in_=in_, **kw).then_inc(sem.h, 16)
        self.emit(queue, run)
        return v

    def vop(self, fn, pre=()):
        self.v_ms.n += 1

        def run(v, fn=fn, pre=tuple(pre)):
            _pre(v, pre)
            fn(v).then_inc(self.v_ms.h, 1)
        self.emit("vector", run)
        return self.v_ms.n

    def gop(self, fn, pre=()):
        self.g_ms.n += 1

        def run(g, fn=fn, pre=tuple(pre)):
            _pre(g, pre)
            fn(g).then_inc(self.g_ms.h, 1)
        self.emit("gpsimd", run)
        return self.g_ms.n

    def aop(self, fn, pre=()):
        self.a_ms.n += 1

        def run(a, fn=fn, pre=tuple(pre)):
            _pre(a, pre)
            fn(a).then_inc(self.a_ms.h, 1)
        self.emit("scalar", run)
        return self.a_ms.n

    def tmark(self):
        self.t_ms.n += 1
        self.emit("tensor", lambda t: t.nop().then_inc(self.t_ms.h, 1))
        return self.t_ms.n

    def barrier(self):
        tgt = self.bar.n + 5
        self.bar.n = tgt
        snap = {q: [(s, s.n) for s in ss if s.n > 0]
                for q, ss in self.qstreams.items()}
        for eng in self.ENGS:
            def run(e, eng=eng, snap=snap):
                for s, v in snap.get(eng, ()):
                    e.wait_ge(s.h, v)
                if eng in ("vector", "gpsimd"):
                    e.engine_nop().then_inc(self.bar.h, 1)
                else:
                    e.nop().then_inc(self.bar.h, 1)
            self.emit(eng, run)
        for eng in self.ENGS:
            self.wait(eng, self.bar, tgt)


def build_program(nz_of_t, n_mask, upto="full", dbg=()):
    K_UPTO_GATE = [False]
    nc = bass.Bass("TRN2", target_bir_lowering=False, debug=False)
    nc.detect_race_conditions = False

    dx = nc.dram_tensor("x", [T, D], F32, kind="ExternalInput").ap()
    dw1 = nc.dram_tensor("w1", [D, D], F32, kind="ExternalInput").ap()
    dw2 = nc.dram_tensor("w2", [D, D], F32, kind="ExternalInput").ap()
    db2 = nc.dram_tensor("b2", [D], F32, kind="ExternalInput").ap()
    dmask = nc.dram_tensor("mask_nz", [max(n_mask, 1) * 128, 128], F32,
                           kind="ExternalInput").ap()
    dqw_gu = nc.dram_tensor("qw_gu", [D, NG // 8], I32, kind="ExternalInput").ap()
    dqz_gu = nc.dram_tensor("qz_gu", [KT, NG // 8], I32, kind="ExternalInput").ap()
    ds_gu = nc.dram_tensor("s_gu", [KT, NG], F32, kind="ExternalInput").ap()
    dqw_dn = nc.dram_tensor("qw_dn", [INTER, D // 8], I32, kind="ExternalInput").ap()
    dqz_dn = nc.dram_tensor("qz_dn", [MT, D // 8], I32, kind="ExternalInput").ap()
    ds_dn = nc.dram_tensor("s_dn", [MT, D], F32, kind="ExternalInput").ap()
    didf = nc.dram_tensor("identf", [128, 128], F32, kind="ExternalInput").ap()
    dones = nc.dram_tensor("ones_col", [128, 1], BF16, kind="ExternalInput").ap()
    dones1b = nc.dram_tensor("ones_row_b", [1, 128], BF16, kind="ExternalInput").ap()
    dones1f = nc.dram_tensor("ones_row_f", [1, 128], F32, kind="ExternalInput").ap()
    dind_gu = nc.dram_tensor("ind_gu", [128, KT * KT], BF16, kind="ExternalInput").ap()
    dind_dn = nc.dram_tensor("ind_dn", [128, NJ * MT], BF16, kind="ExternalInput").ap()
    dstair2 = nc.dram_tensor("stair2", [2, 128], BF16, kind="ExternalInput").ap()
    dout = nc.dram_tensor("out", [T, D], F32, kind="ExternalOutput").ap()

    dxT = [nc.dram_tensor(f"xT{i}", [D, T], F32, kind="Internal").ap()
           for i in range(4)]
    dxn = nc.dram_tensor("xn_sc", [D, T], BF16, kind="Internal").ap()
    dguc = nc.dram_tensor("gu_cache", [2 * NJ * 128, KT * SW], BF16, kind="Internal").ap()
    ddnc = nc.dram_tensor("dn_cache", [MT * 8 * 128, 512], BF16, kind="Internal").ap()
    dh = nc.dram_tensor("h_sc", [MT * 4 * 128, 512], BF16, kind="Internal").ap()
    dzs_gu = nc.dram_tensor("zs_gu_sc", [KT, NG], BF16, kind="Internal").ap()
    dzs_dn = nc.dram_tensor("zs_dn_sc", [MT, D], BF16, kind="Internal").ap()
    dw2bf = nc.dram_tensor("w2bf", [D, D], BF16, kind="Internal").ap()
    dw1c = nc.dram_tensor("w1c", [32 * 128, KT * 128], BF16, kind="Internal").ap()
    dw2tc = nc.dram_tensor("w2tc", [8 * KT * 128, 512], BF16, kind="Internal").ap()

    pb = [nc.psum_tensor(f"pbank{i}", [128, 512], F32).__enter__() for i in range(8)]
    b = B(nc)

    al = Alloc(nc, SBUF_BASE)
    sb_idf = al.take("idf", [128, 128], F32)
    sb_ones = al.take("onesc", [128, 1], BF16)
    sb_ones1b = al.take("ones1b", [1, 128], BF16)
    sb_ones1f = al.take("ones1f", [1, 128], F32)
    sb_indgu = al.take("indgu", [128, KT * KT], BF16)
    sb_stair2 = al.take("stair2", [2, 128], BF16)
    sb_b2 = al.take("b2t", [128, 32], F32)
    PERSIST = al.off

    s_const = b.stream("sync")
    s_w2bf = b.stream("gpsimd")
    for out, in_ in ((sb_idf[:], didf), (sb_ones[:], dones),
                     (sb_ones1b[:], dones1b), (sb_ones1f[:], dones1f),
                     (sb_indgu[:], dind_gu), (sb_stair2[:], dstair2)):
        b.dma("sync", out, in_, s_const)
    b.dma("sync", sb_b2[:], db2.rearrange("(j p) -> p j", p=128), s_const,
          allow_slow_non_contiguous=True)
    s_w2c = [b.sem() for _ in range(4)]
    for r in range(4):
        b.dma("gpsimd",
              dw2bf[1024 * r:1024 * (r + 1), :].rearrange("(a p) c -> p a c", p=128),
              dw2[1024 * r:1024 * (r + 1), :].rearrange("(a p) c -> p a c", p=128),
              s_w2c[r])
    w1cv = dw1c.rearrange("(st p) (dt c) -> st p dt c", p=128, c=128)
    for st in range(32):
        b.dma("gpsimd", w1cv[st, :, :, :],
              dw1[:, 128 * st:128 * (st + 1)].rearrange("(dt p) c -> p dt c", p=128),
              s_w2bf)
    w1c_done = s_w2bf.n
    for eng in ("tensor", "vector", "scalar"):
        b.wait(eng, s_const, s_const.n)

    def emit_tap(dsrc_bytes_ap, nrows, rowlen_f32):
        lo = Alloc(nc, PERSIST)
        tbuf = lo.take("tapbuf", [128, 2048], F32)
        s_tl = b.stream("sync")
        s_tw = b.stream("gpsimd")
        dflat = dout.rearrange("a c -> (a c)")
        for i in range(0, nrows, 128):
            n = min(128, nrows - i)
            wv = b.dma("sync", tbuf[:n, :rowlen_f32], dsrc_bytes_ap[i:i + n, :],
                       s_tl, pre=[(s_tw, s_tw.n)])
            vv = b.vop(lambda v: v.engine_nop(), pre=[(s_tl, wv)])
            seg = dflat[i * rowlen_f32:(i + n) * rowlen_f32]
            b.dma("gpsimd", seg.rearrange("(p f) -> p f", f=rowlen_f32),
                  tbuf[:n, :rowlen_f32], s_tw, pre=[(b.v_ms, vv)])

    def finish():
        b.barrier()
        with nc.Block() as blk:
            for eng in B.ENGS:
                getattr(blk, eng)(lambda e, eng=eng: [f(e) for f in b.ops[eng]])
        return nc

    lnal = Alloc(nc, SBUF_TOP - 66000)
    ln_xb = lnal.take("lnxb", [128, KT * 512], BF16)
    ln_xq = [lnal.take(f"lnxq{i}", [128, 8 * 512], BF16) for i in range(2)]
    ln_r1 = lnal.take("lnr1", [1, 512], F32)
    ln_r2 = lnal.take("lnr2", [1, 512], F32)
    ln_rt = lnal.take("lnrt", [1, 512], F32)
    ln_rb = lnal.take("lnrb", [128, 512], F32)
    ln_mb = lnal.take("lnmb", [128, 512], F32)
    ln_tmp = lnal.take("lntmp", [128, 512], F32)
    ln_xn = [lnal.take(f"lnxn{i}", [128, 512], BF16) for i in range(4)]
    s_lnl = b.stream("gpsimd")
    s_lw = [b.stream("gpsimd") for _ in range(4)]
    ln_pace = b.sem()
    ln_st = {"xbf": 0, "sq0": 0, "sq1": 0, "xn": [0] * 4}

    def ln_load(dsrc, tc, seg, pre):
        p = list(pre)
        if seg is None or seg == 0:
            p.append((b.v_ms, ln_st["xbf"]))
        if seg is None:
            b.dma("gpsimd",
                  ln_xb[:].rearrange("p (dt c) -> p dt c", c=512),
                  dsrc[:, 512 * tc:512 * (tc + 1)]
                  .rearrange("(dt p) c -> p dt c", p=128),
                  s_lnl, pre=p)
        else:
            b.dma("gpsimd",
                  ln_xb[:, 2048 * seg:2048 * (seg + 1)]
                  .rearrange("p (dt c) -> p dt c", c=512),
                  dsrc[512 * seg:512 * (seg + 1), 512 * tc:512 * (tc + 1)]
                  .rearrange("(dt p) c -> p dt c", p=128),
                  s_lnl, pre=p)

    def ln_compute(tc):
        ldv = s_lnl.n
        p0p = ln_pace.n
        sqv = []
        for c in range(4):
            if c == 0:
                pre = [(s_lnl, ldv)]
                if ln_st["sq0"]:
                    pre.append((ln_pace, ln_st["sq0"]))
            elif c == 1:
                pre = [(b.rdy[7], ln_st["sq1"])] if ln_st["sq1"] else []
            elif c == 2:
                pre = [(ln_pace, p0p + 1)]
            else:
                pre = [(ln_pace, p0p + 2)]
            sqv.append(b.aop(lambda a, c=c: a.activation(
                ln_xq[c % 2][:], ln_xb[:, 4096 * c:4096 * (c + 1)], AF.Square),
                pre=pre))
        rv6 = b.group(6, [
            (lambda t, s, e, dt=dt: t.matmul(
                pb[6][:1, :], sb_ones[:], ln_xb[:, 512 * dt:512 * (dt + 1)],
                start=s, stop=e)) for dt in range(KT)],
            pre=[(s_lnl, ldv)])
        wf7 = b.uses[7]
        b.uses[7] += 1
        b.rdy[7].n += 1
        rv7 = b.rdy[7].n

        def sq_run(t, sqv=tuple(sqv), wf7=wf7):
            if wf7 > 0:
                t.wait_ge(b.fre[7].h, wf7)
            for c in range(4):
                t.wait_ge(b.a_ms.h, sqv[c])
                for i in range(8):
                    dt = 8 * c + i
                    inst = t.matmul(pb[7][:1, :], sb_ones[:],
                                    ln_xq[c % 2][:, 512 * i:512 * (i + 1)],
                                    start=(dt == 0), stop=(dt == KT - 1))
                    if dt == KT - 1:
                        inst.then_inc(b.rdy[7].h, 1)
                    elif i == 7:
                        inst.then_inc(ln_pace.h, 1)
        b.emit("tensor", sq_run)
        ln_pace.n += 3
        ln_st["sq0"] = p0p + 3
        ln_st["sq1"] = rv7
        b.consume("vector", 6, rv6,
                  lambda v: v.tensor_scalar_mul(ln_r1[:], pb[6][:1, :], 1.0 / D))
        b.consume("vector", 7, rv7,
                  lambda v: v.tensor_scalar_mul(ln_r2[:], pb[7][:1, :], 1.0 / D))

        def vstat(v):
            v.tensor_tensor(ln_rt[:], ln_r1[:], ln_r1[:], ALU.mult)
            v.tensor_tensor(ln_r2[:], ln_r2[:], ln_rt[:], ALU.subtract)
            return v.tensor_scalar_add(ln_r2[:], ln_r2[:], EPS)
        vv = b.vop(vstat)
        av = b.aop(lambda a: a.activation(ln_r2[:], ln_r2[:], AF.Sqrt),
                   pre=[(b.v_ms, vv)])

        def vrec(v):
            v.reciprocal(ln_r2[:], ln_r2[:])
            return v.tensor_tensor(ln_r1[:], ln_r1[:], ln_r2[:], ALU.mult)
        vv = b.vop(vrec, pre=[(b.a_ms, av)])
        for bnk, srcr, dst in ((6, ln_r2, ln_rb), (7, ln_r1, ln_mb)):
            rvb = b.group(bnk, [
                lambda t, s, e, bnk=bnk, srcr=srcr: t.matmul(
                    pb[bnk][:], sb_ones1f[:], srcr[:], start=s, stop=e)],
                pre=[(b.v_ms, vv)])
            b.consume("vector", bnk, rvb,
                      lambda v, dst=dst, bnk=bnk: v.tensor_copy(dst[:], pb[bnk][:]))
        lastv = 0
        for dt in range(KT):
            sl = dt % 4

            def vnorm(v, dt=dt, sl=sl):
                v.tensor_tensor(ln_tmp[:], ln_xb[:, 512 * dt:512 * (dt + 1)],
                                ln_rb[:], ALU.mult)
                return v.tensor_tensor(ln_xn[sl][:], ln_tmp[:], ln_mb[:],
                                       ALU.subtract)
            pre = [(s_lw[sl], ln_st["xn"][sl])] if ln_st["xn"][sl] else []
            lastv = b.vop(vnorm, pre=pre)
            b.dma("gpsimd",
                  dxn[128 * dt:128 * (dt + 1), 512 * tc:512 * (tc + 1)],
                  ln_xn[sl][:], s_lw[sl], pre=[(b.v_ms, lastv)])
            ln_st["xn"][sl] = s_lw[sl].n
        ln_st["xbf"] = lastv

    p0 = Alloc(nc, PERSIST)
    sb_xh = [p0.take(f"p0xh{i}", [128, D], F32) for i in range(4)]
    sb_xts = [p0.take(f"p0xts{i}", [128, 512], F32) for i in range(2)]
    sb_qzg = p0.take("p0qzg", [KT, NG // 8], I32)
    sb_zu = p0.take("p0zu", [KT, 4096], U16)
    sb_zb = p0.take("p0zb", [KT, 4096], BF16)
    sb_w2ts = [p0.take(f"w2ts{i}", [128, 512], BF16) for i in range(8)]
    sb_qzd = p0.take("p0qzd", [MT, D // 8], I32)
    sb_zud = p0.take("p0zud", [MT, D], U16)
    sb_zbd = p0.take("p0zbd", [MT, D], BF16)

    s_px = b.stream("sync")
    s_pw = [b.stream("gpsimd") for _ in range(2)]
    xts_guard = [0, 0]
    xh_guard = [0]
    for tg in range(4):
        wv = [b.dma("sync", sb_xh[i][:],
                    dx[(4 * tg + i) * 128:(4 * tg + i + 1) * 128, :], s_px,
                    pre=([(b.t_ms, xh_guard[0])] if i == 0 else ()))
              for i in range(4)]
        for dt in range(KT):
            bnk = dt % 2
            wf = b.uses[bnk]
            b.uses[bnk] += 1
            b.rdy[bnk].n += 1
            rv = b.rdy[bnk].n

            def trun(t, dt=dt, bnk=bnk, wf=wf, w=wv[3]):
                t.wait_ge(s_px.h, w)
                if wf > 0:
                    t.wait_ge(b.fre[bnk].h, wf)
                for i in range(4):
                    inst = t.transpose(pb[bnk][:, 128 * i:128 * (i + 1)],
                                       sb_xh[i][:, 128 * dt:128 * (dt + 1)],
                                       sb_idf[:])
                    if i == 3:
                        inst.then_inc(b.rdy[bnk].h, 1)
            b.emit("tensor", trun)
            st = sb_xts[dt % 2]
            fv = b.consume("vector", bnk, rv,
                           lambda v, st=st, bnk=bnk: v.tensor_copy(st[:], pb[bnk][:]),
                           pre=[(s_pw[dt % 2], xts_guard[dt % 2])])
            b.dma("gpsimd", dxT[0][128 * dt:128 * (dt + 1), 512 * tg:512 * (tg + 1)],
                  st[:], s_pw[dt % 2], pre=[(b.fre[bnk], fv)])
            xts_guard[dt % 2] = s_pw[dt % 2].n
        xh_guard[0] = b.tmark()
        ln_load(dxT[0], tg, None, [(s_pw[0], s_pw[0].n), (s_pw[1], s_pw[1].n)])
        ln_compute(tg)

    s_w2s = [b.sem() for _ in range(8)]
    w2w_guard = [0] * 8
    w2tc_v = dw2tc.rearrange("(tcb st p) c -> tcb st p c", st=KT, p=128)
    for dtb in range(8):
        for st in range(KT):
            sl = (dtb * KT + st) % 8
            p_ = [(s_w2c[dtb // 2], 16)]
            if w2w_guard[sl]:
                p_.append((s_w2s[sl], w2w_guard[sl]))
            wv2t = b.dma("scalar", sb_w2ts[sl][:],
                         dw2bf[512 * dtb:512 * (dtb + 1), 128 * st:128 * (st + 1)],
                         s_w2s[sl], transpose=True, pre=p_)
            b.dma("scalar", w2tc_v[dtb, st, :, :], sb_w2ts[sl][:],
                  s_w2s[sl], pre=[(s_w2s[sl], wv2t)])
            w2w_guard[sl] = s_w2s[sl].n
    w2tc_done = [(s, s.n) for s in s_w2s]

    s_zs = b.stream("sync")
    wq = b.dma("sync", sb_qzg[:], dqz_gu, s_zs)
    zb_guard = 0
    for ch in range(6):
        c0 = 4096 * ch
        w = min(4096, NG - c0)

        def zrun(v, c0=c0, w=w):
            qz = sb_qzg[:].bitcast(U16)
            for o in range(0, w, SW):
                u0 = ((c0 + o) // SW) * 64
                for i in range(4):
                    v.tensor_scalar(sb_zu[:, o + 64 * i:o + 64 * (i + 1)],
                                    qz[:, u0:u0 + 64], 4 * i, 15,
                                    ALU.logical_shift_right, ALU.bitwise_and)
            return v.tensor_copy(sb_zb[:, :w], sb_zu[:, :w])
        vv = b.vop(zrun, pre=[(s_zs, max(wq, zb_guard))])
        b.dma("sync", dzs_gu[:, c0:c0 + w], sb_zb[:, :w], s_zs, pre=[(b.v_ms, vv)])
        zb_guard = s_zs.n

    wq = b.dma("sync", sb_qzd[:], dqz_dn, s_zs)

    def zdrun(v):
        qz = sb_qzd[:].bitcast(U16)
        for o in range(0, D, SW):
            u0 = (o // SW) * 64
            for i in range(4):
                v.tensor_scalar(sb_zud[:, o + 64 * i:o + 64 * (i + 1)],
                                qz[:, u0:u0 + 64], 4 * i, 15,
                                ALU.logical_shift_right, ALU.bitwise_and)
        return v.tensor_copy(sb_zbd[:], sb_zud[:])
    vv = b.vop(zdrun, pre=[(s_zs, wq)])
    b.dma("sync", dzs_dn[:], sb_zbd[:], s_zs, pre=[(b.v_ms, vv)])

    b.barrier()

    if upto == "p0":
        emit_tap(dxT[0], D, T)
        return finish()
    if upto == "ln1":
        emit_tap(dxn.bitcast(F32), D, T // 2)
        return finish()

    p1 = Alloc(nc, PERSIST)
    sb_qw = [p1.take(f"dqqw{i}", [128, NG // 8], I32) for i in range(2)]
    sb_srow = [p1.take(f"dqsr{i}", [2, 8192], BF16) for i in range(2)]
    sb_sbc = [p1.take(f"dqsbc{i}", [128, 1024], BF16) for i in range(8)]
    sb_nib = [p1.take(f"dqnib{i}", [128, 512], U16) for i in range(2)]
    sb_nbb = p1.take("dqnbb", [128, 512], BF16)
    sb_wst = [p1.take(f"dqwst{i}", [128, 2048], BF16) for i in range(2)]

    s_qw = [b.stream("sync") for _ in range(2)]
    s_sr = [b.stream("gpsimd") for _ in range(2)]
    s_wc = [b.stream("sync") for _ in range(2)]
    qw_guard = [0, 0]
    srow_guard = [(0, 0), (0, 0)]
    sbc_guard = [0] * 8
    nib_guard = [0, 0]
    wst_guard = [0, 0]
    subctr = [0]

    def emit_dequant(dqw, dcache, n_i32, n_pos, dsrows, dzrows, n_tiles, dn_mode):
        qsel = (dqw.rearrange("(j r s) c -> j s r c", r=64, s=4) if dn_mode else None)
        for g in range(n_tiles):
            qs = sb_qw[g % 2]
            if dn_mode:
                j, q = g // 2, g % 2
                wqv = b.dma("sync", qs[:, :n_i32],
                            qsel[j, 2 * q:2 * q + 2, :, :],
                            s_qw[g % 2], pre=[(b.v_ms, qw_guard[g % 2])])
            else:
                wqv = b.dma("sync", qs[:, :n_i32], dqw[128 * g:128 * (g + 1), :],
                            s_qw[g % 2], pre=[(b.v_ms, qw_guard[g % 2])])
            for ch in range(0, n_pos, 2048):
                w = min(2048, n_pos - ch)
                chi = (ch // 2048) % 2
                sr = sb_srow[chi]
                gb, gv = srow_guard[chi]
                nrow = 2 if dn_mode else 1
                g0 = (g // 2) * 2 if dn_mode else g
                srv = sr[:nrow, :].rearrange("a (u two c) -> a u two c", two=2, c=512)
                wsv = b.dma("gpsimd", srv[:, :w // 512, 0, :],
                            dsrows[g0:g0 + nrow, ch:ch + w]
                            .rearrange("a (u c) -> a u c", c=512),
                            s_sr[chi], pre=[(b.rdy[gb], gv)])
                wzv = b.dma("gpsimd", srv[:, :w // 512, 1, :],
                            dzrows[g0:g0 + nrow, ch:ch + w]
                            .rearrange("a (u c) -> a u c", c=512),
                            s_sr[chi])
                wstg = sb_wst[chi]
                last_rv = (6, 0)
                for sub in range(0, w, 512):
                    si = subctr[0] % 8
                    bnk = (2 * subctr[0]) % 8
                    bnkz = bnk + 1
                    subctr[0] += 1
                    sbc = sb_sbc[si]
                    lhsb = sb_stair2[:] if dn_mode else sb_ones1b[:]
                    rv = b.group(bnk, [
                        (lambda t, s, e, sr=sr, sub=sub, bnk=bnk, lhsb=lhsb,
                         nrow=nrow:
                         t.matmul(pb[bnk][:], lhsb,
                                  sr[:nrow, 2 * sub:2 * sub + 512],
                                  start=s, stop=e))],
                        pre=[(s_sr[chi], wzv)])
                    rvz = b.group(bnkz, [
                        (lambda t, s, e, sr=sr, sub=sub, bnkz=bnkz, lhsb=lhsb,
                         nrow=nrow:
                         t.matmul(pb[bnkz][:], lhsb,
                                  sr[:nrow, 2 * sub + 512:2 * sub + 1024],
                                  start=s, stop=e))])
                    last_rv = (bnkz, rvz)
                    fv = b.consume("scalar", bnk, rv,
                                   lambda a, sbc=sbc, bnk=bnk:
                                   a.activation(sbc[:, :512], pb[bnk][:],
                                                AF.Identity),
                                   pre=[(b.v_ms, sbc_guard[si])])
                    fvz = b.consume("scalar", bnkz, rvz,
                                    lambda a, sbc=sbc, bnkz=bnkz:
                                    a.activation(sbc[:, 512:], pb[bnkz][:],
                                                 AF.Identity))

                    nslot = subctr[0] % 2
                    nib = sb_nib[nslot]

                    def drun(v, qs=qs, sbc=sbc, pos=ch + sub, sub=sub,
                             nib=nib, n_i32=n_i32, wstg=wstg):
                        qu = qs[:, :n_i32].bitcast(U16)
                        for jj in range(2):
                            u0 = ((pos + SW * jj) // SW) * 64
                            for i in range(4):
                                v.tensor_scalar(
                                    nib[:, SW * jj + 64 * i:SW * jj + 64 * (i + 1)],
                                    qu[:, u0:u0 + 64], 4 * i, 15,
                                    ALU.logical_shift_right, ALU.bitwise_and)
                        zbv = sbc[:].rearrange("p (u two c) -> p u two c",
                                               two=2, c=512)
                        v.tensor_tensor(sb_nbb[:], nib[:], zbv[:, 0, 1, :],
                                        ALU.subtract)
                        return v.tensor_tensor(wstg[:, sub:sub + 512],
                                               sb_nbb[:], zbv[:, 0, 0, :], ALU.mult)
                    vv = b.vop(drun, pre=[(s_qw[g % 2], wqv),
                                          (s_wc[chi], wst_guard[chi]),
                                          (b.fre[bnk], fv), (b.fre[bnkz], fvz)])
                    sbc_guard[si] = vv
                srow_guard[chi] = last_rv
                vvw = b.v_ms.n
                if not dn_mode:
                    g4 = dcache.rearrange("(s p) (kt c) -> s p kt c", p=128, kt=KT)
                    for o in range(0, w, SW):
                        s_idx = (ch + o) // SW
                        b.dma("sync", g4[s_idx, :, g, :],
                              wstg[:, o:o + SW], s_wc[chi],
                              pre=([(b.v_ms, vvw)] if o == 0 else ()))
                else:
                    d4 = dcache.rearrange("(mt oc p) c -> mt oc p c", oc=8, p=128)
                    for o in range(0, w, 512):
                        b.dma("sync", d4[g, (ch + o) // 512, :, :],
                              wstg[:, o:o + 512], s_wc[chi],
                              pre=([(b.v_ms, vvw)] if o == 0 else ()))
                wst_guard[chi] = s_wc[chi].n
            qw_guard[g % 2] = b.v_ms.n

    emit_dequant(dqw_gu, dguc, NG // 8, NG, ds_gu, dzs_gu, KT, False)
    emit_dequant(dqw_dn, ddnc, D // 8, D, ds_dn, dzs_dn, MT, True)

    b.barrier()

    s_gx = b.stream("sync")
    s_rg = [b.stream("sync") for _ in range(4)]
    s_hw = [b.stream("gpsimd") for _ in range(4)]
    s_hh = b.stream("sync")
    s_wd = [b.stream("scalar") for _ in range(8)]
    s_xr = b.stream("gpsimd")
    s_os = [b.stream("gpsimd") for _ in range(8)]

    def emit_ffn(dsrc, ddst, ln_dst=False):
        lo = Alloc(nc, PERSIST)
        xnh = lo.take("gxn", [128, KT * 1024], BF16)
        ring = [lo.take(f"grg{i}", [128, KT * SW], BF16) for i in range(4)]
        silub = [lo.take(f"gsl{i}", [128, 2048], BF16) for i in range(2)]
        hstg = [lo.take(f"ghs{i}", [128, 1024], BF16) for i in range(4)]

        ring_guard = [(0, 0)] * 4
        silu_guard = [(0, 0), (0, 0)]
        silu_fv = {}
        hmul_fv = {}
        hst_guard = [0, 0, 0, 0]
        rot = [0]

        for half in range(2):
            pre0 = ([(b.rdy[k], b.rdy[k].n) for k in range(8)] if half == 1 else [])
            xwv = [b.dma("sync", xnh[:, 1024 * k:1024 * (k + 1)],
                         dxn[128 * k:128 * (k + 1), HALF * half:HALF * (half + 1)],
                         s_gx, pre=(pre0 if k == 0 else ()))
                   for k in range(KT)]

            for jp in range(2 * NJ):
                j, side = jp // 2, jp % 2
                slot = ring[jp % 4]
                pos0 = side * INTER + SW * j
                gb, gv = ring_guard[jp % 4]
                s_idx = pos0 // SW
                wv = b.dma("sync", slot[:],
                           dguc[128 * s_idx:128 * (s_idx + 1), :],
                           s_rg[jp % 4], pre=[(b.rdy[gb], gv)])
                for q in range(2):
                    for tc in range(2):
                        bnk = rot[0]
                        rot[0] = (rot[0] + 1) % 6
                        mms = [(lambda t, s, e, k=k, q=q, tc=tc, slot=slot, bnk=bnk:
                                t.matmul(pb[bnk][:],
                                         slot[:, SW * k + 128 * q:SW * k + 128 * (q + 1)],
                                         xnh[:, 1024 * k + 512 * tc:1024 * k + 512 * (tc + 1)],
                                         start=s, stop=e))
                               for k in range(KT)]
                        rv = b.group(bnk, mms,
                                     pre=([(s_rg[jp % 4], wv), (s_gx, xwv[-1])]
                                          if (q == 0 and tc == 0) else ()))
                        ring_guard[jp % 4] = (bnk, rv)
                        if side == 0:
                            sl = silub[j % 2]
                            gb2, gv2 = silu_guard[j % 2]
                            sfv = b.consume("scalar", bnk, rv,
                                            lambda a, sl=sl, q=q, tc=tc, bnk=bnk:
                                            a.activation(
                                                sl[:, 1024 * q + 512 * tc:1024 * q + 512 * (tc + 1)],
                                                pb[bnk][:], AF.Silu),
                                            pre=([(b.fre[gb2], gv2)]
                                                 if (q == 0 and tc == 0) else ()))
                            silu_fv[(q, tc)] = (bnk, sfv)
                        else:
                            hs = hstg[2 * (j % 2) + q]
                            sl = silub[j % 2]

                            def hmul(v, hs=hs, sl=sl, q=q, tc=tc, bnk=bnk):
                                return v.tensor_tensor(
                                    hs[:, 512 * tc:512 * (tc + 1)], pb[bnk][:],
                                    sl[:, 1024 * q + 512 * tc:1024 * q + 512 * (tc + 1)],
                                    ALU.mult)
                            sb_, sv_ = silu_fv[(q, tc)]
                            fv = b.consume("vector", bnk, rv, hmul,
                                           pre=([(b.fre[sb_], sv_),
                                                 (s_hw[2 * (j % 2) + q],
                                                  hst_guard[2 * (j % 2) + q])]
                                                if tc == 0 else [(b.fre[sb_], sv_)]))
                            hmul_fv[(j % 2, q)] = (bnk, fv)
                            if tc == 1:
                                silu_guard[j % 2] = (bnk, fv)
                if side == 1:
                    dh4 = dh.rearrange("(mt tq p) c -> mt tq p c", tq=4, p=128)
                    for q in range(2):
                        hb, hv = hmul_fv[(j % 2, q)]
                        for tc in range(2):
                            b.dma("gpsimd",
                                  dh4[2 * j + q, 2 * half + tc, :, :],
                                  hstg[2 * (j % 2) + q][:, 512 * tc:512 * (tc + 1)],
                                  s_hw[2 * (j % 2) + q],
                                  pre=([(b.fre[hb], hv)] if tc == 0 else ()))
                        hst_guard[2 * (j % 2) + q] = s_hw[2 * (j % 2) + q].n

        b.barrier()
        if K_UPTO_GATE[0]:
            return

        lo = Alloc(nc, PERSIST)
        hh = lo.take("dnh", [128, MT * 512], BF16)
        wdn = [lo.take(f"dnw{i}", [128, 512], BF16) for i in range(8)]
        xres = [lo.take(f"dnxr{i}", [128, 512], F32) for i in range(8)]
        ost = [lo.take(f"dnos{i}", [128, 512], F32) for i in range(8)]

        wdn_guard = [0] * 8
        xres_guard = [0] * 8
        ost_guard = [0] * 8
        dsel = ddnc.rearrange("(mt oc p) c -> mt oc p c", oc=8, p=128)
        dh4 = dh.rearrange("(mt tq p) c -> mt tq p c", tq=4, p=128)
        xsel = dsrc.rearrange("(c r s) t -> c s r t", r=64, s=4)
        osel = ddst.rearrange("(c r s) t -> c s r t", r=64, s=4)

        for tq in range(4):
            pre0 = [(b.rdy[k], b.rdy[k].n) for k in range(8)]
            hwv = [b.dma("sync", hh[:, 512 * mt:512 * (mt + 1)],
                         dh4[mt, tq, :, :],
                         s_hh, pre=(pre0 if mt == 0 else ()))
                   for mt in range(MT)]

            for og in range(8):
                base = 4 * (og % 2)
                banks = [base + i for i in range(4)]
                for ot in range(4):
                    bnk = banks[ot]
                    jj, q = 2 * og + ot // 2, ot % 2
                    b.dma("gpsimd", xres[bnk][:],
                          xsel[jj, 2 * q:2 * q + 2, :, 512 * tq:512 * (tq + 1)],
                          s_xr, pre=[(b.fre[bnk], xres_guard[bnk])])
                xr_done = s_xr.n
                open_wf = []
                rvs = {}
                for bnk in banks:
                    open_wf.append((bnk, b.uses[bnk]))
                    b.uses[bnk] += 1
                    b.rdy[bnk].n += 1
                    rvs[bnk] = b.rdy[bnk].n
                for mt in range(MT):
                    ws = wdn[mt % 8]
                    wv = b.dma("scalar", ws[:], dsel[mt, og, :, :],
                               s_wd[mt % 8], pre=[(b.t_ms, wdn_guard[mt % 8])])

                    def dmm(t, mt=mt, ws=ws, wv=wv, banks=tuple(banks),
                            open_wf=tuple(open_wf), hv=hwv[-1]):
                        t.wait_ge(s_wd[mt % 8].h, wv)
                        if mt == 0:
                            t.wait_ge(s_hh.h, hv)
                            for bnk, wf in open_wf:
                                if wf > 0:
                                    t.wait_ge(b.fre[bnk].h, wf)
                        for ot in range(4):
                            inst = t.matmul(
                                pb[banks[ot]][:], ws[:, 128 * ot:128 * (ot + 1)],
                                hh[:, 512 * mt:512 * (mt + 1)],
                                start=(mt == 0), stop=(mt == MT - 1))
                            if mt == MT - 1:
                                inst.then_inc(b.rdy[banks[ot]].h, 1)
                        if mt != MT - 1:
                            inst.then_inc(b.t_ms.h, 1)
                        else:
                            t.nop().then_inc(b.t_ms.h, 1)
                    b.t_ms.n += 1
                    b.emit("tensor", dmm)
                    wdn_guard[mt % 8] = b.t_ms.n
                for ot in range(4):
                    bnk = banks[ot]
                    jj, q = 2 * og + ot // 2, ot % 2
                    if "noxres" in dbg:
                        fv = b.consume("vector", bnk, rvs[bnk],
                                       lambda v, bnk=bnk: v.tensor_scalar_mul(
                                           ost[bnk][:], pb[bnk][:], 0.5),
                                       pre=[(s_xr, xr_done),
                                            (s_os[bnk], ost_guard[bnk])])
                    else:
                        fv = b.consume("vector", bnk, rvs[bnk],
                                       lambda v, bnk=bnk: v.scalar_tensor_tensor(
                                           ost[bnk][:], pb[bnk][:], 0.5, xres[bnk][:],
                                           ALU.mult, ALU.add),
                                       pre=[(s_xr, xr_done),
                                            (s_os[bnk], ost_guard[bnk])])
                    xres_guard[bnk] = fv
                    b.dma("gpsimd",
                          osel[jj, 2 * q:2 * q + 2, :, 512 * tq:512 * (tq + 1)],
                          ost[bnk][:], s_os[bnk], pre=[(b.fre[bnk], fv)])
                    ost_guard[bnk] = s_os[bnk].n
                if ln_dst:
                    ln_load(ddst, tq, og,
                            [(s_os[bk], ost_guard[bk]) for bk in banks])
            if ln_dst:
                ln_compute(tq)
        b.barrier()

    def emit_attn(dsrc, ddst):
        lo = Alloc(nc, PERSIST)
        abuf = lo.take("abuf", [128, max(n_mask, 1) * 128], BF16)
        A_END = lo.off
        w1s = [lo.take(f"aw1{i}", [128, KT * 128], BF16) for i in range(4)]
        w2t = [lo.take(f"aw2{i}", [128, KT * 512], BF16) for i in range(2)]
        amask = [lo.take(f"amk{i}", [128, 128], F32) for i in range(2)]

        nz_of_chunk = []
        for tcb in range(8):
            ss = {}
            for tt in range(4 * tcb, 4 * tcb + 4):
                for (s_t, slot) in nz_of_t[tt]:
                    ss.setdefault(s_t, []).append((tt, slot))
            nz_of_chunk.append(sorted(ss.items()))

        s_w2 = [b.stream("sync") for _ in range(2)]
        s_w1 = [b.stream("sync") for _ in range(4)]
        s_mk = [b.stream("sync") for _ in range(2)]
        w1_guard = [0] * 4
        w2_guard = [0, 0]
        mk_guard = [("v", 0, 0), ("v", 0, 0)]
        arot = [0]
        mki = [0]
        for tcb in range(8):
            w2s = w2t[tcb % 2]
            wv2 = b.dma("sync", w2s[:].rearrange("p (dt c) -> p dt c", c=512),
                        dw2tc[4096 * tcb:4096 * (tcb + 1), :]
                        .rearrange("(dt p) c -> p dt c", p=128),
                        s_w2[tcb % 2],
                        pre=([(b.t_ms, w2_guard[tcb % 2])] +
                             (w2tc_done if tcb == 0 else [])))
            si = 0
            for (s_t, uses) in nz_of_chunk[tcb]:
                w1b = w1s[si % 4]
                wv1 = b.dma("sync", w1b[:],
                            dw1c[128 * s_t:128 * (s_t + 1), :],
                            s_w1[si % 4],
                            pre=[(b.t_ms, w1_guard[si % 4]),
                                 (s_w2bf, w1c_done)])
                bnk = arot[0]
                arot[0] = (arot[0] + 1) % 6
                mms = [(lambda t, s, e, k=k, w1b=w1b, w2s=w2s, bnk=bnk:
                        t.matmul(pb[bnk][:], w1b[:, 128 * k:128 * (k + 1)],
                                 w2s[:, 512 * k:512 * (k + 1)],
                                 start=s, stop=e)) for k in range(KT)]
                rv = b.group(bnk, mms, pre=[(s_w2[tcb % 2], wv2),
                                            (s_w1[si % 4], wv1)])
                tm = b.tmark()
                w1_guard[si % 4] = tm
                w2_guard[tcb % 2] = tm
                nuses = len(uses)
                for ui, (tt, slot) in enumerate(uses):
                    mi = mki[0] % 2
                    mk = amask[mi]
                    kind, gb3, gv3 = mk_guard[mi]
                    wmk = b.dma("sync", mk[:], dmask[128 * slot:128 * (slot + 1), :],
                                s_mk[mi],
                                pre=[(b.v_ms, gv3) if kind == "v"
                                     else (b.fre[gb3], gv3)])
                    mki[0] += 1
                    qq = tt - 4 * tcb

                    def amul(v, mk=mk, qq=qq, slot=slot, bnk=bnk):
                        return v.tensor_tensor(
                            abuf[:, 128 * slot:128 * (slot + 1)],
                            pb[bnk][:, 128 * qq:128 * (qq + 1)], mk[:], ALU.mult)
                    if ui == nuses - 1:
                        fv = b.consume("vector", bnk, rv, amul,
                                       pre=[(s_mk[mi], wmk)])
                        mk_guard[mi] = ("f", bnk, fv)
                    else:
                        vv = b.vop(amul, pre=[(s_mk[mi], wmk),
                                              (b.rdy[bnk], rv)])
                        mk_guard[mi] = ("v", 0, vv)
                si += 1
        b.barrier()

        lo = Alloc(nc, A_END)
        xn2 = [lo.take(f"yxn{i}", [128, KT * 512], BF16) for i in range(2)]
        gsb = [lo.take(f"ygs{i}", [128, 512], F32) for i in range(4)]
        x1t = [lo.take(f"yx1{i}", [128, 512], F32) for i in range(4)]
        x2t = [lo.take(f"yx2{i}", [128, 512], F32) for i in range(4)]

        s_x2l = b.stream("sync")
        s_x1l = [b.stream("gpsimd") for _ in range(4)]
        s_x2w = [b.stream("gpsimd") for _ in range(4)]
        yrot = [0]
        gs_guard = [0] * 4
        x1_guard = [0] * 4
        x2_guard = [0] * 4
        xn2_guard = [0, 0]
        for tk in range(4):
            sl2 = tk % 2
            xw = b.dma("sync", xn2[sl2][:].rearrange("p (k c) -> p k c", c=512),
                       dxn[:, 512 * tk:512 * (tk + 1)]
                       .rearrange("(k p) c -> p k c", p=128),
                       s_x2l, pre=[(b.t_ms, xn2_guard[sl2])])
            for J in range(32):
                pairs = nz_of_t[J]
                assert pairs, "empty attn row-tile unsupported"
                bnk = yrot[0]
                yrot[0] = (yrot[0] + 1) % 6
                mms = [(lambda t, s, e, st=st, slot=slot, bnk=bnk, sl2=sl2:
                        t.matmul(pb[bnk][:],
                                 abuf[:, 128 * slot:128 * (slot + 1)],
                                 xn2[sl2][:, 512 * st:512 * (st + 1)],
                                 start=s, stop=e))
                       for (st, slot) in pairs]
                rv = b.group(bnk, mms, pre=([(s_x2l, xw)] if J == 0 else []))
                gi = J % 4
                gs = gsb[gi]
                fvg = b.consume("scalar", bnk, rv,
                                lambda a, gs=gs, J=J, bnk=bnk: a.activation(
                                    gs[:], pb[bnk][:], AF.Gelu,
                                    bias=sb_b2[:, J:J + 1]),
                                pre=[(b.v_ms, gs_guard[gi])])
                wx1 = b.dma("gpsimd", x1t[gi][:],
                            dsrc[128 * J:128 * (J + 1), 512 * tk:512 * (tk + 1)],
                            s_x1l[gi], pre=[(b.v_ms, x1_guard[gi])])

                def yadd(v, gs=gs, gi=gi):
                    return v.tensor_tensor(x2t[gi][:], x1t[gi][:], gs[:], ALU.add)
                vv = b.vop(yadd, pre=[(b.fre[bnk], fvg), (s_x1l[gi], wx1),
                                      (s_x2w[gi], x2_guard[gi])])
                gs_guard[gi] = vv
                x1_guard[gi] = vv
                b.dma("gpsimd",
                      ddst[128 * J:128 * (J + 1), 512 * tk:512 * (tk + 1)],
                      x2t[gi][:], s_x2w[gi], pre=[(b.v_ms, vv)])
                x2_guard[gi] = s_x2w[gi].n
                if J % 4 == 3:
                    ln_load(ddst, tk, J // 4,
                            [(s_x2w[g2], x2_guard[g2]) for g2 in range(4)])
            xn2_guard[sl2] = b.tmark()
            ln_compute(tk)
        b.barrier()

    def emit_final():
        lo = Alloc(nc, PERSIST)
        fst = [lo.take(f"ffs{i}", [128, KT * 128], F32) for i in range(2)]
        ostg = [lo.take(f"fos{i}", [128, D], F32) for i in range(2)]
        s_fl = [b.stream("sync") for _ in range(2)]
        s_fo = [b.stream("gpsimd") for _ in range(2)]
        f_guard = [0, 0]
        o_guard = [0, 0]
        for tt in range(16):
            fs = fst[tt % 2]
            wv = b.dma("sync", fs[:].rearrange("p (dt c) -> p dt c", dt=KT),
                       dxT[3][:, 128 * tt:128 * (tt + 1)]
                       .rearrange("(dt p) c -> p dt c", p=128),
                       s_fl[tt % 2], pre=[(b.t_ms, f_guard[tt % 2])])
            og = ostg[tt % 2]
            fvs = {}
            for bg in range(8):
                bnk = bg % 2 + 2
                wf = b.uses[bnk]
                b.uses[bnk] += 1
                b.rdy[bnk].n += 1
                rv = b.rdy[bnk].n

                def trun(t, bg=bg, bnk=bnk, wf=wf, wv=wv, fs=fs, tt=tt):
                    t.wait_ge(s_fl[tt % 2].h, wv)
                    if wf > 0:
                        t.wait_ge(b.fre[bnk].h, wf)
                    for i in range(4):
                        inst = t.transpose(
                            pb[bnk][:, 128 * i:128 * (i + 1)],
                            fs[:, 128 * (4 * bg + i):128 * (4 * bg + i + 1)],
                            sb_idf[:])
                        if i == 3:
                            inst.then_inc(b.rdy[bnk].h, 1)
                b.emit("tensor", trun)
                fvs[bnk] = b.consume(
                    "vector", bnk, rv,
                    lambda v, og=og, bg=bg, bnk=bnk:
                    v.tensor_copy(og[:, 512 * bg:512 * (bg + 1)], pb[bnk][:]),
                    pre=([(s_fo[tt % 2], o_guard[tt % 2])] if bg == 0 else ()))
            f_guard[tt % 2] = b.tmark()
            b.dma("gpsimd", dout[128 * tt:128 * (tt + 1), :], og[:],
                  s_fo[tt % 2], pre=[(b.fre[2], fvs[2]), (b.fre[3], fvs[3])])
            o_guard[tt % 2] = s_fo[tt % 2].n

    if upto == "p1":
        emit_tap(ddnc.bitcast(F32), 2048, 2048)
        return finish()
    if upto == "gate":
        K_UPTO_GATE[0] = True
    emit_ffn(dxT[0], dxT[1], ln_dst=True)
    if upto == "gate":
        emit_tap(dh.bitcast(F32), 4096, 1024)
        return finish()
    if upto == "ffn1":
        emit_tap(dxT[1], D, T)
        return finish()
    emit_attn(dxT[1], dxT[2])
    if upto == "attn":
        emit_tap(dxT[2], D, T)
        return finish()
    emit_ffn(dxT[2], dxT[3])
    emit_final()
    b.barrier()

    with nc.Block() as blk:
        for eng in B.ENGS:
            getattr(blk, eng)(lambda e, eng=eng: [f(e) for f in b.ops[eng]])
    return nc



def _host_prep(inputs):
    mask = np.asarray(inputs["L0_attn_sparse_mask"], dtype=np.float32)
    tiles = mask.reshape(32, 128, 32, 128)
    nzmap = (tiles != 0).any(axis=(1, 3))
    nz_of_t = [[] for _ in range(32)]
    packed = []
    slot = 0
    for tj in range(32):
        for si in range(32):
            if nzmap[si, tj]:
                packed.append(np.ascontiguousarray(tiles[si, :, tj, :]))
                nz_of_t[tj].append((si, slot))
                slot += 1
    mask_nz = (np.concatenate(packed, axis=0) if packed
               else np.zeros((128, 128), np.float32))
    n_mask = max(slot, 1)

    perm_gu = np.concatenate([_pos2dense(INTER), INTER + _pos2dense(INTER)])
    perm_dn = _pos2dense(D)
    s_gu = np.ascontiguousarray(
        np.asarray(inputs["L0_ffn_scales_gu"], np.float32)[:, perm_gu])
    s_dn = np.ascontiguousarray(
        np.asarray(inputs["L0_ffn_scales_dn"], np.float32)[:, perm_dn])

    ind_gu = np.zeros((128, KT * KT), dtype=bfdt)
    for g in range(KT):
        ind_gu[:, KT * g + g] = 1.0
    ind_dn = np.zeros((128, NJ * MT), dtype=bfdt)
    p = np.arange(128)
    for j in range(NJ):
        ind_dn[p, MT * j + 2 * j + (p % 64) // 32] = 1.0

    common = dict(
        w1=np.ascontiguousarray(np.asarray(inputs["L0_attn_w1"], np.float32)),
        w2=np.ascontiguousarray(np.asarray(inputs["L0_attn_w2"], np.float32)),
        b2=np.ascontiguousarray(np.asarray(inputs["L0_attn_b2"], np.float32)),
        mask_nz=np.ascontiguousarray(mask_nz),
        qw_gu=np.ascontiguousarray(np.asarray(inputs["L0_ffn_qweight_gu"], np.int32)),
        qz_gu=np.ascontiguousarray(np.asarray(inputs["L0_ffn_qzeros_gu"], np.int32)),
        s_gu=s_gu,
        qw_dn=np.ascontiguousarray(np.asarray(inputs["L0_ffn_qweight_dn"], np.int32)),
        qz_dn=np.ascontiguousarray(np.asarray(inputs["L0_ffn_qzeros_dn"], np.int32)),
        s_dn=s_dn,
        identf=np.eye(128, dtype=np.float32),
        stair2=np.stack([((np.arange(128) % 64) // 32 == k).astype(bfdt)
                         for k in range(2)]),
        ones_col=np.ones((128, 1), dtype=bfdt),
        ones_row_b=np.ones((1, 128), dtype=bfdt),
        ones_row_f=np.ones((1, 128), dtype=np.float32),
        ind_gu=ind_gu, ind_dn=ind_dn,
    )
    return nz_of_t, n_mask, common


def kernel(**inputs):
    nz_of_t, n_mask, common = _host_prep(inputs)
    x = np.asarray(inputs["L0_attn_x"], np.float32).reshape(SEQ, D)
    nc = build_program(nz_of_t, n_mask)
    in_maps = []
    for c in range(NCORES):
        m = dict(common)
        m["x"] = np.ascontiguousarray(x[T * c:T * (c + 1), :])
        in_maps.append(m)
    res = run_bass_kernel_spmd(nc, in_maps, core_ids=list(range(NCORES)))
    out = np.concatenate([res.results[c]["out"] for c in range(NCORES)], axis=0)
    return out.reshape(1, SEQ, D).astype(np.float32)


# version 17
# speedup vs baseline: 1.0826x; 1.0501x over previous
import numpy as np
import ml_dtypes
import concourse.bass as bass
import concourse.mybir as mybir
from concourse.bass_utils import run_bass_kernel_spmd

F32 = mybir.dt.float32
BF16 = mybir.dt.bfloat16
I32 = mybir.dt.int32
U16 = mybir.dt.uint16
AF = mybir.ActivationFunctionType
ALU = mybir.AluOpType
bfdt = ml_dtypes.bfloat16

NCORES = 8
D = 4096
SEQ = 16384
T = SEQ // NCORES
INTER = 11008
NG = 2 * INTER
NJ = INTER // 256
SW = 256
MT = 2 * NJ
KT = D // 128
HALF = T // 2
EPS = 1e-5
SBUF_BASE = 16512
SBUF_TOP = 229300


def _pos2dense(n):
    p = np.arange(n)
    j, rem = p // SW, p % SW
    i, r = rem // 64, rem % 64
    return j * SW + 4 * r + i


class Alloc:
    def __init__(self, nc, base, top=SBUF_TOP):
        self.nc, self.off, self.top = nc, base, top

    def take(self, name, shape, dtype, align=64):
        esz = {F32: 4, BF16: 2, I32: 4, U16: 2}[dtype]
        self.off = -(-self.off // align) * align
        h = self.nc.alloc_sbuf_tensor_at(name, list(shape), dtype, offset=self.off)
        self.off += shape[-1] * esz
        assert self.off <= self.top, f"SBUF overflow at {name}: {self.off}"
        return h


class Sem:
    __slots__ = ("h", "n")

    def __init__(self, h):
        self.h, self.n = h, 0

    def inc(self, by=1):
        self.n += by
        return self.n


def _pre(e, pre):
    for s, v in pre:
        if v > 0:
            e.wait_ge(s.h, v)


class B:
    ENGS = ("tensor", "vector", "scalar", "sync", "gpsimd")

    def __init__(self, nc):
        self.nc = nc
        self.ops = {e: [] for e in self.ENGS}
        self._si = 0
        self.rdy = [self.sem() for _ in range(8)]
        self.fre = [self.sem() for _ in range(8)]
        self.uses = [0] * 8
        self.v_ms = self.sem()
        self.a_ms = self.sem()
        self.g_ms = self.sem()
        self.t_ms = self.sem()
        self.bar = self.sem()
        self.qstreams = {"sync": [], "scalar": [], "gpsimd": []}

    def stream(self, queue):
        s = self.sem()
        self.qstreams[queue].append(s)
        return s

    def sem(self):
        h = self.nc.semaphore(f"sm{self._si}").__enter__()
        self._si += 1
        return Sem(h)

    def emit(self, eng, fn):
        self.ops[eng].append(fn)

    def wait(self, eng, sem, val):
        if val > 0:
            self.ops[eng].append(lambda e, s=sem, v=val: e.wait_ge(s.h, v))

    def group(self, bnk, mms, pre=()):
        wf = self.uses[bnk]
        self.uses[bnk] += 1
        self.rdy[bnk].n += 1
        rv = self.rdy[bnk].n

        def run(t, mms=mms, bnk=bnk, wf=wf, pre=tuple(pre)):
            _pre(t, pre)
            if wf > 0:
                t.wait_ge(self.fre[bnk].h, wf)
            n = len(mms)
            for i, m in enumerate(mms):
                inst = m(t, i == 0, i == n - 1)
                if i == n - 1:
                    inst.then_inc(self.rdy[bnk].h, 1)
        self.emit("tensor", run)
        return rv

    def consume(self, eng, bnk, rv, fn, pre=()):
        self.fre[bnk].n += 1

        def run(e, bnk=bnk, rv=rv, fn=fn, pre=tuple(pre)):
            _pre(e, pre)
            e.wait_ge(self.rdy[bnk].h, rv)
            fn(e).then_inc(self.fre[bnk].h, 1)
        self.emit(eng, run)
        return self.fre[bnk].n

    def dma(self, queue, out, in_, sem, pre=(), **kw):
        v = sem.inc(16)

        def run(e, out=out, in_=in_, kw=kw, sem=sem, pre=tuple(pre)):
            _pre(e, pre)
            e.dma_start(out=out, in_=in_, **kw).then_inc(sem.h, 16)
        self.emit(queue, run)
        return v

    def vop(self, fn, pre=()):
        self.v_ms.n += 1

        def run(v, fn=fn, pre=tuple(pre)):
            _pre(v, pre)
            fn(v).then_inc(self.v_ms.h, 1)
        self.emit("vector", run)
        return self.v_ms.n

    def gop(self, fn, pre=()):
        self.g_ms.n += 1

        def run(g, fn=fn, pre=tuple(pre)):
            _pre(g, pre)
            fn(g).then_inc(self.g_ms.h, 1)
        self.emit("gpsimd", run)
        return self.g_ms.n

    def aop(self, fn, pre=()):
        self.a_ms.n += 1

        def run(a, fn=fn, pre=tuple(pre)):
            _pre(a, pre)
            fn(a).then_inc(self.a_ms.h, 1)
        self.emit("scalar", run)
        return self.a_ms.n

    def tmark(self):
        self.t_ms.n += 1
        self.emit("tensor", lambda t: t.nop().then_inc(self.t_ms.h, 1))
        return self.t_ms.n

    def barrier(self):
        tgt = self.bar.n + 5
        self.bar.n = tgt
        snap = {q: [(s, s.n) for s in ss if s.n > 0]
                for q, ss in self.qstreams.items()}
        for eng in self.ENGS:
            def run(e, eng=eng, snap=snap):
                for s, v in snap.get(eng, ()):
                    e.wait_ge(s.h, v)
                if eng in ("vector", "gpsimd"):
                    e.engine_nop().then_inc(self.bar.h, 1)
                else:
                    e.nop().then_inc(self.bar.h, 1)
            self.emit(eng, run)
        for eng in self.ENGS:
            self.wait(eng, self.bar, tgt)


def build_program(nz_of_t, n_mask, upto="full", dbg=()):
    K_UPTO_GATE = [False]
    nc = bass.Bass("TRN2", target_bir_lowering=False, debug=False)
    nc.detect_race_conditions = False

    dx = nc.dram_tensor("x", [T, D], F32, kind="ExternalInput").ap()
    dw1 = nc.dram_tensor("w1", [D, D], F32, kind="ExternalInput").ap()
    dw2 = nc.dram_tensor("w2", [D, D], F32, kind="ExternalInput").ap()
    db2 = nc.dram_tensor("b2", [D], F32, kind="ExternalInput").ap()
    dmask = nc.dram_tensor("mask_nz", [max(n_mask, 1) * 128, 128], F32,
                           kind="ExternalInput").ap()
    dqw_gu = nc.dram_tensor("qw_gu", [D, NG // 8], I32, kind="ExternalInput").ap()
    dqz_gu = nc.dram_tensor("qz_gu", [KT, NG // 8], I32, kind="ExternalInput").ap()
    ds_gu = nc.dram_tensor("s_gu", [KT, NG], F32, kind="ExternalInput").ap()
    dqw_dn = nc.dram_tensor("qw_dn", [INTER, D // 8], I32, kind="ExternalInput").ap()
    dqz_dn = nc.dram_tensor("qz_dn", [MT, D // 8], I32, kind="ExternalInput").ap()
    ds_dn = nc.dram_tensor("s_dn", [MT, D], F32, kind="ExternalInput").ap()
    didf = nc.dram_tensor("identf", [128, 128], F32, kind="ExternalInput").ap()
    dones = nc.dram_tensor("ones_col", [128, 1], BF16, kind="ExternalInput").ap()
    dones1b = nc.dram_tensor("ones_row_b", [1, 128], BF16, kind="ExternalInput").ap()
    dones1f = nc.dram_tensor("ones_row_f", [1, 128], F32, kind="ExternalInput").ap()
    dind_gu = nc.dram_tensor("ind_gu", [128, KT * KT], BF16, kind="ExternalInput").ap()
    dind_dn = nc.dram_tensor("ind_dn", [128, NJ * MT], BF16, kind="ExternalInput").ap()
    dstair2 = nc.dram_tensor("stair2", [2, 128], BF16, kind="ExternalInput").ap()
    dout = nc.dram_tensor("out", [T, D], F32, kind="ExternalOutput").ap()

    dxT = [nc.dram_tensor(f"xT{i}", [D, T], F32, kind="Internal").ap()
           for i in range(4)]
    dxn = nc.dram_tensor("xn_sc", [D, T], BF16, kind="Internal").ap()
    dguc = nc.dram_tensor("gu_cache", [2 * NJ * 128, KT * SW], BF16, kind="Internal").ap()
    ddnc = nc.dram_tensor("dn_cache", [MT * 8 * 128, 512], BF16, kind="Internal").ap()
    dh = nc.dram_tensor("h_sc", [MT * 4 * 128, 512], BF16, kind="Internal").ap()
    dzs_gu = nc.dram_tensor("zs_gu_sc", [KT, NG], BF16, kind="Internal").ap()
    dzs_dn = nc.dram_tensor("zs_dn_sc", [MT, D], BF16, kind="Internal").ap()
    dw2bf = nc.dram_tensor("w2bf", [D, D], BF16, kind="Internal").ap()
    dw1c = nc.dram_tensor("w1c", [32 * 128, KT * 128], BF16, kind="Internal").ap()

    pb = [nc.psum_tensor(f"pbank{i}", [128, 512], F32).__enter__() for i in range(8)]
    b = B(nc)

    al = Alloc(nc, SBUF_BASE)
    sb_idf = al.take("idf", [128, 128], F32)
    sb_ones = al.take("onesc", [128, 1], BF16)
    sb_ones1b = al.take("ones1b", [1, 128], BF16)
    sb_ones1f = al.take("ones1f", [1, 128], F32)
    sb_indgu = al.take("indgu", [128, KT * KT], BF16)
    sb_stair2 = al.take("stair2", [2, 128], BF16)
    sb_b2 = al.take("b2t", [128, 32], F32)
    sb_idfb = al.take("idfb", [128, 128], BF16)
    PERSIST = al.off

    s_const = b.stream("sync")
    s_w2bf = b.stream("gpsimd")
    for out, in_ in ((sb_idf[:], didf), (sb_ones[:], dones),
                     (sb_ones1b[:], dones1b), (sb_ones1f[:], dones1f),
                     (sb_indgu[:], dind_gu), (sb_stair2[:], dstair2)):
        b.dma("sync", out, in_, s_const)
    b.dma("sync", sb_b2[:], db2.rearrange("(j p) -> p j", p=128), s_const,
          allow_slow_non_contiguous=True)
    for r in range(4):
        b.dma("gpsimd",
              dw2bf[1024 * r:1024 * (r + 1), :].rearrange("(a p) c -> p a c", p=128),
              dw2[1024 * r:1024 * (r + 1), :].rearrange("(a p) c -> p a c", p=128),
              s_w2bf)
    w2bf_done = s_w2bf.n
    w1cv = dw1c.rearrange("(st p) (dt c) -> st p dt c", p=128, c=128)
    for st in range(32):
        b.dma("gpsimd", w1cv[st, :, :, :],
              dw1[:, 128 * st:128 * (st + 1)].rearrange("(dt p) c -> p dt c", p=128),
              s_w2bf)
    w1c_done = s_w2bf.n
    b.vop(lambda v: v.tensor_copy(sb_idfb[:], sb_idf[:]),
          pre=[(s_const, s_const.n)])
    for eng in ("tensor", "vector", "scalar"):
        b.wait(eng, s_const, s_const.n)

    def emit_tap(dsrc_bytes_ap, nrows, rowlen_f32):
        lo = Alloc(nc, PERSIST)
        tbuf = lo.take("tapbuf", [128, 2048], F32)
        s_tl = b.stream("sync")
        s_tw = b.stream("gpsimd")
        dflat = dout.rearrange("a c -> (a c)")
        for i in range(0, nrows, 128):
            n = min(128, nrows - i)
            wv = b.dma("sync", tbuf[:n, :rowlen_f32], dsrc_bytes_ap[i:i + n, :],
                       s_tl, pre=[(s_tw, s_tw.n)])
            vv = b.vop(lambda v: v.engine_nop(), pre=[(s_tl, wv)])
            seg = dflat[i * rowlen_f32:(i + n) * rowlen_f32]
            b.dma("gpsimd", seg.rearrange("(p f) -> p f", f=rowlen_f32),
                  tbuf[:n, :rowlen_f32], s_tw, pre=[(b.v_ms, vv)])

    def finish():
        b.barrier()
        with nc.Block() as blk:
            for eng in B.ENGS:
                getattr(blk, eng)(lambda e, eng=eng: [f(e) for f in b.ops[eng]])
        return nc

    lnal = Alloc(nc, SBUF_TOP - 66000)
    ln_xb = lnal.take("lnxb", [128, KT * 512], BF16)
    ln_xq = [lnal.take(f"lnxq{i}", [128, 8 * 512], BF16) for i in range(2)]
    ln_r1 = lnal.take("lnr1", [1, 512], F32)
    ln_r2 = lnal.take("lnr2", [1, 512], F32)
    ln_rt = lnal.take("lnrt", [1, 512], F32)
    ln_rb = lnal.take("lnrb", [128, 512], F32)
    ln_mb = lnal.take("lnmb", [128, 512], F32)
    ln_tmp = lnal.take("lntmp", [128, 512], F32)
    ln_xn = [lnal.take(f"lnxn{i}", [128, 512], BF16) for i in range(4)]
    s_lnl = b.stream("gpsimd")
    s_lw = [b.stream("gpsimd") for _ in range(4)]
    ln_pace = b.sem()
    ln_st = {"xbf": 0, "sq0": 0, "sq1": 0, "xn": [0] * 4}

    def ln_load(dsrc, tc, seg, pre):
        p = list(pre)
        if seg is None or seg == 0:
            p.append((b.v_ms, ln_st["xbf"]))
        if seg is None:
            b.dma("gpsimd",
                  ln_xb[:].rearrange("p (dt c) -> p dt c", c=512),
                  dsrc[:, 512 * tc:512 * (tc + 1)]
                  .rearrange("(dt p) c -> p dt c", p=128),
                  s_lnl, pre=p)
        else:
            b.dma("gpsimd",
                  ln_xb[:, 2048 * seg:2048 * (seg + 1)]
                  .rearrange("p (dt c) -> p dt c", c=512),
                  dsrc[512 * seg:512 * (seg + 1), 512 * tc:512 * (tc + 1)]
                  .rearrange("(dt p) c -> p dt c", p=128),
                  s_lnl, pre=p)

    def ln_compute(tc, ld_pre=None):
        if ld_pre is None:
            ld_pre = [(s_lnl, s_lnl.n)]
        p0p = ln_pace.n
        sqv = []
        for c in range(4):
            if c == 0:
                pre = list(ld_pre)
                if ln_st["sq0"]:
                    pre.append((ln_pace, ln_st["sq0"]))
            elif c == 1:
                pre = [(b.rdy[7], ln_st["sq1"])] if ln_st["sq1"] else []
            elif c == 2:
                pre = [(ln_pace, p0p + 1)]
            else:
                pre = [(ln_pace, p0p + 2)]
            sqv.append(b.aop(lambda a, c=c: a.activation(
                ln_xq[c % 2][:], ln_xb[:, 4096 * c:4096 * (c + 1)], AF.Square),
                pre=pre))
        rv6 = b.group(6, [
            (lambda t, s, e, dt=dt: t.matmul(
                pb[6][:1, :], sb_ones[:], ln_xb[:, 512 * dt:512 * (dt + 1)],
                start=s, stop=e)) for dt in range(KT)],
            pre=list(ld_pre))
        wf7 = b.uses[7]
        b.uses[7] += 1
        b.rdy[7].n += 1
        rv7 = b.rdy[7].n

        def sq_run(t, sqv=tuple(sqv), wf7=wf7):
            if wf7 > 0:
                t.wait_ge(b.fre[7].h, wf7)
            for c in range(4):
                t.wait_ge(b.a_ms.h, sqv[c])
                for i in range(8):
                    dt = 8 * c + i
                    inst = t.matmul(pb[7][:1, :], sb_ones[:],
                                    ln_xq[c % 2][:, 512 * i:512 * (i + 1)],
                                    start=(dt == 0), stop=(dt == KT - 1))
                    if dt == KT - 1:
                        inst.then_inc(b.rdy[7].h, 1)
                    elif i == 7:
                        inst.then_inc(ln_pace.h, 1)
        b.emit("tensor", sq_run)
        ln_pace.n += 3
        ln_st["sq0"] = p0p + 3
        ln_st["sq1"] = rv7
        b.consume("vector", 6, rv6,
                  lambda v: v.tensor_scalar_mul(ln_r1[:], pb[6][:1, :], 1.0 / D))
        b.consume("vector", 7, rv7,
                  lambda v: v.tensor_scalar_mul(ln_r2[:], pb[7][:1, :], 1.0 / D))

        def vstat(v):
            v.tensor_tensor(ln_rt[:], ln_r1[:], ln_r1[:], ALU.mult)
            v.tensor_tensor(ln_r2[:], ln_r2[:], ln_rt[:], ALU.subtract)
            return v.tensor_scalar_add(ln_r2[:], ln_r2[:], EPS)
        vv = b.vop(vstat)
        av = b.aop(lambda a: a.activation(ln_r2[:], ln_r2[:], AF.Sqrt),
                   pre=[(b.v_ms, vv)])

        def vrec(v):
            v.reciprocal(ln_r2[:], ln_r2[:])
            return v.tensor_tensor(ln_r1[:], ln_r1[:], ln_r2[:], ALU.mult)
        vv = b.vop(vrec, pre=[(b.a_ms, av)])
        for bnk, srcr, dst in ((6, ln_r2, ln_rb), (7, ln_r1, ln_mb)):
            rvb = b.group(bnk, [
                lambda t, s, e, bnk=bnk, srcr=srcr: t.matmul(
                    pb[bnk][:], sb_ones1f[:], srcr[:], start=s, stop=e)],
                pre=[(b.v_ms, vv)])
            b.consume("vector", bnk, rvb,
                      lambda v, dst=dst, bnk=bnk: v.tensor_copy(dst[:], pb[bnk][:]))
        lastv = 0
        for dt in range(KT):
            sl = dt % 4

            def vnorm(v, dt=dt, sl=sl):
                v.tensor_tensor(ln_tmp[:], ln_xb[:, 512 * dt:512 * (dt + 1)],
                                ln_rb[:], ALU.mult)
                return v.tensor_tensor(ln_xn[sl][:], ln_tmp[:], ln_mb[:],
                                       ALU.subtract)
            pre = [(s_lw[sl], ln_st["xn"][sl])] if ln_st["xn"][sl] else []
            lastv = b.vop(vnorm, pre=pre)
            b.dma("gpsimd",
                  dxn[128 * dt:128 * (dt + 1), 512 * tc:512 * (tc + 1)],
                  ln_xn[sl][:], s_lw[sl], pre=[(b.v_ms, lastv)])
            ln_st["xn"][sl] = s_lw[sl].n
        ln_st["xbf"] = lastv

    p0 = Alloc(nc, PERSIST)
    sb_xh = [p0.take(f"p0xh{i}", [128, D], F32) for i in range(4)]
    sb_xts = [p0.take(f"p0xts{i}", [128, 512], F32) for i in range(2)]
    sb_qzg = p0.take("p0qzg", [KT, NG // 8], I32)
    sb_zu = p0.take("p0zu", [KT, 4096], U16)
    sb_zb = p0.take("p0zb", [KT, 4096], BF16)
    sb_qzd = p0.take("p0qzd", [MT, D // 8], I32)
    sb_zud = p0.take("p0zud", [MT, D], U16)
    sb_zbd = p0.take("p0zbd", [MT, D], BF16)

    s_px = b.stream("sync")
    s_pw = [b.stream("gpsimd") for _ in range(2)]
    xts_guard = [0, 0]
    xh_guard = [0]
    for tg in range(4):
        wv = [b.dma("sync", sb_xh[i][:],
                    dx[(4 * tg + i) * 128:(4 * tg + i + 1) * 128, :], s_px,
                    pre=([(b.t_ms, xh_guard[0])] if i == 0 else ()))
              for i in range(4)]
        for dt in range(KT):
            bnk = dt % 2
            wf = b.uses[bnk]
            b.uses[bnk] += 1
            b.rdy[bnk].n += 1
            rv = b.rdy[bnk].n

            def trun(t, dt=dt, bnk=bnk, wf=wf, w=wv[3]):
                t.wait_ge(s_px.h, w)
                if wf > 0:
                    t.wait_ge(b.fre[bnk].h, wf)
                for i in range(4):
                    inst = t.transpose(pb[bnk][:, 128 * i:128 * (i + 1)],
                                       sb_xh[i][:, 128 * dt:128 * (dt + 1)],
                                       sb_idf[:])
                    if i == 3:
                        inst.then_inc(b.rdy[bnk].h, 1)
            b.emit("tensor", trun)
            st = sb_xts[dt % 2]
            fv = b.consume("vector", bnk, rv,
                           lambda v, st=st, bnk=bnk: v.tensor_copy(st[:], pb[bnk][:]),
                           pre=[(s_pw[dt % 2], xts_guard[dt % 2])])
            b.dma("gpsimd", dxT[0][128 * dt:128 * (dt + 1), 512 * tg:512 * (tg + 1)],
                  st[:], s_pw[dt % 2], pre=[(b.fre[bnk], fv)])
            xts_guard[dt % 2] = s_pw[dt % 2].n
        xh_guard[0] = b.tmark()
        ln_load(dxT[0], tg, None, [(s_pw[0], s_pw[0].n), (s_pw[1], s_pw[1].n)])
        ln_compute(tg)

    s_zs = b.stream("sync")
    wq = b.dma("sync", sb_qzg[:], dqz_gu, s_zs)
    zb_guard = 0
    for ch in range(6):
        c0 = 4096 * ch
        w = min(4096, NG - c0)

        def zrun(v, c0=c0, w=w):
            qz = sb_qzg[:].bitcast(U16)
            for o in range(0, w, SW):
                u0 = ((c0 + o) // SW) * 64
                for i in range(4):
                    v.tensor_scalar(sb_zu[:, o + 64 * i:o + 64 * (i + 1)],
                                    qz[:, u0:u0 + 64], 4 * i, 15,
                                    ALU.logical_shift_right, ALU.bitwise_and)
            return v.tensor_copy(sb_zb[:, :w], sb_zu[:, :w])
        vv = b.vop(zrun, pre=[(s_zs, max(wq, zb_guard))])
        b.dma("sync", dzs_gu[:, c0:c0 + w], sb_zb[:, :w], s_zs, pre=[(b.v_ms, vv)])
        zb_guard = s_zs.n

    wq = b.dma("sync", sb_qzd[:], dqz_dn, s_zs)

    def zdrun(v):
        qz = sb_qzd[:].bitcast(U16)
        for o in range(0, D, SW):
            u0 = (o // SW) * 64
            for i in range(4):
                v.tensor_scalar(sb_zud[:, o + 64 * i:o + 64 * (i + 1)],
                                qz[:, u0:u0 + 64], 4 * i, 15,
                                ALU.logical_shift_right, ALU.bitwise_and)
        return v.tensor_copy(sb_zbd[:], sb_zud[:])
    vv = b.vop(zdrun, pre=[(s_zs, wq)])
    b.dma("sync", dzs_dn[:], sb_zbd[:], s_zs, pre=[(b.v_ms, vv)])

    b.barrier()

    if upto == "p0":
        emit_tap(dxT[0], D, T)
        return finish()
    if upto == "ln1":
        emit_tap(dxn.bitcast(F32), D, T // 2)
        return finish()

    p1 = Alloc(nc, PERSIST)
    sb_qw = [p1.take(f"dqqw{i}", [128, NG // 8], I32) for i in range(2)]
    sb_srow = [p1.take(f"dqsr{i}", [2, 8192], BF16) for i in range(2)]
    sb_sbc = [p1.take(f"dqsbc{i}", [128, 1024], BF16) for i in range(8)]
    sb_nib = [p1.take(f"dqnib{i}", [128, 512], U16) for i in range(2)]
    sb_nbb = p1.take("dqnbb", [128, 512], BF16)
    sb_wst = [p1.take(f"dqwst{i}", [128, 2048], BF16) for i in range(2)]

    s_qw = [b.stream("sync") for _ in range(2)]
    s_sr = [b.stream("gpsimd") for _ in range(2)]
    s_wc = [b.stream("sync") for _ in range(2)]
    qw_guard = [0, 0]
    srow_guard = [(0, 0), (0, 0)]
    sbc_guard = [0] * 8
    nib_guard = [0, 0]
    wst_guard = [0, 0]
    subctr = [0]

    def emit_dequant(dqw, dcache, n_i32, n_pos, dsrows, dzrows, n_tiles, dn_mode):
        qsel = (dqw.rearrange("(j r s) c -> j s r c", r=64, s=4) if dn_mode else None)
        for g in range(n_tiles):
            qs = sb_qw[g % 2]
            if dn_mode:
                j, q = g // 2, g % 2
                wqv = b.dma("sync", qs[:, :n_i32],
                            qsel[j, 2 * q:2 * q + 2, :, :],
                            s_qw[g % 2], pre=[(b.v_ms, qw_guard[g % 2])])
            else:
                wqv = b.dma("sync", qs[:, :n_i32], dqw[128 * g:128 * (g + 1), :],
                            s_qw[g % 2], pre=[(b.v_ms, qw_guard[g % 2])])
            for ch in range(0, n_pos, 2048):
                w = min(2048, n_pos - ch)
                chi = (ch // 2048) % 2
                sr = sb_srow[chi]
                gb, gv = srow_guard[chi]
                nrow = 2 if dn_mode else 1
                g0 = (g // 2) * 2 if dn_mode else g
                srv = sr[:nrow, :].rearrange("a (u two c) -> a u two c", two=2, c=512)
                wsv = b.dma("gpsimd", srv[:, :w // 512, 0, :],
                            dsrows[g0:g0 + nrow, ch:ch + w]
                            .rearrange("a (u c) -> a u c", c=512),
                            s_sr[chi], pre=[(b.rdy[gb], gv)])
                wzv = b.dma("gpsimd", srv[:, :w // 512, 1, :],
                            dzrows[g0:g0 + nrow, ch:ch + w]
                            .rearrange("a (u c) -> a u c", c=512),
                            s_sr[chi])
                wstg = sb_wst[chi]
                last_rv = (6, 0)
                for sub in range(0, w, 512):
                    si = subctr[0] % 8
                    bnk = (2 * subctr[0]) % 8
                    bnkz = bnk + 1
                    subctr[0] += 1
                    sbc = sb_sbc[si]
                    lhsb = sb_stair2[:] if dn_mode else sb_ones1b[:]
                    rv = b.group(bnk, [
                        (lambda t, s, e, sr=sr, sub=sub, bnk=bnk, lhsb=lhsb,
                         nrow=nrow:
                         t.matmul(pb[bnk][:], lhsb,
                                  sr[:nrow, 2 * sub:2 * sub + 512],
                                  start=s, stop=e))],
                        pre=[(s_sr[chi], wzv)])
                    rvz = b.group(bnkz, [
                        (lambda t, s, e, sr=sr, sub=sub, bnkz=bnkz, lhsb=lhsb,
                         nrow=nrow:
                         t.matmul(pb[bnkz][:], lhsb,
                                  sr[:nrow, 2 * sub + 512:2 * sub + 1024],
                                  start=s, stop=e))])
                    last_rv = (bnkz, rvz)
                    fv = b.consume("scalar", bnk, rv,
                                   lambda a, sbc=sbc, bnk=bnk:
                                   a.activation(sbc[:, :512], pb[bnk][:],
                                                AF.Identity),
                                   pre=[(b.v_ms, sbc_guard[si])])
                    fvz = b.consume("scalar", bnkz, rvz,
                                    lambda a, sbc=sbc, bnkz=bnkz:
                                    a.activation(sbc[:, 512:], pb[bnkz][:],
                                                 AF.Identity))

                    nslot = subctr[0] % 2
                    nib = sb_nib[nslot]

                    def drun(v, qs=qs, sbc=sbc, pos=ch + sub, sub=sub,
                             nib=nib, n_i32=n_i32, wstg=wstg):
                        qu = qs[:, :n_i32].bitcast(U16)
                        for jj in range(2):
                            u0 = ((pos + SW * jj) // SW) * 64
                            for i in range(4):
                                v.tensor_scalar(
                                    nib[:, SW * jj + 64 * i:SW * jj + 64 * (i + 1)],
                                    qu[:, u0:u0 + 64], 4 * i, 15,
                                    ALU.logical_shift_right, ALU.bitwise_and)
                        zbv = sbc[:].rearrange("p (u two c) -> p u two c",
                                               two=2, c=512)
                        v.tensor_tensor(sb_nbb[:], nib[:], zbv[:, 0, 1, :],
                                        ALU.subtract)
                        return v.tensor_tensor(wstg[:, sub:sub + 512],
                                               sb_nbb[:], zbv[:, 0, 0, :], ALU.mult)
                    vv = b.vop(drun, pre=[(s_qw[g % 2], wqv),
                                          (s_wc[chi], wst_guard[chi]),
                                          (b.fre[bnk], fv), (b.fre[bnkz], fvz)])
                    sbc_guard[si] = vv
                srow_guard[chi] = last_rv
                vvw = b.v_ms.n
                if not dn_mode:
                    g4 = dcache.rearrange("(s p) (kt c) -> s p kt c", p=128, kt=KT)
                    for o in range(0, w, SW):
                        s_idx = (ch + o) // SW
                        b.dma("sync", g4[s_idx, :, g, :],
                              wstg[:, o:o + SW], s_wc[chi],
                              pre=([(b.v_ms, vvw)] if o == 0 else ()))
                else:
                    d4 = dcache.rearrange("(mt oc p) c -> mt oc p c", oc=8, p=128)
                    for o in range(0, w, 512):
                        b.dma("sync", d4[g, (ch + o) // 512, :, :],
                              wstg[:, o:o + 512], s_wc[chi],
                              pre=([(b.v_ms, vvw)] if o == 0 else ()))
                wst_guard[chi] = s_wc[chi].n
            qw_guard[g % 2] = b.v_ms.n

    emit_dequant(dqw_gu, dguc, NG // 8, NG, ds_gu, dzs_gu, KT, False)
    emit_dequant(dqw_dn, ddnc, D // 8, D, ds_dn, dzs_dn, MT, True)

    b.barrier()

    s_gx = b.stream("sync")
    s_rg = [b.stream("sync") for _ in range(4)]
    s_hw = [b.stream("gpsimd") for _ in range(4)]
    s_hh = b.stream("sync")
    s_wd = [b.stream("scalar") for _ in range(8)]
    s_xr = b.stream("gpsimd")
    s_os = [b.stream("gpsimd") for _ in range(8)]

    def emit_ffn(dsrc, ddst, ln_dst=False):
        lo = Alloc(nc, PERSIST)
        xnh = lo.take("gxn", [128, KT * 1024], BF16)
        ring = [lo.take(f"grg{i}", [128, KT * SW], BF16) for i in range(4)]
        silub = [lo.take(f"gsl{i}", [128, 2048], BF16) for i in range(2)]
        hstg = [lo.take(f"ghs{i}", [128, 1024], BF16) for i in range(4)]

        ring_guard = [(0, 0)] * 4
        silu_guard = [(0, 0), (0, 0)]
        silu_fv = {}
        hmul_fv = {}
        hst_guard = [0, 0, 0, 0]
        rot = [0]

        for half in range(2):
            pre0 = ([(b.rdy[k], b.rdy[k].n) for k in range(8)] if half == 1 else [])
            xwv = [b.dma("sync", xnh[:, 1024 * k:1024 * (k + 1)],
                         dxn[128 * k:128 * (k + 1), HALF * half:HALF * (half + 1)],
                         s_gx, pre=(pre0 if k == 0 else ()))
                   for k in range(KT)]

            for jp in range(2 * NJ):
                j, side = jp // 2, jp % 2
                slot = ring[jp % 4]
                pos0 = side * INTER + SW * j
                gb, gv = ring_guard[jp % 4]
                s_idx = pos0 // SW
                wv = b.dma("sync", slot[:],
                           dguc[128 * s_idx:128 * (s_idx + 1), :],
                           s_rg[jp % 4], pre=[(b.rdy[gb], gv)])
                for q in range(2):
                    for tc in range(2):
                        bnk = rot[0]
                        rot[0] = (rot[0] + 1) % 6
                        mms = [(lambda t, s, e, k=k, q=q, tc=tc, slot=slot, bnk=bnk:
                                t.matmul(pb[bnk][:],
                                         slot[:, SW * k + 128 * q:SW * k + 128 * (q + 1)],
                                         xnh[:, 1024 * k + 512 * tc:1024 * k + 512 * (tc + 1)],
                                         start=s, stop=e))
                               for k in range(KT)]
                        rv = b.group(bnk, mms,
                                     pre=([(s_rg[jp % 4], wv), (s_gx, xwv[-1])]
                                          if (q == 0 and tc == 0) else ()))
                        ring_guard[jp % 4] = (bnk, rv)
                        if side == 0:
                            sl = silub[j % 2]
                            gb2, gv2 = silu_guard[j % 2]
                            sfv = b.consume("scalar", bnk, rv,
                                            lambda a, sl=sl, q=q, tc=tc, bnk=bnk:
                                            a.activation(
                                                sl[:, 1024 * q + 512 * tc:1024 * q + 512 * (tc + 1)],
                                                pb[bnk][:], AF.Silu),
                                            pre=([(b.fre[gb2], gv2)]
                                                 if (q == 0 and tc == 0) else ()))
                            silu_fv[(q, tc)] = (bnk, sfv)
                        else:
                            hs = hstg[2 * (j % 2) + q]
                            sl = silub[j % 2]

                            def hmul(v, hs=hs, sl=sl, q=q, tc=tc, bnk=bnk):
                                return v.tensor_tensor(
                                    hs[:, 512 * tc:512 * (tc + 1)], pb[bnk][:],
                                    sl[:, 1024 * q + 512 * tc:1024 * q + 512 * (tc + 1)],
                                    ALU.mult)
                            sb_, sv_ = silu_fv[(q, tc)]
                            fv = b.consume("vector", bnk, rv, hmul,
                                           pre=([(b.fre[sb_], sv_),
                                                 (s_hw[2 * (j % 2) + q],
                                                  hst_guard[2 * (j % 2) + q])]
                                                if tc == 0 else [(b.fre[sb_], sv_)]))
                            hmul_fv[(j % 2, q)] = (bnk, fv)
                            if tc == 1:
                                silu_guard[j % 2] = (bnk, fv)
                if side == 1:
                    dh4 = dh.rearrange("(mt tq p) c -> mt tq p c", tq=4, p=128)
                    for q in range(2):
                        hb, hv = hmul_fv[(j % 2, q)]
                        for tc in range(2):
                            b.dma("gpsimd",
                                  dh4[2 * j + q, 2 * half + tc, :, :],
                                  hstg[2 * (j % 2) + q][:, 512 * tc:512 * (tc + 1)],
                                  s_hw[2 * (j % 2) + q],
                                  pre=([(b.fre[hb], hv)] if tc == 0 else ()))
                        hst_guard[2 * (j % 2) + q] = s_hw[2 * (j % 2) + q].n

        b.barrier()
        if K_UPTO_GATE[0]:
            return

        lo = Alloc(nc, PERSIST)
        hh = lo.take("dnh", [128, MT * 512], BF16)
        wdn = [lo.take(f"dnw{i}", [128, 512], BF16) for i in range(8)]
        xres = [lo.take(f"dnxr{i}", [128, 512], F32) for i in range(8)]
        ost = [lo.take(f"dnos{i}", [128, 512], F32) for i in range(8)]

        wdn_guard = [0] * 8
        xres_guard = [0] * 8
        ost_guard = [0] * 8
        dsel = ddnc.rearrange("(mt oc p) c -> mt oc p c", oc=8, p=128)
        dh4 = dh.rearrange("(mt tq p) c -> mt tq p c", tq=4, p=128)
        xsel = dsrc.rearrange("(c r s) t -> c s r t", r=64, s=4)
        osel = ddst.rearrange("(c r s) t -> c s r t", r=64, s=4)

        for tq in range(4):
            pre0 = [(b.rdy[k], b.rdy[k].n) for k in range(8)]
            hwv = [b.dma("sync", hh[:, 512 * mt:512 * (mt + 1)],
                         dh4[mt, tq, :, :],
                         s_hh, pre=(pre0 if mt == 0 else ()))
                   for mt in range(MT)]

            for og in range(8):
                base = 4 * (og % 2)
                banks = [base + i for i in range(4)]
                for ot in range(4):
                    bnk = banks[ot]
                    jj, q = 2 * og + ot // 2, ot % 2
                    b.dma("gpsimd", xres[bnk][:],
                          xsel[jj, 2 * q:2 * q + 2, :, 512 * tq:512 * (tq + 1)],
                          s_xr, pre=[(b.fre[bnk], xres_guard[bnk])])
                xr_done = s_xr.n
                open_wf = []
                rvs = {}
                for bnk in banks:
                    open_wf.append((bnk, b.uses[bnk]))
                    b.uses[bnk] += 1
                    b.rdy[bnk].n += 1
                    rvs[bnk] = b.rdy[bnk].n
                for mt in range(MT):
                    ws = wdn[mt % 8]
                    wv = b.dma("scalar", ws[:], dsel[mt, og, :, :],
                               s_wd[mt % 8], pre=[(b.t_ms, wdn_guard[mt % 8])])

                    def dmm(t, mt=mt, ws=ws, wv=wv, banks=tuple(banks),
                            open_wf=tuple(open_wf), hv=hwv[-1]):
                        t.wait_ge(s_wd[mt % 8].h, wv)
                        if mt == 0:
                            t.wait_ge(s_hh.h, hv)
                            for bnk, wf in open_wf:
                                if wf > 0:
                                    t.wait_ge(b.fre[bnk].h, wf)
                        for ot in range(4):
                            inst = t.matmul(
                                pb[banks[ot]][:], ws[:, 128 * ot:128 * (ot + 1)],
                                hh[:, 512 * mt:512 * (mt + 1)],
                                start=(mt == 0), stop=(mt == MT - 1))
                            if mt == MT - 1:
                                inst.then_inc(b.rdy[banks[ot]].h, 1)
                        if mt != MT - 1:
                            inst.then_inc(b.t_ms.h, 1)
                        else:
                            t.nop().then_inc(b.t_ms.h, 1)
                    b.t_ms.n += 1
                    b.emit("tensor", dmm)
                    wdn_guard[mt % 8] = b.t_ms.n
                for ot in range(4):
                    bnk = banks[ot]
                    jj, q = 2 * og + ot // 2, ot % 2
                    if "noxres" in dbg:
                        fv = b.consume("vector", bnk, rvs[bnk],
                                       lambda v, bnk=bnk: v.tensor_scalar_mul(
                                           ost[bnk][:], pb[bnk][:], 0.5),
                                       pre=[(s_xr, xr_done),
                                            (s_os[bnk], ost_guard[bnk])])
                    else:
                        fv = b.consume("vector", bnk, rvs[bnk],
                                       lambda v, bnk=bnk: v.scalar_tensor_tensor(
                                           ost[bnk][:], pb[bnk][:], 0.5, xres[bnk][:],
                                           ALU.mult, ALU.add),
                                       pre=[(s_xr, xr_done),
                                            (s_os[bnk], ost_guard[bnk])])
                    xres_guard[bnk] = fv
                    b.dma("gpsimd",
                          osel[jj, 2 * q:2 * q + 2, :, 512 * tq:512 * (tq + 1)],
                          ost[bnk][:], s_os[bnk], pre=[(b.fre[bnk], fv)])
                    ost_guard[bnk] = s_os[bnk].n
                if ln_dst:
                    ln_load(ddst, tq, og,
                            [(s_os[bk], ost_guard[bk]) for bk in banks])
            if ln_dst:
                ln_compute(tq)
        b.barrier()

    def emit_attn(dsrc, ddst):
        lo = Alloc(nc, PERSIST)
        abuf = lo.take("abuf", [128, max(n_mask, 1) * 128], BF16)
        A_END = lo.off
        w1s = [lo.take(f"aw1{i}", [128, KT * 128], BF16) for i in range(4)]
        w2t = lo.take("aw2", [128, KT * 512], BF16)
        w2bs = [lo.take(f"aw2b{i}", [128, 4096], BF16) for i in range(4)]
        amask = [lo.take(f"amk{i}", [128, 128], F32) for i in range(2)]

        nz_of_chunk = []
        for tcb in range(8):
            ss = {}
            for tt in range(4 * tcb, 4 * tcb + 4):
                for (s_t, slot) in nz_of_t[tt]:
                    ss.setdefault(s_t, []).append((tt, slot))
            nz_of_chunk.append(sorted(ss.items()))

        s_w2 = [b.stream("sync") for _ in range(2)]
        s_w1 = [b.stream("sync") for _ in range(4)]
        s_mk = [b.stream("sync") for _ in range(2)]
        w1_guard = [0] * 4
        w2_guard = [0]
        w2b_guard = [0]
        mk_guard = [("v", 0, 0), ("v", 0, 0)]
        arot = [0]
        mki = [0]
        for tcb in range(8):
            w2s = w2t
            wv2b = [b.dma("sync", w2bs[a][:],
                          dw2bf[512 * tcb + 128 * a:512 * tcb + 128 * (a + 1), :],
                          s_w2[a % 2],
                          pre=([(b.t_ms, w2b_guard[0]), (s_w2bf, w2bf_done)]
                               if a == 0 else ()))
                    for a in range(4)]
            ldw = [(s_w2[0], s_w2[0].n), (s_w2[1], s_w2[1].n)]
            evf = {}
            for dt in range(KT):
                bnk = 6 + (dt % 2)
                wf = b.uses[bnk]
                b.uses[bnk] += 1
                b.rdy[bnk].n += 1
                rv = b.rdy[bnk].n

                def w2run(t, dt=dt, bnk=bnk, wf=wf, ldw=tuple(ldw)):
                    if dt == 0:
                        for s_, v_ in ldw:
                            t.wait_ge(s_.h, v_)
                    if wf > 0:
                        t.wait_ge(b.fre[bnk].h, wf)
                    pbb = pb[bnk].bitcast(BF16)
                    for a in range(4):
                        inst = t.transpose(pbb[:, 128 * a:128 * (a + 1)],
                                           w2bs[a][:, 128 * dt:128 * (dt + 1)],
                                           sb_idfb[:])
                        if a == 3:
                            inst.then_inc(b.rdy[bnk].h, 1)
                b.emit("tensor", w2run)
                evf[bnk] = b.consume(
                    "scalar", bnk, rv,
                    lambda a_, dt=dt, bnk=bnk: a_.activation(
                        w2s[:, 512 * dt:512 * (dt + 1)],
                        pb[bnk].bitcast(BF16)[:, :512], AF.Identity),
                    pre=([(b.t_ms, w2_guard[0])] if dt == 0 else ()))
            w2b_guard[0] = b.tmark()
            si = 0
            for (s_t, uses) in nz_of_chunk[tcb]:
                w1b = w1s[si % 4]
                wv1 = b.dma("sync", w1b[:],
                            dw1c[128 * s_t:128 * (s_t + 1), :],
                            s_w1[si % 4],
                            pre=[(b.t_ms, w1_guard[si % 4]),
                                 (s_w2bf, w1c_done)])
                bnk = arot[0]
                arot[0] = (arot[0] + 1) % 6
                mms = [(lambda t, s, e, k=k, w1b=w1b, w2s=w2s, bnk=bnk:
                        t.matmul(pb[bnk][:], w1b[:, 128 * k:128 * (k + 1)],
                                 w2s[:, 512 * k:512 * (k + 1)],
                                 start=s, stop=e)) for k in range(KT)]
                rv = b.group(bnk, mms,
                             pre=[(s_w1[si % 4], wv1)] +
                                 ([(b.fre[6], evf[6]), (b.fre[7], evf[7])]
                                  if si == 0 else []))
                tm = b.tmark()
                w1_guard[si % 4] = tm
                w2_guard[0] = tm
                nuses = len(uses)
                for ui, (tt, slot) in enumerate(uses):
                    mi = mki[0] % 2
                    mk = amask[mi]
                    kind, gb3, gv3 = mk_guard[mi]
                    wmk = b.dma("sync", mk[:], dmask[128 * slot:128 * (slot + 1), :],
                                s_mk[mi],
                                pre=[(b.v_ms, gv3) if kind == "v"
                                     else (b.fre[gb3], gv3)])
                    mki[0] += 1
                    qq = tt - 4 * tcb

                    def amul(v, mk=mk, qq=qq, slot=slot, bnk=bnk):
                        return v.tensor_tensor(
                            abuf[:, 128 * slot:128 * (slot + 1)],
                            pb[bnk][:, 128 * qq:128 * (qq + 1)], mk[:], ALU.mult)
                    if ui == nuses - 1:
                        fv = b.consume("vector", bnk, rv, amul,
                                       pre=[(s_mk[mi], wmk)])
                        mk_guard[mi] = ("f", bnk, fv)
                    else:
                        vv = b.vop(amul, pre=[(s_mk[mi], wmk),
                                              (b.rdy[bnk], rv)])
                        mk_guard[mi] = ("v", 0, vv)
                si += 1
        b.barrier()

        lo = Alloc(nc, A_END)
        xn2 = [lo.take(f"yxn{i}", [128, KT * 512], BF16) for i in range(2)]
        gsb = [lo.take(f"ygs{i}", [128, 512], F32) for i in range(4)]
        x1t = [lo.take(f"yx1{i}", [128, 512], F32) for i in range(4)]
        x2t = [lo.take(f"yx2{i}", [128, 512], F32) for i in range(4)]

        s_x2l = b.stream("sync")
        s_x1l = [b.stream("sync") for _ in range(4)]
        s_x2w = [b.stream("gpsimd") for _ in range(4)]
        yrot = [0]
        gs_guard = [0] * 4
        x1_guard = [0] * 4
        x2_guard = [0] * 4
        xc_guard = [0] * 4
        xn2_guard = [0, 0]
        for tk in range(4):
            sl2 = tk % 2
            xw = b.dma("sync", xn2[sl2][:].rearrange("p (k c) -> p k c", c=512),
                       dxn[:, 512 * tk:512 * (tk + 1)]
                       .rearrange("(k p) c -> p k c", p=128),
                       s_x2l, pre=[(b.t_ms, xn2_guard[sl2])])
            for J in range(32):
                pairs = nz_of_t[J]
                assert pairs, "empty attn row-tile unsupported"
                bnk = yrot[0]
                yrot[0] = (yrot[0] + 1) % 6
                mms = [(lambda t, s, e, st=st, slot=slot, bnk=bnk, sl2=sl2:
                        t.matmul(pb[bnk][:],
                                 abuf[:, 128 * slot:128 * (slot + 1)],
                                 xn2[sl2][:, 512 * st:512 * (st + 1)],
                                 start=s, stop=e))
                       for (st, slot) in pairs]
                rv = b.group(bnk, mms, pre=([(s_x2l, xw)] if J == 0 else []))
                gi = J % 4
                gs = gsb[gi]
                fvg = b.consume("scalar", bnk, rv,
                                lambda a, gs=gs, J=J, bnk=bnk: a.activation(
                                    gs[:], pb[bnk][:], AF.Gelu,
                                    bias=sb_b2[:, J:J + 1]),
                                pre=[(b.v_ms, gs_guard[gi])])
                wx1 = b.dma("sync", x1t[gi][:],
                            dsrc[128 * J:128 * (J + 1), 512 * tk:512 * (tk + 1)],
                            s_x1l[gi], pre=[(b.v_ms, x1_guard[gi])])

                def yadd(v, gs=gs, gi=gi):
                    return v.tensor_tensor(x2t[gi][:], x1t[gi][:], gs[:], ALU.add)
                vv = b.vop(yadd, pre=[(b.fre[bnk], fvg), (s_x1l[gi], wx1),
                                      (s_x2w[gi], x2_guard[gi]),
                                      (b.a_ms, xc_guard[gi])])
                gs_guard[gi] = vv
                x1_guard[gi] = vv
                b.dma("gpsimd",
                      ddst[128 * J:128 * (J + 1), 512 * tk:512 * (tk + 1)],
                      x2t[gi][:], s_x2w[gi], pre=[(b.v_ms, vv)])
                x2_guard[gi] = s_x2w[gi].n
                cast_last = b.aop(lambda a_, J=J, gi=gi: a_.activation(
                    ln_xb[:, 512 * J:512 * (J + 1)], x2t[gi][:], AF.Identity),
                    pre=[(b.v_ms, vv)])
                xc_guard[gi] = cast_last
            xn2_guard[sl2] = b.tmark()
            ln_compute(tk, ld_pre=[(b.a_ms, cast_last)])
        b.barrier()

    def emit_final():
        lo = Alloc(nc, PERSIST)
        fst = [lo.take(f"ffs{i}", [128, KT * 128], F32) for i in range(2)]
        ostg = [lo.take(f"fos{i}", [128, D], F32) for i in range(2)]
        s_fl = [b.stream("sync") for _ in range(2)]
        s_fo = [b.stream("gpsimd") for _ in range(2)]
        f_guard = [0, 0]
        o_guard = [0, 0]
        for tt in range(16):
            fs = fst[tt % 2]
            wv = b.dma("sync", fs[:].rearrange("p (dt c) -> p dt c", dt=KT),
                       dxT[3][:, 128 * tt:128 * (tt + 1)]
                       .rearrange("(dt p) c -> p dt c", p=128),
                       s_fl[tt % 2], pre=[(b.t_ms, f_guard[tt % 2])])
            og = ostg[tt % 2]
            fvs = {}
            for bg in range(8):
                bnk = bg % 2 + 2
                wf = b.uses[bnk]
                b.uses[bnk] += 1
                b.rdy[bnk].n += 1
                rv = b.rdy[bnk].n

                def trun(t, bg=bg, bnk=bnk, wf=wf, wv=wv, fs=fs, tt=tt):
                    t.wait_ge(s_fl[tt % 2].h, wv)
                    if wf > 0:
                        t.wait_ge(b.fre[bnk].h, wf)
                    for i in range(4):
                        inst = t.transpose(
                            pb[bnk][:, 128 * i:128 * (i + 1)],
                            fs[:, 128 * (4 * bg + i):128 * (4 * bg + i + 1)],
                            sb_idf[:])
                        if i == 3:
                            inst.then_inc(b.rdy[bnk].h, 1)
                b.emit("tensor", trun)
                fvs[bnk] = b.consume(
                    "vector", bnk, rv,
                    lambda v, og=og, bg=bg, bnk=bnk:
                    v.tensor_copy(og[:, 512 * bg:512 * (bg + 1)], pb[bnk][:]),
                    pre=([(s_fo[tt % 2], o_guard[tt % 2])] if bg == 0 else ()))
            f_guard[tt % 2] = b.tmark()
            b.dma("gpsimd", dout[128 * tt:128 * (tt + 1), :], og[:],
                  s_fo[tt % 2], pre=[(b.fre[2], fvs[2]), (b.fre[3], fvs[3])])
            o_guard[tt % 2] = s_fo[tt % 2].n

    if upto == "p1":
        emit_tap(ddnc.bitcast(F32), 2048, 2048)
        return finish()
    if upto == "gate":
        K_UPTO_GATE[0] = True
    emit_ffn(dxT[0], dxT[1], ln_dst=True)
    if upto == "gate":
        emit_tap(dh.bitcast(F32), 4096, 1024)
        return finish()
    if upto == "ffn1":
        emit_tap(dxT[1], D, T)
        return finish()
    emit_attn(dxT[1], dxT[2])
    if upto == "attn":
        emit_tap(dxT[2], D, T)
        return finish()
    emit_ffn(dxT[2], dxT[3])
    emit_final()
    b.barrier()

    with nc.Block() as blk:
        for eng in B.ENGS:
            getattr(blk, eng)(lambda e, eng=eng: [f(e) for f in b.ops[eng]])
    return nc



def _host_prep(inputs):
    mask = np.asarray(inputs["L0_attn_sparse_mask"], dtype=np.float32)
    tiles = mask.reshape(32, 128, 32, 128)
    nzmap = (tiles != 0).any(axis=(1, 3))
    nz_of_t = [[] for _ in range(32)]
    packed = []
    slot = 0
    for tj in range(32):
        for si in range(32):
            if nzmap[si, tj]:
                packed.append(np.ascontiguousarray(tiles[si, :, tj, :]))
                nz_of_t[tj].append((si, slot))
                slot += 1
    mask_nz = (np.concatenate(packed, axis=0) if packed
               else np.zeros((128, 128), np.float32))
    n_mask = max(slot, 1)

    perm_gu = np.concatenate([_pos2dense(INTER), INTER + _pos2dense(INTER)])
    perm_dn = _pos2dense(D)
    s_gu = np.ascontiguousarray(
        np.asarray(inputs["L0_ffn_scales_gu"], np.float32)[:, perm_gu])
    s_dn = np.ascontiguousarray(
        np.asarray(inputs["L0_ffn_scales_dn"], np.float32)[:, perm_dn])

    ind_gu = np.zeros((128, KT * KT), dtype=bfdt)
    for g in range(KT):
        ind_gu[:, KT * g + g] = 1.0
    ind_dn = np.zeros((128, NJ * MT), dtype=bfdt)
    p = np.arange(128)
    for j in range(NJ):
        ind_dn[p, MT * j + 2 * j + (p % 64) // 32] = 1.0

    common = dict(
        w1=np.ascontiguousarray(np.asarray(inputs["L0_attn_w1"], np.float32)),
        w2=np.ascontiguousarray(np.asarray(inputs["L0_attn_w2"], np.float32)),
        b2=np.ascontiguousarray(np.asarray(inputs["L0_attn_b2"], np.float32)),
        mask_nz=np.ascontiguousarray(mask_nz),
        qw_gu=np.ascontiguousarray(np.asarray(inputs["L0_ffn_qweight_gu"], np.int32)),
        qz_gu=np.ascontiguousarray(np.asarray(inputs["L0_ffn_qzeros_gu"], np.int32)),
        s_gu=s_gu,
        qw_dn=np.ascontiguousarray(np.asarray(inputs["L0_ffn_qweight_dn"], np.int32)),
        qz_dn=np.ascontiguousarray(np.asarray(inputs["L0_ffn_qzeros_dn"], np.int32)),
        s_dn=s_dn,
        identf=np.eye(128, dtype=np.float32),
        stair2=np.stack([((np.arange(128) % 64) // 32 == k).astype(bfdt)
                         for k in range(2)]),
        ones_col=np.ones((128, 1), dtype=bfdt),
        ones_row_b=np.ones((1, 128), dtype=bfdt),
        ones_row_f=np.ones((1, 128), dtype=np.float32),
        ind_gu=ind_gu, ind_dn=ind_dn,
    )
    return nz_of_t, n_mask, common


def kernel(**inputs):
    nz_of_t, n_mask, common = _host_prep(inputs)
    x = np.asarray(inputs["L0_attn_x"], np.float32).reshape(SEQ, D)
    nc = build_program(nz_of_t, n_mask)
    in_maps = []
    for c in range(NCORES):
        m = dict(common)
        m["x"] = np.ascontiguousarray(x[T * c:T * (c + 1), :])
        in_maps.append(m)
    res = run_bass_kernel_spmd(nc, in_maps, core_ids=list(range(NCORES)))
    out = np.concatenate([res.results[c]["out"] for c in range(NCORES)], axis=0)
    return out.reshape(1, SEQ, D).astype(np.float32)


# version 18
# speedup vs baseline: 1.1280x; 1.0420x over previous
import numpy as np
import ml_dtypes
import concourse.bass as bass
import concourse.mybir as mybir
from concourse.bass_utils import run_bass_kernel_spmd

F32 = mybir.dt.float32
BF16 = mybir.dt.bfloat16
I32 = mybir.dt.int32
U16 = mybir.dt.uint16
AF = mybir.ActivationFunctionType
ALU = mybir.AluOpType
bfdt = ml_dtypes.bfloat16

NCORES = 8
D = 4096
SEQ = 16384
T = SEQ // NCORES
INTER = 11008
NG = 2 * INTER
NJ = INTER // 256
SW = 256
MT = 2 * NJ
KT = D // 128
HALF = T // 2
EPS = 1e-5
SBUF_BASE = 16512
SBUF_TOP = 229300


def _pos2dense(n):
    p = np.arange(n)
    j, rem = p // SW, p % SW
    i, r = rem // 64, rem % 64
    return j * SW + 4 * r + i


class Alloc:
    def __init__(self, nc, base, top=SBUF_TOP):
        self.nc, self.off, self.top = nc, base, top

    def take(self, name, shape, dtype, align=64):
        esz = {F32: 4, BF16: 2, I32: 4, U16: 2}[dtype]
        self.off = -(-self.off // align) * align
        h = self.nc.alloc_sbuf_tensor_at(name, list(shape), dtype, offset=self.off)
        self.off += shape[-1] * esz
        assert self.off <= self.top, f"SBUF overflow at {name}: {self.off}"
        return h


class Sem:
    __slots__ = ("h", "n")

    def __init__(self, h):
        self.h, self.n = h, 0

    def inc(self, by=1):
        self.n += by
        return self.n


def _pre(e, pre):
    for s, v in pre:
        if v > 0:
            e.wait_ge(s.h, v)


class B:
    ENGS = ("tensor", "vector", "scalar", "sync", "gpsimd")

    def __init__(self, nc):
        self.nc = nc
        self.ops = {e: [] for e in self.ENGS}
        self._si = 0
        self.rdy = [self.sem() for _ in range(8)]
        self.fre = [self.sem() for _ in range(8)]
        self.uses = [0] * 8
        self.v_ms = self.sem()
        self.a_ms = self.sem()
        self.g_ms = self.sem()
        self.t_ms = self.sem()
        self.bar = self.sem()
        self.qstreams = {"sync": [], "scalar": [], "gpsimd": []}

    def stream(self, queue):
        s = self.sem()
        self.qstreams[queue].append(s)
        return s

    def sem(self):
        h = self.nc.semaphore(f"sm{self._si}").__enter__()
        self._si += 1
        return Sem(h)

    def emit(self, eng, fn):
        self.ops[eng].append(fn)

    def wait(self, eng, sem, val):
        if val > 0:
            self.ops[eng].append(lambda e, s=sem, v=val: e.wait_ge(s.h, v))

    def group(self, bnk, mms, pre=()):
        wf = self.uses[bnk]
        self.uses[bnk] += 1
        self.rdy[bnk].n += 1
        rv = self.rdy[bnk].n

        def run(t, mms=mms, bnk=bnk, wf=wf, pre=tuple(pre)):
            _pre(t, pre)
            if wf > 0:
                t.wait_ge(self.fre[bnk].h, wf)
            n = len(mms)
            for i, m in enumerate(mms):
                inst = m(t, i == 0, i == n - 1)
                if i == n - 1:
                    inst.then_inc(self.rdy[bnk].h, 1)
        self.emit("tensor", run)
        return rv

    def consume(self, eng, bnk, rv, fn, pre=()):
        self.fre[bnk].n += 1

        def run(e, bnk=bnk, rv=rv, fn=fn, pre=tuple(pre)):
            _pre(e, pre)
            e.wait_ge(self.rdy[bnk].h, rv)
            fn(e).then_inc(self.fre[bnk].h, 1)
        self.emit(eng, run)
        return self.fre[bnk].n

    def dma(self, queue, out, in_, sem, pre=(), **kw):
        v = sem.inc(16)

        def run(e, out=out, in_=in_, kw=kw, sem=sem, pre=tuple(pre)):
            _pre(e, pre)
            e.dma_start(out=out, in_=in_, **kw).then_inc(sem.h, 16)
        self.emit(queue, run)
        return v

    def vop(self, fn, pre=()):
        self.v_ms.n += 1

        def run(v, fn=fn, pre=tuple(pre)):
            _pre(v, pre)
            fn(v).then_inc(self.v_ms.h, 1)
        self.emit("vector", run)
        return self.v_ms.n

    def gop(self, fn, pre=()):
        self.g_ms.n += 1

        def run(g, fn=fn, pre=tuple(pre)):
            _pre(g, pre)
            fn(g).then_inc(self.g_ms.h, 1)
        self.emit("gpsimd", run)
        return self.g_ms.n

    def aop(self, fn, pre=()):
        self.a_ms.n += 1

        def run(a, fn=fn, pre=tuple(pre)):
            _pre(a, pre)
            fn(a).then_inc(self.a_ms.h, 1)
        self.emit("scalar", run)
        return self.a_ms.n

    def tmark(self):
        self.t_ms.n += 1
        self.emit("tensor", lambda t: t.nop().then_inc(self.t_ms.h, 1))
        return self.t_ms.n

    def barrier(self):
        tgt = self.bar.n + 5
        self.bar.n = tgt
        snap = {q: [(s, s.n) for s in ss if s.n > 0]
                for q, ss in self.qstreams.items()}
        for eng in self.ENGS:
            def run(e, eng=eng, snap=snap):
                for s, v in snap.get(eng, ()):
                    e.wait_ge(s.h, v)
                if eng in ("vector", "gpsimd"):
                    e.engine_nop().then_inc(self.bar.h, 1)
                else:
                    e.nop().then_inc(self.bar.h, 1)
            self.emit(eng, run)
        for eng in self.ENGS:
            self.wait(eng, self.bar, tgt)


def build_program(nz_of_t, n_mask, upto="full", dbg=()):
    K_UPTO_GATE = [False]
    nc = bass.Bass("TRN2", target_bir_lowering=False, debug=False)
    nc.detect_race_conditions = False

    dx = nc.dram_tensor("x", [T, D], F32, kind="ExternalInput").ap()
    dw1 = nc.dram_tensor("w1", [D, D], F32, kind="ExternalInput").ap()
    dw2 = nc.dram_tensor("w2", [D, D], F32, kind="ExternalInput").ap()
    db2 = nc.dram_tensor("b2", [D], F32, kind="ExternalInput").ap()
    dmask = nc.dram_tensor("mask_nz", [max(n_mask, 1) * 128, 128], F32,
                           kind="ExternalInput").ap()
    dqw_gu = nc.dram_tensor("qw_gu", [D, NG // 8], I32, kind="ExternalInput").ap()
    dqz_gu = nc.dram_tensor("qz_gu", [KT, NG // 8], I32, kind="ExternalInput").ap()
    ds_gu = nc.dram_tensor("s_gu", [KT, NG], F32, kind="ExternalInput").ap()
    dqw_dn = nc.dram_tensor("qw_dn", [INTER, D // 8], I32, kind="ExternalInput").ap()
    dqz_dn = nc.dram_tensor("qz_dn", [MT, D // 8], I32, kind="ExternalInput").ap()
    ds_dn = nc.dram_tensor("s_dn", [MT, D], F32, kind="ExternalInput").ap()
    didf = nc.dram_tensor("identf", [128, 128], F32, kind="ExternalInput").ap()
    dones = nc.dram_tensor("ones_col", [128, 1], BF16, kind="ExternalInput").ap()
    dones1b = nc.dram_tensor("ones_row_b", [1, 128], BF16, kind="ExternalInput").ap()
    dones1f = nc.dram_tensor("ones_row_f", [1, 128], F32, kind="ExternalInput").ap()
    dind_gu = nc.dram_tensor("ind_gu", [128, KT * KT], BF16, kind="ExternalInput").ap()
    dind_dn = nc.dram_tensor("ind_dn", [128, NJ * MT], BF16, kind="ExternalInput").ap()
    dstair2 = nc.dram_tensor("stair2", [2, 128], BF16, kind="ExternalInput").ap()
    dout = nc.dram_tensor("out", [T, D], F32, kind="ExternalOutput").ap()

    dxT = [nc.dram_tensor(f"xT{i}", [D, T], F32, kind="Internal").ap()
           for i in range(4)]
    dxn = nc.dram_tensor("xn_sc", [D, T], BF16, kind="Internal").ap()
    dguc = nc.dram_tensor("gu_cache", [2 * NJ * 128, KT * SW], BF16, kind="Internal").ap()
    ddnc = nc.dram_tensor("dn_cache", [MT * 8 * 128, 512], BF16, kind="Internal").ap()
    dh = nc.dram_tensor("h_sc", [MT * 4 * 128, 512], BF16, kind="Internal").ap()
    dzs_gu = nc.dram_tensor("zs_gu_sc", [KT, NG], BF16, kind="Internal").ap()
    dzs_dn = nc.dram_tensor("zs_dn_sc", [MT, D], BF16, kind="Internal").ap()
    dw2bf = nc.dram_tensor("w2bf", [D, D], BF16, kind="Internal").ap()
    dw1c = nc.dram_tensor("w1c", [32 * 128, KT * 128], BF16, kind="Internal").ap()

    pb = [nc.psum_tensor(f"pbank{i}", [128, 512], F32).__enter__() for i in range(8)]
    b = B(nc)

    al = Alloc(nc, SBUF_BASE)
    sb_idf = al.take("idf", [128, 128], F32)
    sb_ones = al.take("onesc", [128, 1], BF16)
    sb_ones1b = al.take("ones1b", [1, 128], BF16)
    sb_ones1f = al.take("ones1f", [1, 128], F32)
    sb_indgu = al.take("indgu", [128, KT * KT], BF16)
    sb_stair2 = al.take("stair2", [2, 128], BF16)
    sb_b2 = al.take("b2t", [128, 32], F32)
    sb_idfb = al.take("idfb", [128, 128], BF16)
    PERSIST = al.off

    s_const = b.stream("sync")
    s_w2bf = b.stream("gpsimd")
    for out, in_ in ((sb_idf[:], didf), (sb_ones[:], dones),
                     (sb_ones1b[:], dones1b), (sb_ones1f[:], dones1f),
                     (sb_indgu[:], dind_gu), (sb_stair2[:], dstair2)):
        b.dma("sync", out, in_, s_const)
    b.dma("sync", sb_b2[:], db2.rearrange("(j p) -> p j", p=128), s_const,
          allow_slow_non_contiguous=True)
    for r in range(4):
        b.dma("gpsimd",
              dw2bf[1024 * r:1024 * (r + 1), :].rearrange("(a p) c -> p a c", p=128),
              dw2[1024 * r:1024 * (r + 1), :].rearrange("(a p) c -> p a c", p=128),
              s_w2bf)
    w2bf_done = s_w2bf.n
    w1cv = dw1c.rearrange("(st p) (dt c) -> st p dt c", p=128, c=128)
    for st in range(32):
        b.dma("gpsimd", w1cv[st, :, :, :],
              dw1[:, 128 * st:128 * (st + 1)].rearrange("(dt p) c -> p dt c", p=128),
              s_w2bf)
    w1c_done = s_w2bf.n
    b.vop(lambda v: v.tensor_copy(sb_idfb[:], sb_idf[:]),
          pre=[(s_const, s_const.n)])
    for eng in ("tensor", "vector", "scalar"):
        b.wait(eng, s_const, s_const.n)

    def emit_tap(dsrc_bytes_ap, nrows, rowlen_f32):
        lo = Alloc(nc, PERSIST)
        tbuf = lo.take("tapbuf", [128, 2048], F32)
        s_tl = b.stream("sync")
        s_tw = b.stream("gpsimd")
        dflat = dout.rearrange("a c -> (a c)")
        for i in range(0, nrows, 128):
            n = min(128, nrows - i)
            wv = b.dma("sync", tbuf[:n, :rowlen_f32], dsrc_bytes_ap[i:i + n, :],
                       s_tl, pre=[(s_tw, s_tw.n)])
            vv = b.vop(lambda v: v.engine_nop(), pre=[(s_tl, wv)])
            seg = dflat[i * rowlen_f32:(i + n) * rowlen_f32]
            b.dma("gpsimd", seg.rearrange("(p f) -> p f", f=rowlen_f32),
                  tbuf[:n, :rowlen_f32], s_tw, pre=[(b.v_ms, vv)])

    def finish():
        b.barrier()
        with nc.Block() as blk:
            for eng in B.ENGS:
                getattr(blk, eng)(lambda e, eng=eng: [f(e) for f in b.ops[eng]])
        return nc

    lnal = Alloc(nc, SBUF_TOP - 66000)
    ln_xb = lnal.take("lnxb", [128, KT * 512], BF16)
    ln_xq = [lnal.take(f"lnxq{i}", [128, 8 * 512], BF16) for i in range(2)]
    ln_r1 = lnal.take("lnr1", [1, 512], F32)
    ln_r2 = lnal.take("lnr2", [1, 512], F32)
    ln_rt = lnal.take("lnrt", [1, 512], F32)
    ln_rb = lnal.take("lnrb", [128, 512], F32)
    ln_mb = lnal.take("lnmb", [128, 512], F32)
    ln_tmp = lnal.take("lntmp", [128, 512], F32)
    ln_xn = [lnal.take(f"lnxn{i}", [128, 512], BF16) for i in range(4)]
    s_lnl = b.stream("gpsimd")
    s_lw = [b.stream("gpsimd") for _ in range(4)]
    ln_pace = b.sem()
    ln_st = {"xbf": 0, "sq0": 0, "sq1": 0, "xn": [0] * 4}

    def ln_load(dsrc, tc, seg, pre):
        p = list(pre)
        if seg is None or seg == 0:
            p.append((b.v_ms, ln_st["xbf"]))
        if seg is None:
            b.dma("gpsimd",
                  ln_xb[:].rearrange("p (dt c) -> p dt c", c=512),
                  dsrc[:, 512 * tc:512 * (tc + 1)]
                  .rearrange("(dt p) c -> p dt c", p=128),
                  s_lnl, pre=p)
        else:
            b.dma("gpsimd",
                  ln_xb[:, 2048 * seg:2048 * (seg + 1)]
                  .rearrange("p (dt c) -> p dt c", c=512),
                  dsrc[512 * seg:512 * (seg + 1), 512 * tc:512 * (tc + 1)]
                  .rearrange("(dt p) c -> p dt c", p=128),
                  s_lnl, pre=p)

    def ln_compute(tc, ld_pre=None):
        if ld_pre is None:
            ld_pre = [(s_lnl, s_lnl.n)]
        p0p = ln_pace.n
        sqv = []
        for c in range(4):
            if c == 0:
                pre = list(ld_pre)
                if ln_st["sq0"]:
                    pre.append((ln_pace, ln_st["sq0"]))
            elif c == 1:
                pre = [(b.rdy[7], ln_st["sq1"])] if ln_st["sq1"] else []
            elif c == 2:
                pre = [(ln_pace, p0p + 1)]
            else:
                pre = [(ln_pace, p0p + 2)]
            sqv.append(b.aop(lambda a, c=c: a.activation(
                ln_xq[c % 2][:], ln_xb[:, 4096 * c:4096 * (c + 1)], AF.Square),
                pre=pre))
        rv6 = b.group(6, [
            (lambda t, s, e, dt=dt: t.matmul(
                pb[6][:1, :], sb_ones[:], ln_xb[:, 512 * dt:512 * (dt + 1)],
                start=s, stop=e)) for dt in range(KT)],
            pre=list(ld_pre))
        wf7 = b.uses[7]
        b.uses[7] += 1
        b.rdy[7].n += 1
        rv7 = b.rdy[7].n

        def sq_run(t, sqv=tuple(sqv), wf7=wf7):
            if wf7 > 0:
                t.wait_ge(b.fre[7].h, wf7)
            for c in range(4):
                t.wait_ge(b.a_ms.h, sqv[c])
                for i in range(8):
                    dt = 8 * c + i
                    inst = t.matmul(pb[7][:1, :], sb_ones[:],
                                    ln_xq[c % 2][:, 512 * i:512 * (i + 1)],
                                    start=(dt == 0), stop=(dt == KT - 1))
                    if dt == KT - 1:
                        inst.then_inc(b.rdy[7].h, 1)
                    elif i == 7:
                        inst.then_inc(ln_pace.h, 1)
        b.emit("tensor", sq_run)
        ln_pace.n += 3
        ln_st["sq0"] = p0p + 3
        ln_st["sq1"] = rv7
        b.consume("vector", 6, rv6,
                  lambda v: v.tensor_scalar_mul(ln_r1[:], pb[6][:1, :], 1.0 / D))
        b.consume("vector", 7, rv7,
                  lambda v: v.tensor_scalar_mul(ln_r2[:], pb[7][:1, :], 1.0 / D))

        def vstat(v):
            v.tensor_tensor(ln_rt[:], ln_r1[:], ln_r1[:], ALU.mult)
            v.tensor_tensor(ln_r2[:], ln_r2[:], ln_rt[:], ALU.subtract)
            return v.tensor_scalar_add(ln_r2[:], ln_r2[:], EPS)
        vv = b.vop(vstat)
        av = b.aop(lambda a: a.activation(ln_r2[:], ln_r2[:], AF.Sqrt),
                   pre=[(b.v_ms, vv)])

        def vrec(v):
            v.reciprocal(ln_r2[:], ln_r2[:])
            return v.tensor_tensor(ln_r1[:], ln_r1[:], ln_r2[:], ALU.mult)
        vv = b.vop(vrec, pre=[(b.a_ms, av)])
        for bnk, srcr, dst in ((6, ln_r2, ln_rb), (7, ln_r1, ln_mb)):
            rvb = b.group(bnk, [
                lambda t, s, e, bnk=bnk, srcr=srcr: t.matmul(
                    pb[bnk][:], sb_ones1f[:], srcr[:], start=s, stop=e)],
                pre=[(b.v_ms, vv)])
            b.consume("vector", bnk, rvb,
                      lambda v, dst=dst, bnk=bnk: v.tensor_copy(dst[:], pb[bnk][:]))
        lastv = 0
        for dt in range(KT):
            sl = dt % 4

            def vnorm(v, dt=dt, sl=sl):
                v.tensor_tensor(ln_tmp[:], ln_xb[:, 512 * dt:512 * (dt + 1)],
                                ln_rb[:], ALU.mult)
                return v.tensor_tensor(ln_xn[sl][:], ln_tmp[:], ln_mb[:],
                                       ALU.subtract)
            pre = [(s_lw[sl], ln_st["xn"][sl])] if ln_st["xn"][sl] else []
            lastv = b.vop(vnorm, pre=pre)
            b.dma("gpsimd",
                  dxn[128 * dt:128 * (dt + 1), 512 * tc:512 * (tc + 1)],
                  ln_xn[sl][:], s_lw[sl], pre=[(b.v_ms, lastv)])
            ln_st["xn"][sl] = s_lw[sl].n
        ln_st["xbf"] = lastv

    p0 = Alloc(nc, PERSIST)
    sb_xh = [p0.take(f"p0xh{i}", [128, D], F32) for i in range(4)]
    sb_xts = [p0.take(f"p0xts{i}", [128, 512], F32) for i in range(2)]
    sb_qzg = p0.take("p0qzg", [KT, NG // 8], I32)
    sb_zu = p0.take("p0zu", [KT, 4096], U16)
    sb_zb = p0.take("p0zb", [KT, 4096], BF16)
    sb_qzd = p0.take("p0qzd", [MT, D // 8], I32)
    sb_zud = p0.take("p0zud", [MT, D], U16)
    sb_zbd = p0.take("p0zbd", [MT, D], BF16)

    s_px = b.stream("sync")
    s_pw = [b.stream("gpsimd") for _ in range(2)]
    xts_guard = [0, 0]
    xh_guard = [0]
    for tg in range(4):
        wv = [b.dma("sync", sb_xh[i][:],
                    dx[(4 * tg + i) * 128:(4 * tg + i + 1) * 128, :], s_px,
                    pre=([(b.t_ms, xh_guard[0])] if i == 0 else ()))
              for i in range(4)]
        for dt in range(KT):
            bnk = dt % 2
            wf = b.uses[bnk]
            b.uses[bnk] += 1
            b.rdy[bnk].n += 1
            rv = b.rdy[bnk].n

            def trun(t, dt=dt, bnk=bnk, wf=wf, w=wv[3]):
                t.wait_ge(s_px.h, w)
                if wf > 0:
                    t.wait_ge(b.fre[bnk].h, wf)
                for i in range(4):
                    inst = t.transpose(pb[bnk][:, 128 * i:128 * (i + 1)],
                                       sb_xh[i][:, 128 * dt:128 * (dt + 1)],
                                       sb_idf[:])
                    if i == 3:
                        inst.then_inc(b.rdy[bnk].h, 1)
            b.emit("tensor", trun)
            st = sb_xts[dt % 2]
            fv = b.consume("vector", bnk, rv,
                           lambda v, st=st, bnk=bnk: v.tensor_copy(st[:], pb[bnk][:]),
                           pre=[(s_pw[dt % 2], xts_guard[dt % 2])])
            b.dma("gpsimd", dxT[0][128 * dt:128 * (dt + 1), 512 * tg:512 * (tg + 1)],
                  st[:], s_pw[dt % 2], pre=[(b.fre[bnk], fv)])
            xts_guard[dt % 2] = s_pw[dt % 2].n
        xh_guard[0] = b.tmark()
        ln_load(dxT[0], tg, None, [(s_pw[0], s_pw[0].n), (s_pw[1], s_pw[1].n)])
        ln_compute(tg)

    s_zs = b.stream("sync")
    wq = b.dma("sync", sb_qzg[:], dqz_gu, s_zs)
    zb_guard = 0
    for ch in range(6):
        c0 = 4096 * ch
        w = min(4096, NG - c0)

        def zrun(v, c0=c0, w=w):
            qz = sb_qzg[:].bitcast(U16)
            for o in range(0, w, SW):
                u0 = ((c0 + o) // SW) * 64
                for i in range(4):
                    v.tensor_scalar(sb_zu[:, o + 64 * i:o + 64 * (i + 1)],
                                    qz[:, u0:u0 + 64], 4 * i, 15,
                                    ALU.logical_shift_right, ALU.bitwise_and)
            return v.tensor_copy(sb_zb[:, :w], sb_zu[:, :w])
        vv = b.vop(zrun, pre=[(s_zs, max(wq, zb_guard))])
        b.dma("sync", dzs_gu[:, c0:c0 + w], sb_zb[:, :w], s_zs, pre=[(b.v_ms, vv)])
        zb_guard = s_zs.n

    wq = b.dma("sync", sb_qzd[:], dqz_dn, s_zs)

    def zdrun(v):
        qz = sb_qzd[:].bitcast(U16)
        for o in range(0, D, SW):
            u0 = (o // SW) * 64
            for i in range(4):
                v.tensor_scalar(sb_zud[:, o + 64 * i:o + 64 * (i + 1)],
                                qz[:, u0:u0 + 64], 4 * i, 15,
                                ALU.logical_shift_right, ALU.bitwise_and)
        return v.tensor_copy(sb_zbd[:], sb_zud[:])
    vv = b.vop(zdrun, pre=[(s_zs, wq)])
    b.dma("sync", dzs_dn[:], sb_zbd[:], s_zs, pre=[(b.v_ms, vv)])

    b.barrier()

    if upto == "p0":
        emit_tap(dxT[0], D, T)
        return finish()
    if upto == "ln1":
        emit_tap(dxn.bitcast(F32), D, T // 2)
        return finish()

    p1 = Alloc(nc, PERSIST)
    sb_qw = [p1.take(f"dqqw{i}", [128, NG // 8], I32) for i in range(2)]
    sb_srow = [p1.take(f"dqsr{i}", [2, 8192], BF16) for i in range(2)]
    sb_sbc = [p1.take(f"dqsbc{i}", [128, 1024], BF16) for i in range(8)]
    sb_nib = [p1.take(f"dqnib{i}", [128, 512], U16) for i in range(2)]
    sb_nbb = p1.take("dqnbb", [128, 512], BF16)
    sb_wst = [p1.take(f"dqwst{i}", [128, 2048], BF16) for i in range(2)]

    s_qw = [b.stream("sync") for _ in range(2)]
    s_sr = [b.stream("gpsimd") for _ in range(2)]
    s_wc = [b.stream("sync") for _ in range(2)]
    DQ = dict(qw=sb_qw, srow=sb_srow, sbc=sb_sbc, nib=sb_nib, nbb=sb_nbb,
              wst=sb_wst, s_qw=s_qw, s_sr=s_sr, s_wc=s_wc, q_wc="sync",
              qw_guard=[0, 0], srow_guard=[(0, 0), (0, 0)],
              sbc_guard=[0] * 8, wst_guard=[0, 0], subctr=[0], banks=None)

    dnal = Alloc(nc, 168000)
    DQD = dict(qw=[dnal.take(f"dnqw{i}", [128, D // 8], I32) for i in range(2)],
               srow=[dnal.take(f"dnsr{i}", [2, 4096], BF16) for i in range(2)],
               sbc=[dnal.take(f"dnsbc{i}", [128, 1024], BF16) for i in range(8)],
               nib=[dnal.take(f"dnnib{i}", [128, 512], U16) for i in range(2)],
               nbb=dnal.take("dnnbb", [128, 512], BF16),
               wst=[dnal.take(f"dnwst{i}", [128, 2048], BF16) for i in range(2)],
               s_qw=[b.stream("sync") for _ in range(2)],
               s_sr=[b.stream("gpsimd") for _ in range(2)],
               s_wc=[b.stream("gpsimd") for _ in range(2)],
               q_wc="gpsimd",
               qw_guard=[0, 0], srow_guard=[(0, 0), (0, 0)],
               sbc_guard=[0] * 8, wst_guard=[0, 0], subctr=[0], banks=(6, 7))

    def emit_dequant(S, dqw, dcache, n_i32, n_pos, dsrows, dzrows, n_tiles,
                     dn_mode, g0=0, g1=None):
        if g1 is None:
            g1 = n_tiles
        qsel = (dqw.rearrange("(j r s) c -> j s r c", r=64, s=4) if dn_mode else None)
        sb_qw_, sb_srow_, sb_sbc_ = S["qw"], S["srow"], S["sbc"]
        sb_nib_, sb_nbb_, sb_wst_ = S["nib"], S["nbb"], S["wst"]
        s_qw_, s_sr_, s_wc_ = S["s_qw"], S["s_sr"], S["s_wc"]
        qw_guard, srow_guard = S["qw_guard"], S["srow_guard"]
        sbc_guard, wst_guard = S["sbc_guard"], S["wst_guard"]
        subctr = S["subctr"]
        qwv = S.setdefault("qwv", {})

        def qload(g):
            if g in qwv or g >= n_tiles:
                return
            qs = sb_qw_[g % 2]
            if dn_mode:
                j, q = g // 2, g % 2
                qwv[g] = b.dma("sync", qs[:, :n_i32],
                               qsel[j, 2 * q:2 * q + 2, :, :],
                               s_qw_[g % 2], pre=[(b.v_ms, qw_guard[g % 2])])
            else:
                qwv[g] = b.dma("sync", qs[:, :n_i32],
                               dqw[128 * g:128 * (g + 1), :],
                               s_qw_[g % 2], pre=[(b.v_ms, qw_guard[g % 2])])

        def srload(g, ch, w):
            chi = (ch // 2048) % 2
            sr = sb_srow_[chi]
            gb, gv = srow_guard[chi]
            nrow = 2 if dn_mode else 1
            gg = (g // 2) * 2 if dn_mode else g
            srv = sr[:nrow, :].rearrange("a (u two c) -> a u two c", two=2, c=512)
            wsv = b.dma("gpsimd", srv[:, :w // 512, 0, :],
                        dsrows[gg:gg + nrow, ch:ch + w]
                        .rearrange("a (u c) -> a u c", c=512),
                        s_sr_[chi], pre=[(b.rdy[gb], gv)])
            wzv = b.dma("gpsimd", srv[:, :w // 512, 1, :],
                        dzrows[gg:gg + nrow, ch:ch + w]
                        .rearrange("a (u c) -> a u c", c=512),
                        s_sr_[chi])
            return (wsv, wzv)

        for g in range(g0, g1):
            qload(g)
            qs = sb_qw_[g % 2]
            srw = {}
            if dn_mode:
                for ch in range(0, n_pos, 2048):
                    srw[ch] = srload(g, ch, min(2048, n_pos - ch))
            qload(g + 1)
            for ch in range(0, n_pos, 2048):
                w = min(2048, n_pos - ch)
                chi = (ch // 2048) % 2
                sr = sb_srow_[chi]
                nrow = 2 if dn_mode else 1
                wsv, wzv = srw[ch] if ch in srw else srload(g, ch, w)
                wstg = sb_wst_[chi]
                last_rv = (6, 0)
                for sub in range(0, w, 512):
                    si = subctr[0] % 8
                    if S["banks"] is not None:
                        bnk = 4 + 2 * (subctr[0] % 2)
                        bnkz = bnk + 1
                    else:
                        bnk = (2 * subctr[0]) % 8
                        bnkz = bnk + 1
                    subctr[0] += 1
                    sbc = sb_sbc_[si]
                    lhsb = sb_stair2[:] if dn_mode else sb_ones1b[:]
                    rv = b.group(bnk, [
                        (lambda t, s, e, sr=sr, sub=sub, bnk=bnk, lhsb=lhsb,
                         nrow=nrow:
                         t.matmul(pb[bnk][:], lhsb,
                                  sr[:nrow, 2 * sub:2 * sub + 512],
                                  start=s, stop=e))],
                        pre=[(s_sr_[chi], wzv)])
                    rvz = b.group(bnkz, [
                        (lambda t, s, e, sr=sr, sub=sub, bnkz=bnkz, lhsb=lhsb,
                         nrow=nrow:
                         t.matmul(pb[bnkz][:], lhsb,
                                  sr[:nrow, 2 * sub + 512:2 * sub + 1024],
                                  start=s, stop=e))])
                    last_rv = (bnkz, rvz)
                    fv = b.consume("scalar", bnk, rv,
                                   lambda a, sbc=sbc, bnk=bnk:
                                   a.activation(sbc[:, :512], pb[bnk][:],
                                                AF.Identity),
                                   pre=[(b.v_ms, sbc_guard[si])])
                    fvz = b.consume("scalar", bnkz, rvz,
                                    lambda a, sbc=sbc, bnkz=bnkz:
                                    a.activation(sbc[:, 512:], pb[bnkz][:],
                                                 AF.Identity))

                    nib = sb_nib_[subctr[0] % 2]

                    def drun(v, qs=qs, sbc=sbc, pos=ch + sub, sub=sub,
                             nib=nib, n_i32=n_i32, wstg=wstg):
                        qu = qs[:, :n_i32].bitcast(U16)
                        for jj in range(2):
                            u0 = ((pos + SW * jj) // SW) * 64
                            for i in range(4):
                                v.tensor_scalar(
                                    nib[:, SW * jj + 64 * i:SW * jj + 64 * (i + 1)],
                                    qu[:, u0:u0 + 64], 4 * i, 15,
                                    ALU.logical_shift_right, ALU.bitwise_and)
                        zbv = sbc[:].rearrange("p (u two c) -> p u two c",
                                               two=2, c=512)
                        v.tensor_tensor(sb_nbb_[:], nib[:], zbv[:, 0, 1, :],
                                        ALU.subtract)
                        return v.tensor_tensor(wstg[:, sub:sub + 512],
                                               sb_nbb_[:], zbv[:, 0, 0, :],
                                               ALU.mult)
                    vv = b.vop(drun, pre=[(s_qw_[g % 2], qwv[g]),
                                          (s_wc_[chi], wst_guard[chi]),
                                          (b.fre[bnk], fv), (b.fre[bnkz], fvz)])
                    sbc_guard[si] = vv
                srow_guard[chi] = last_rv
                vvw = b.v_ms.n
                if not dn_mode:
                    g4 = dcache.rearrange("(s p) (kt c) -> s p kt c", p=128, kt=KT)
                    for o in range(0, w, SW):
                        s_idx = (ch + o) // SW
                        b.dma(S["q_wc"], g4[s_idx, :, g, :],
                              wstg[:, o:o + SW], s_wc_[chi],
                              pre=([(b.v_ms, vvw)] if o == 0 else ()))
                else:
                    d4 = dcache.rearrange("(mt oc p) c -> mt oc p c", oc=8, p=128)
                    for o in range(0, w, 512):
                        b.dma(S["q_wc"], d4[g, (ch + o) // 512, :, :],
                              wstg[:, o:o + 512], s_wc_[chi],
                              pre=([(b.v_ms, vvw)] if o == 0 else ()))
                wst_guard[chi] = s_wc_[chi].n
            qw_guard[g % 2] = b.v_ms.n

    emit_dequant(DQ, dqw_gu, dguc, NG // 8, NG, ds_gu, dzs_gu, KT, False)
    b.barrier()

    s_gx = b.stream("sync")
    s_rg = [b.stream("sync") for _ in range(4)]
    s_hw = [b.stream("gpsimd") for _ in range(4)]
    s_hh = b.stream("sync")
    s_wd = [b.stream("scalar") for _ in range(8)]
    s_xr = b.stream("gpsimd")
    s_os = [b.stream("gpsimd") for _ in range(8)]

    def emit_ffn(dsrc, ddst, ln_dst=False, co=None):
        lo = Alloc(nc, PERSIST)
        xnh = lo.take("gxn", [128, KT * 1024], BF16)
        ring = [lo.take(f"grg{i}", [128, KT * SW], BF16) for i in range(4)]
        silub = [lo.take(f"gsl{i}", [128, 2048], BF16) for i in range(2)]
        hstg = [lo.take(f"ghs{i}", [128, 1024], BF16) for i in range(4)]

        ring_guard = [(0, 0)] * 4
        silu_guard = [(0, 0), (0, 0)]
        silu_fv = {}
        hmul_fv = {}
        hst_guard = [0, 0, 0, 0]
        rot = [0]

        for half in range(2):
            pre0 = ([(b.rdy[k], b.rdy[k].n) for k in range(8)] if half == 1 else [])
            xwv = [b.dma("sync", xnh[:, 1024 * k:1024 * (k + 1)],
                         dxn[128 * k:128 * (k + 1), HALF * half:HALF * (half + 1)],
                         s_gx, pre=(pre0 if k == 0 else ()))
                   for k in range(KT)]

            for jp in range(2 * NJ):
                if co and (half * 2 * NJ + jp) % 2 == 0:
                    co.pop(0)()
                j, side = jp // 2, jp % 2
                slot = ring[jp % 4]
                pos0 = side * INTER + SW * j
                gb, gv = ring_guard[jp % 4]
                s_idx = pos0 // SW
                wv = b.dma("sync", slot[:],
                           dguc[128 * s_idx:128 * (s_idx + 1), :],
                           s_rg[jp % 4], pre=[(b.rdy[gb], gv)])
                for q in range(2):
                    for tc in range(2):
                        bnk = rot[0]
                        rot[0] = (rot[0] + 1) % 4
                        mms = [(lambda t, s, e, k=k, q=q, tc=tc, slot=slot, bnk=bnk:
                                t.matmul(pb[bnk][:],
                                         slot[:, SW * k + 128 * q:SW * k + 128 * (q + 1)],
                                         xnh[:, 1024 * k + 512 * tc:1024 * k + 512 * (tc + 1)],
                                         start=s, stop=e))
                               for k in range(KT)]
                        rv = b.group(bnk, mms,
                                     pre=([(s_rg[jp % 4], wv), (s_gx, xwv[-1])]
                                          if (q == 0 and tc == 0) else ()))
                        ring_guard[jp % 4] = (bnk, rv)
                        if side == 0:
                            sl = silub[j % 2]
                            gb2, gv2 = silu_guard[j % 2]
                            sfv = b.consume("scalar", bnk, rv,
                                            lambda a, sl=sl, q=q, tc=tc, bnk=bnk:
                                            a.activation(
                                                sl[:, 1024 * q + 512 * tc:1024 * q + 512 * (tc + 1)],
                                                pb[bnk][:], AF.Silu),
                                            pre=([(b.fre[gb2], gv2)]
                                                 if (q == 0 and tc == 0) else ()))
                            silu_fv[(q, tc)] = (bnk, sfv)
                        else:
                            hs = hstg[2 * (j % 2) + q]
                            sl = silub[j % 2]

                            def hmul(v, hs=hs, sl=sl, q=q, tc=tc, bnk=bnk):
                                return v.tensor_tensor(
                                    hs[:, 512 * tc:512 * (tc + 1)], pb[bnk][:],
                                    sl[:, 1024 * q + 512 * tc:1024 * q + 512 * (tc + 1)],
                                    ALU.mult)
                            sb_, sv_ = silu_fv[(q, tc)]
                            fv = b.consume("vector", bnk, rv, hmul,
                                           pre=([(b.fre[sb_], sv_),
                                                 (s_hw[2 * (j % 2) + q],
                                                  hst_guard[2 * (j % 2) + q])]
                                                if tc == 0 else [(b.fre[sb_], sv_)]))
                            hmul_fv[(j % 2, q)] = (bnk, fv)
                            if tc == 1:
                                silu_guard[j % 2] = (bnk, fv)
                if side == 1:
                    dh4 = dh.rearrange("(mt tq p) c -> mt tq p c", tq=4, p=128)
                    for q in range(2):
                        hb, hv = hmul_fv[(j % 2, q)]
                        for tc in range(2):
                            b.dma("gpsimd",
                                  dh4[2 * j + q, 2 * half + tc, :, :],
                                  hstg[2 * (j % 2) + q][:, 512 * tc:512 * (tc + 1)],
                                  s_hw[2 * (j % 2) + q],
                                  pre=([(b.fre[hb], hv)] if tc == 0 else ()))
                        hst_guard[2 * (j % 2) + q] = s_hw[2 * (j % 2) + q].n

        while co:
            co.pop(0)()
        b.barrier()
        if K_UPTO_GATE[0]:
            return

        lo = Alloc(nc, PERSIST)
        hh = lo.take("dnh", [128, MT * 512], BF16)
        wdn = [lo.take(f"dnw{i}", [128, 512], BF16) for i in range(8)]
        xres = [lo.take(f"dnxr{i}", [128, 512], F32) for i in range(8)]
        ost = [lo.take(f"dnos{i}", [128, 512], F32) for i in range(8)]

        wdn_guard = [0] * 8
        xres_guard = [0] * 8
        ost_guard = [0] * 8
        dsel = ddnc.rearrange("(mt oc p) c -> mt oc p c", oc=8, p=128)
        dh4 = dh.rearrange("(mt tq p) c -> mt tq p c", tq=4, p=128)
        xsel = dsrc.rearrange("(c r s) t -> c s r t", r=64, s=4)
        osel = ddst.rearrange("(c r s) t -> c s r t", r=64, s=4)

        for tq in range(4):
            pre0 = [(b.rdy[k], b.rdy[k].n) for k in range(8)]
            hwv = [b.dma("sync", hh[:, 512 * mt:512 * (mt + 1)],
                         dh4[mt, tq, :, :],
                         s_hh, pre=(pre0 if mt == 0 else ()))
                   for mt in range(MT)]

            for og in range(8):
                base = 4 * (og % 2)
                banks = [base + i for i in range(4)]
                for ot in range(4):
                    bnk = banks[ot]
                    jj, q = 2 * og + ot // 2, ot % 2
                    b.dma("gpsimd", xres[bnk][:],
                          xsel[jj, 2 * q:2 * q + 2, :, 512 * tq:512 * (tq + 1)],
                          s_xr, pre=[(b.fre[bnk], xres_guard[bnk])])
                xr_done = s_xr.n
                open_wf = []
                rvs = {}
                for bnk in banks:
                    open_wf.append((bnk, b.uses[bnk]))
                    b.uses[bnk] += 1
                    b.rdy[bnk].n += 1
                    rvs[bnk] = b.rdy[bnk].n
                for mt in range(MT):
                    ws = wdn[mt % 8]
                    wv = b.dma("scalar", ws[:], dsel[mt, og, :, :],
                               s_wd[mt % 8], pre=[(b.t_ms, wdn_guard[mt % 8])])

                    def dmm(t, mt=mt, ws=ws, wv=wv, banks=tuple(banks),
                            open_wf=tuple(open_wf), hv=hwv[-1]):
                        t.wait_ge(s_wd[mt % 8].h, wv)
                        if mt == 0:
                            t.wait_ge(s_hh.h, hv)
                            for bnk, wf in open_wf:
                                if wf > 0:
                                    t.wait_ge(b.fre[bnk].h, wf)
                        for ot in range(4):
                            inst = t.matmul(
                                pb[banks[ot]][:], ws[:, 128 * ot:128 * (ot + 1)],
                                hh[:, 512 * mt:512 * (mt + 1)],
                                start=(mt == 0), stop=(mt == MT - 1))
                            if mt == MT - 1:
                                inst.then_inc(b.rdy[banks[ot]].h, 1)
                        if mt != MT - 1:
                            inst.then_inc(b.t_ms.h, 1)
                        else:
                            t.nop().then_inc(b.t_ms.h, 1)
                    b.t_ms.n += 1
                    b.emit("tensor", dmm)
                    wdn_guard[mt % 8] = b.t_ms.n
                for ot in range(4):
                    bnk = banks[ot]
                    jj, q = 2 * og + ot // 2, ot % 2
                    if "noxres" in dbg:
                        fv = b.consume("vector", bnk, rvs[bnk],
                                       lambda v, bnk=bnk: v.tensor_scalar_mul(
                                           ost[bnk][:], pb[bnk][:], 0.5),
                                       pre=[(s_xr, xr_done),
                                            (s_os[bnk], ost_guard[bnk])])
                    else:
                        fv = b.consume("vector", bnk, rvs[bnk],
                                       lambda v, bnk=bnk: v.scalar_tensor_tensor(
                                           ost[bnk][:], pb[bnk][:], 0.5, xres[bnk][:],
                                           ALU.mult, ALU.add),
                                       pre=[(s_xr, xr_done),
                                            (s_os[bnk], ost_guard[bnk])])
                    xres_guard[bnk] = fv
                    b.dma("gpsimd",
                          osel[jj, 2 * q:2 * q + 2, :, 512 * tq:512 * (tq + 1)],
                          ost[bnk][:], s_os[bnk], pre=[(b.fre[bnk], fv)])
                    ost_guard[bnk] = s_os[bnk].n
                if ln_dst:
                    ln_load(ddst, tq, og,
                            [(s_os[bk], ost_guard[bk]) for bk in banks])
            if ln_dst:
                ln_compute(tq)
        b.barrier()

    def emit_attn(dsrc, ddst):
        lo = Alloc(nc, PERSIST)
        abuf = lo.take("abuf", [128, max(n_mask, 1) * 128], BF16)
        A_END = lo.off
        w1s = [lo.take(f"aw1{i}", [128, KT * 128], BF16) for i in range(4)]
        w2t = lo.take("aw2", [128, KT * 512], BF16)
        w2bs = [lo.take(f"aw2b{i}", [128, 4096], BF16) for i in range(4)]
        amask = [lo.take(f"amk{i}", [128, 128], F32) for i in range(2)]

        nz_of_chunk = []
        for tcb in range(8):
            ss = {}
            for tt in range(4 * tcb, 4 * tcb + 4):
                for (s_t, slot) in nz_of_t[tt]:
                    ss.setdefault(s_t, []).append((tt, slot))
            nz_of_chunk.append(sorted(ss.items()))

        s_w2 = [b.stream("sync") for _ in range(2)]
        s_w1 = [b.stream("sync") for _ in range(4)]
        s_mk = [b.stream("sync") for _ in range(2)]
        w1_guard = [0] * 4
        w2_guard = [0]
        w2b_guard = [0]
        mk_guard = [("v", 0, 0), ("v", 0, 0)]
        arot = [0]
        mki = [0]
        for tcb in range(8):
            w2s = w2t
            wv2b = [b.dma("sync", w2bs[a][:],
                          dw2bf[512 * tcb + 128 * a:512 * tcb + 128 * (a + 1), :],
                          s_w2[a % 2],
                          pre=([(b.t_ms, w2b_guard[0]), (s_w2bf, w2bf_done)]
                               if a == 0 else ()))
                    for a in range(4)]
            ldw = [(s_w2[0], s_w2[0].n), (s_w2[1], s_w2[1].n)]
            evf = {}
            for dt in range(KT):
                bnk = 6 + (dt % 2)
                wf = b.uses[bnk]
                b.uses[bnk] += 1
                b.rdy[bnk].n += 1
                rv = b.rdy[bnk].n

                def w2run(t, dt=dt, bnk=bnk, wf=wf, ldw=tuple(ldw)):
                    if dt == 0:
                        for s_, v_ in ldw:
                            t.wait_ge(s_.h, v_)
                    if wf > 0:
                        t.wait_ge(b.fre[bnk].h, wf)
                    pbb = pb[bnk].bitcast(BF16)
                    for a in range(4):
                        inst = t.transpose(pbb[:, 128 * a:128 * (a + 1)],
                                           w2bs[a][:, 128 * dt:128 * (dt + 1)],
                                           sb_idfb[:])
                        if a == 3:
                            inst.then_inc(b.rdy[bnk].h, 1)
                b.emit("tensor", w2run)
                evf[bnk] = b.consume(
                    "scalar", bnk, rv,
                    lambda a_, dt=dt, bnk=bnk: a_.activation(
                        w2s[:, 512 * dt:512 * (dt + 1)],
                        pb[bnk].bitcast(BF16)[:, :512], AF.Identity),
                    pre=([(b.t_ms, w2_guard[0])] if dt == 0 else ()))
            w2b_guard[0] = b.tmark()
            si = 0
            for (s_t, uses) in nz_of_chunk[tcb]:
                w1b = w1s[si % 4]
                wv1 = b.dma("sync", w1b[:],
                            dw1c[128 * s_t:128 * (s_t + 1), :],
                            s_w1[si % 4],
                            pre=[(b.t_ms, w1_guard[si % 4]),
                                 (s_w2bf, w1c_done)])
                bnk = arot[0]
                arot[0] = (arot[0] + 1) % 6
                mms = [(lambda t, s, e, k=k, w1b=w1b, w2s=w2s, bnk=bnk:
                        t.matmul(pb[bnk][:], w1b[:, 128 * k:128 * (k + 1)],
                                 w2s[:, 512 * k:512 * (k + 1)],
                                 start=s, stop=e)) for k in range(KT)]
                rv = b.group(bnk, mms,
                             pre=[(s_w1[si % 4], wv1)] +
                                 ([(b.fre[6], evf[6]), (b.fre[7], evf[7])]
                                  if si == 0 else []))
                tm = b.tmark()
                w1_guard[si % 4] = tm
                w2_guard[0] = tm
                nuses = len(uses)
                for ui, (tt, slot) in enumerate(uses):
                    mi = mki[0] % 2
                    mk = amask[mi]
                    kind, gb3, gv3 = mk_guard[mi]
                    wmk = b.dma("sync", mk[:], dmask[128 * slot:128 * (slot + 1), :],
                                s_mk[mi],
                                pre=[(b.v_ms, gv3) if kind == "v"
                                     else (b.fre[gb3], gv3)])
                    mki[0] += 1
                    qq = tt - 4 * tcb

                    def amul(v, mk=mk, qq=qq, slot=slot, bnk=bnk):
                        return v.tensor_tensor(
                            abuf[:, 128 * slot:128 * (slot + 1)],
                            pb[bnk][:, 128 * qq:128 * (qq + 1)], mk[:], ALU.mult)
                    if ui == nuses - 1:
                        fv = b.consume("vector", bnk, rv, amul,
                                       pre=[(s_mk[mi], wmk)])
                        mk_guard[mi] = ("f", bnk, fv)
                    else:
                        vv = b.vop(amul, pre=[(s_mk[mi], wmk),
                                              (b.rdy[bnk], rv)])
                        mk_guard[mi] = ("v", 0, vv)
                si += 1
        b.barrier()

        lo = Alloc(nc, A_END)
        xn2 = [lo.take(f"yxn{i}", [128, KT * 512], BF16) for i in range(2)]
        gsb = [lo.take(f"ygs{i}", [128, 512], F32) for i in range(4)]
        x1t = [lo.take(f"yx1{i}", [128, 512], F32) for i in range(4)]
        x2t = [lo.take(f"yx2{i}", [128, 512], F32) for i in range(4)]

        s_x2l = b.stream("sync")
        s_x1l = [b.stream("sync") for _ in range(4)]
        s_x2w = [b.stream("gpsimd") for _ in range(4)]
        yrot = [0]
        gs_guard = [0] * 4
        x1_guard = [0] * 4
        x2_guard = [0] * 4
        xc_guard = [0] * 4
        xn2_guard = [0, 0]
        for tk in range(4):
            sl2 = tk % 2
            xw = b.dma("sync", xn2[sl2][:].rearrange("p (k c) -> p k c", c=512),
                       dxn[:, 512 * tk:512 * (tk + 1)]
                       .rearrange("(k p) c -> p k c", p=128),
                       s_x2l, pre=[(b.t_ms, xn2_guard[sl2])])
            for J in range(32):
                pairs = nz_of_t[J]
                assert pairs, "empty attn row-tile unsupported"
                bnk = yrot[0]
                yrot[0] = (yrot[0] + 1) % 6
                mms = [(lambda t, s, e, st=st, slot=slot, bnk=bnk, sl2=sl2:
                        t.matmul(pb[bnk][:],
                                 abuf[:, 128 * slot:128 * (slot + 1)],
                                 xn2[sl2][:, 512 * st:512 * (st + 1)],
                                 start=s, stop=e))
                       for (st, slot) in pairs]
                rv = b.group(bnk, mms, pre=([(s_x2l, xw)] if J == 0 else []))
                gi = J % 4
                gs = gsb[gi]
                fvg = b.consume("scalar", bnk, rv,
                                lambda a, gs=gs, J=J, bnk=bnk: a.activation(
                                    gs[:], pb[bnk][:], AF.Gelu,
                                    bias=sb_b2[:, J:J + 1]),
                                pre=[(b.v_ms, gs_guard[gi])])
                wx1 = b.dma("sync", x1t[gi][:],
                            dsrc[128 * J:128 * (J + 1), 512 * tk:512 * (tk + 1)],
                            s_x1l[gi], pre=[(b.v_ms, x1_guard[gi])])

                def yadd(v, gs=gs, gi=gi):
                    return v.tensor_tensor(x2t[gi][:], x1t[gi][:], gs[:], ALU.add)
                vv = b.vop(yadd, pre=[(b.fre[bnk], fvg), (s_x1l[gi], wx1),
                                      (s_x2w[gi], x2_guard[gi]),
                                      (b.a_ms, xc_guard[gi])])
                gs_guard[gi] = vv
                x1_guard[gi] = vv
                b.dma("gpsimd",
                      ddst[128 * J:128 * (J + 1), 512 * tk:512 * (tk + 1)],
                      x2t[gi][:], s_x2w[gi], pre=[(b.v_ms, vv)])
                x2_guard[gi] = s_x2w[gi].n
                cast_last = b.aop(lambda a_, J=J, gi=gi: a_.activation(
                    ln_xb[:, 512 * J:512 * (J + 1)], x2t[gi][:], AF.Identity),
                    pre=[(b.v_ms, vv)])
                xc_guard[gi] = cast_last
            xn2_guard[sl2] = b.tmark()
            ln_compute(tk, ld_pre=[(b.a_ms, cast_last)])
        b.barrier()

    def emit_final():
        lo = Alloc(nc, PERSIST)
        fst = [lo.take(f"ffs{i}", [128, KT * 128], F32) for i in range(2)]
        ostg = [lo.take(f"fos{i}", [128, D], F32) for i in range(2)]
        s_fl = [b.stream("sync") for _ in range(2)]
        s_fo = [b.stream("gpsimd") for _ in range(2)]
        f_guard = [0, 0]
        o_guard = [0, 0]
        for tt in range(16):
            fs = fst[tt % 2]
            wv = b.dma("sync", fs[:].rearrange("p (dt c) -> p dt c", dt=KT),
                       dxT[3][:, 128 * tt:128 * (tt + 1)]
                       .rearrange("(dt p) c -> p dt c", p=128),
                       s_fl[tt % 2], pre=[(b.t_ms, f_guard[tt % 2])])
            og = ostg[tt % 2]
            fvs = {}
            for bg in range(8):
                bnk = bg % 2 + 2
                wf = b.uses[bnk]
                b.uses[bnk] += 1
                b.rdy[bnk].n += 1
                rv = b.rdy[bnk].n

                def trun(t, bg=bg, bnk=bnk, wf=wf, wv=wv, fs=fs, tt=tt):
                    t.wait_ge(s_fl[tt % 2].h, wv)
                    if wf > 0:
                        t.wait_ge(b.fre[bnk].h, wf)
                    for i in range(4):
                        inst = t.transpose(
                            pb[bnk][:, 128 * i:128 * (i + 1)],
                            fs[:, 128 * (4 * bg + i):128 * (4 * bg + i + 1)],
                            sb_idf[:])
                        if i == 3:
                            inst.then_inc(b.rdy[bnk].h, 1)
                b.emit("tensor", trun)
                fvs[bnk] = b.consume(
                    "vector", bnk, rv,
                    lambda v, og=og, bg=bg, bnk=bnk:
                    v.tensor_copy(og[:, 512 * bg:512 * (bg + 1)], pb[bnk][:]),
                    pre=([(s_fo[tt % 2], o_guard[tt % 2])] if bg == 0 else ()))
            f_guard[tt % 2] = b.tmark()
            b.dma("gpsimd", dout[128 * tt:128 * (tt + 1), :], og[:],
                  s_fo[tt % 2], pre=[(b.fre[2], fvs[2]), (b.fre[3], fvs[3])])
            o_guard[tt % 2] = s_fo[tt % 2].n

    if upto == "p1":
        emit_tap(ddnc.bitcast(F32), 2048, 2048)
        return finish()
    if upto == "gate":
        K_UPTO_GATE[0] = True
    dn_parts = [
        (lambda k=k: emit_dequant(
            DQD, dqw_dn, ddnc, D // 8, D, ds_dn, dzs_dn, MT, True, k, k + 1))
        for k in range(MT)]
    emit_ffn(dxT[0], dxT[1], ln_dst=True, co=dn_parts)
    if upto == "gate":
        emit_tap(dh.bitcast(F32), 4096, 1024)
        return finish()
    if upto == "ffn1":
        emit_tap(dxT[1], D, T)
        return finish()
    emit_attn(dxT[1], dxT[2])
    if upto == "attn":
        emit_tap(dxT[2], D, T)
        return finish()
    emit_ffn(dxT[2], dxT[3])
    emit_final()
    b.barrier()

    with nc.Block() as blk:
        for eng in B.ENGS:
            getattr(blk, eng)(lambda e, eng=eng: [f(e) for f in b.ops[eng]])
    return nc



def _host_prep(inputs):
    mask = np.asarray(inputs["L0_attn_sparse_mask"], dtype=np.float32)
    tiles = mask.reshape(32, 128, 32, 128)
    nzmap = (tiles != 0).any(axis=(1, 3))
    nz_of_t = [[] for _ in range(32)]
    packed = []
    slot = 0
    for tj in range(32):
        for si in range(32):
            if nzmap[si, tj]:
                packed.append(np.ascontiguousarray(tiles[si, :, tj, :]))
                nz_of_t[tj].append((si, slot))
                slot += 1
    mask_nz = (np.concatenate(packed, axis=0) if packed
               else np.zeros((128, 128), np.float32))
    n_mask = max(slot, 1)

    perm_gu = np.concatenate([_pos2dense(INTER), INTER + _pos2dense(INTER)])
    perm_dn = _pos2dense(D)
    s_gu = np.ascontiguousarray(
        np.asarray(inputs["L0_ffn_scales_gu"], np.float32)[:, perm_gu])
    s_dn = np.ascontiguousarray(
        np.asarray(inputs["L0_ffn_scales_dn"], np.float32)[:, perm_dn])

    ind_gu = np.zeros((128, KT * KT), dtype=bfdt)
    for g in range(KT):
        ind_gu[:, KT * g + g] = 1.0
    ind_dn = np.zeros((128, NJ * MT), dtype=bfdt)
    p = np.arange(128)
    for j in range(NJ):
        ind_dn[p, MT * j + 2 * j + (p % 64) // 32] = 1.0

    common = dict(
        w1=np.ascontiguousarray(np.asarray(inputs["L0_attn_w1"], np.float32)),
        w2=np.ascontiguousarray(np.asarray(inputs["L0_attn_w2"], np.float32)),
        b2=np.ascontiguousarray(np.asarray(inputs["L0_attn_b2"], np.float32)),
        mask_nz=np.ascontiguousarray(mask_nz),
        qw_gu=np.ascontiguousarray(np.asarray(inputs["L0_ffn_qweight_gu"], np.int32)),
        qz_gu=np.ascontiguousarray(np.asarray(inputs["L0_ffn_qzeros_gu"], np.int32)),
        s_gu=s_gu,
        qw_dn=np.ascontiguousarray(np.asarray(inputs["L0_ffn_qweight_dn"], np.int32)),
        qz_dn=np.ascontiguousarray(np.asarray(inputs["L0_ffn_qzeros_dn"], np.int32)),
        s_dn=s_dn,
        identf=np.eye(128, dtype=np.float32),
        stair2=np.stack([((np.arange(128) % 64) // 32 == k).astype(bfdt)
                         for k in range(2)]),
        ones_col=np.ones((128, 1), dtype=bfdt),
        ones_row_b=np.ones((1, 128), dtype=bfdt),
        ones_row_f=np.ones((1, 128), dtype=np.float32),
        ind_gu=ind_gu, ind_dn=ind_dn,
    )
    return nz_of_t, n_mask, common


def kernel(**inputs):
    nz_of_t, n_mask, common = _host_prep(inputs)
    x = np.asarray(inputs["L0_attn_x"], np.float32).reshape(SEQ, D)
    nc = build_program(nz_of_t, n_mask)
    in_maps = []
    for c in range(NCORES):
        m = dict(common)
        m["x"] = np.ascontiguousarray(x[T * c:T * (c + 1), :])
        in_maps.append(m)
    res = run_bass_kernel_spmd(nc, in_maps, core_ids=list(range(NCORES)))
    out = np.concatenate([res.results[c]["out"] for c in range(NCORES)], axis=0)
    return out.reshape(1, SEQ, D).astype(np.float32)


# version 19 (final)
# speedup vs baseline: 1.1356x; 1.0067x over previous
import numpy as np
import ml_dtypes
import concourse.bass as bass
import concourse.mybir as mybir
from concourse.bass_utils import run_bass_kernel_spmd

F32 = mybir.dt.float32
BF16 = mybir.dt.bfloat16
I32 = mybir.dt.int32
U16 = mybir.dt.uint16
AF = mybir.ActivationFunctionType
ALU = mybir.AluOpType
bfdt = ml_dtypes.bfloat16

NCORES = 8
D = 4096
SEQ = 16384
T = SEQ // NCORES
INTER = 11008
NG = 2 * INTER
NJ = INTER // 256
SW = 256
MT = 2 * NJ
KT = D // 128
HALF = T // 2
EPS = 1e-5
SBUF_BASE = 16512
SBUF_TOP = 229300


def _pos2dense(n):
    p = np.arange(n)
    j, rem = p // SW, p % SW
    i, r = rem // 64, rem % 64
    return j * SW + 4 * r + i


class Alloc:
    def __init__(self, nc, base, top=SBUF_TOP):
        self.nc, self.off, self.top = nc, base, top

    def take(self, name, shape, dtype, align=64):
        esz = {F32: 4, BF16: 2, I32: 4, U16: 2}[dtype]
        self.off = -(-self.off // align) * align
        h = self.nc.alloc_sbuf_tensor_at(name, list(shape), dtype, offset=self.off)
        self.off += shape[-1] * esz
        assert self.off <= self.top, f"SBUF overflow at {name}: {self.off}"
        return h


class Sem:
    __slots__ = ("h", "n")

    def __init__(self, h):
        self.h, self.n = h, 0

    def inc(self, by=1):
        self.n += by
        return self.n


def _pre(e, pre):
    for s, v in pre:
        if v > 0:
            e.wait_ge(s.h, v)


class B:
    ENGS = ("tensor", "vector", "scalar", "sync", "gpsimd")

    def __init__(self, nc):
        self.nc = nc
        self.ops = {e: [] for e in self.ENGS}
        self._si = 0
        self.rdy = [self.sem() for _ in range(8)]
        self.fre = [self.sem() for _ in range(8)]
        self.uses = [0] * 8
        self.v_ms = self.sem()
        self.a_ms = self.sem()
        self.g_ms = self.sem()
        self.t_ms = self.sem()
        self.bar = self.sem()
        self.qstreams = {"sync": [], "scalar": [], "gpsimd": []}

    def stream(self, queue):
        s = self.sem()
        self.qstreams[queue].append(s)
        return s

    def sem(self):
        h = self.nc.semaphore(f"sm{self._si}").__enter__()
        self._si += 1
        return Sem(h)

    def emit(self, eng, fn):
        self.ops[eng].append(fn)

    def wait(self, eng, sem, val):
        if val > 0:
            self.ops[eng].append(lambda e, s=sem, v=val: e.wait_ge(s.h, v))

    def group(self, bnk, mms, pre=()):
        wf = self.uses[bnk]
        self.uses[bnk] += 1
        self.rdy[bnk].n += 1
        rv = self.rdy[bnk].n

        def run(t, mms=mms, bnk=bnk, wf=wf, pre=tuple(pre)):
            _pre(t, pre)
            if wf > 0:
                t.wait_ge(self.fre[bnk].h, wf)
            n = len(mms)
            for i, m in enumerate(mms):
                inst = m(t, i == 0, i == n - 1)
                if i == n - 1:
                    inst.then_inc(self.rdy[bnk].h, 1)
        self.emit("tensor", run)
        return rv

    def consume(self, eng, bnk, rv, fn, pre=()):
        self.fre[bnk].n += 1

        def run(e, bnk=bnk, rv=rv, fn=fn, pre=tuple(pre)):
            _pre(e, pre)
            e.wait_ge(self.rdy[bnk].h, rv)
            fn(e).then_inc(self.fre[bnk].h, 1)
        self.emit(eng, run)
        return self.fre[bnk].n

    def dma(self, queue, out, in_, sem, pre=(), **kw):
        v = sem.inc(16)

        def run(e, out=out, in_=in_, kw=kw, sem=sem, pre=tuple(pre)):
            _pre(e, pre)
            e.dma_start(out=out, in_=in_, **kw).then_inc(sem.h, 16)
        self.emit(queue, run)
        return v

    def vop(self, fn, pre=()):
        self.v_ms.n += 1

        def run(v, fn=fn, pre=tuple(pre)):
            _pre(v, pre)
            fn(v).then_inc(self.v_ms.h, 1)
        self.emit("vector", run)
        return self.v_ms.n

    def gop(self, fn, pre=()):
        self.g_ms.n += 1

        def run(g, fn=fn, pre=tuple(pre)):
            _pre(g, pre)
            fn(g).then_inc(self.g_ms.h, 1)
        self.emit("gpsimd", run)
        return self.g_ms.n

    def aop(self, fn, pre=()):
        self.a_ms.n += 1

        def run(a, fn=fn, pre=tuple(pre)):
            _pre(a, pre)
            fn(a).then_inc(self.a_ms.h, 1)
        self.emit("scalar", run)
        return self.a_ms.n

    def tmark(self):
        self.t_ms.n += 1
        self.emit("tensor", lambda t: t.nop().then_inc(self.t_ms.h, 1))
        return self.t_ms.n

    def barrier(self):
        tgt = self.bar.n + 5
        self.bar.n = tgt
        snap = {q: [(s, s.n) for s in ss if s.n > 0]
                for q, ss in self.qstreams.items()}
        for eng in self.ENGS:
            def run(e, eng=eng, snap=snap):
                for s, v in snap.get(eng, ()):
                    e.wait_ge(s.h, v)
                if eng in ("vector", "gpsimd"):
                    e.engine_nop().then_inc(self.bar.h, 1)
                else:
                    e.nop().then_inc(self.bar.h, 1)
            self.emit(eng, run)
        for eng in self.ENGS:
            self.wait(eng, self.bar, tgt)


def build_program(nz_of_t, n_mask, upto="full", dbg=()):
    K_UPTO_GATE = [False]
    nc = bass.Bass("TRN2", target_bir_lowering=False, debug=False)
    nc.detect_race_conditions = False

    dx = nc.dram_tensor("x", [T, D], F32, kind="ExternalInput").ap()
    dw1 = nc.dram_tensor("w1", [D, D], F32, kind="ExternalInput").ap()
    dw2 = nc.dram_tensor("w2", [D, D], F32, kind="ExternalInput").ap()
    db2 = nc.dram_tensor("b2", [D], F32, kind="ExternalInput").ap()
    dmask = nc.dram_tensor("mask_nz", [max(n_mask, 1) * 128, 128], F32,
                           kind="ExternalInput").ap()
    dqw_gu = nc.dram_tensor("qw_gu", [D, NG // 8], I32, kind="ExternalInput").ap()
    dqz_gu = nc.dram_tensor("qz_gu", [KT, NG // 8], I32, kind="ExternalInput").ap()
    ds_gu = nc.dram_tensor("s_gu", [KT, NG], F32, kind="ExternalInput").ap()
    dqw_dn = nc.dram_tensor("qw_dn", [INTER, D // 8], I32, kind="ExternalInput").ap()
    dqz_dn = nc.dram_tensor("qz_dn", [MT, D // 8], I32, kind="ExternalInput").ap()
    ds_dn = nc.dram_tensor("s_dn", [MT, D], F32, kind="ExternalInput").ap()
    didf = nc.dram_tensor("identf", [128, 128], F32, kind="ExternalInput").ap()
    dones = nc.dram_tensor("ones_col", [128, 1], BF16, kind="ExternalInput").ap()
    dones1b = nc.dram_tensor("ones_row_b", [1, 128], BF16, kind="ExternalInput").ap()
    dones1f = nc.dram_tensor("ones_row_f", [1, 128], F32, kind="ExternalInput").ap()
    dind_gu = nc.dram_tensor("ind_gu", [128, KT * KT], BF16, kind="ExternalInput").ap()
    dind_dn = nc.dram_tensor("ind_dn", [128, NJ * MT], BF16, kind="ExternalInput").ap()
    dstair2 = nc.dram_tensor("stair2", [2, 128], BF16, kind="ExternalInput").ap()
    dout = nc.dram_tensor("out", [T, D], F32, kind="ExternalOutput").ap()

    dxT = [nc.dram_tensor(f"xT{i}", [D, T], F32, kind="Internal").ap()
           for i in range(4)]
    dxn = nc.dram_tensor("xn_sc", [D, T], BF16, kind="Internal").ap()
    dguc = nc.dram_tensor("gu_cache", [2 * NJ * 128, KT * SW], BF16, kind="Internal").ap()
    ddnc = nc.dram_tensor("dn_cache", [MT * 8 * 128, 512], BF16, kind="Internal").ap()
    dh = nc.dram_tensor("h_sc", [MT * 4 * 128, 512], BF16, kind="Internal").ap()
    dzs_gu = nc.dram_tensor("zs_gu_sc", [KT, NG], BF16, kind="Internal").ap()
    dzs_dn = nc.dram_tensor("zs_dn_sc", [MT, D], BF16, kind="Internal").ap()
    dw2bf = nc.dram_tensor("w2bf", [D, D], BF16, kind="Internal").ap()
    dw1c = nc.dram_tensor("w1c", [32 * 128, KT * 128], BF16, kind="Internal").ap()

    pb = [nc.psum_tensor(f"pbank{i}", [128, 512], F32).__enter__() for i in range(8)]
    b = B(nc)

    al = Alloc(nc, SBUF_BASE)
    sb_idf = al.take("idf", [128, 128], F32)
    sb_ones = al.take("onesc", [128, 1], BF16)
    sb_ones1b = al.take("ones1b", [1, 128], BF16)
    sb_ones1f = al.take("ones1f", [1, 128], F32)
    sb_indgu = al.take("indgu", [128, KT * KT], BF16)
    sb_stair2 = al.take("stair2", [2, 128], BF16)
    sb_b2 = al.take("b2t", [128, 32], F32)
    sb_idfb = al.take("idfb", [128, 128], BF16)
    PERSIST = al.off

    s_const = b.stream("sync")
    s_w2bf = b.stream("gpsimd")
    for out, in_ in ((sb_idf[:], didf), (sb_ones[:], dones),
                     (sb_ones1b[:], dones1b), (sb_ones1f[:], dones1f),
                     (sb_indgu[:], dind_gu), (sb_stair2[:], dstair2)):
        b.dma("sync", out, in_, s_const)
    b.dma("sync", sb_b2[:], db2.rearrange("(j p) -> p j", p=128), s_const,
          allow_slow_non_contiguous=True)
    w1cv = dw1c.rearrange("(st p) (dt c) -> st p dt c", p=128, c=128)
    for st in range(32):
        b.dma("gpsimd", w1cv[st, :, :, :],
              dw1[:, 128 * st:128 * (st + 1)].rearrange("(dt p) c -> p dt c", p=128),
              s_w2bf)
    w1c_done = s_w2bf.n
    b.vop(lambda v: v.tensor_copy(sb_idfb[:], sb_idf[:]),
          pre=[(s_const, s_const.n)])
    for eng in ("tensor", "vector", "scalar"):
        b.wait(eng, s_const, s_const.n)

    def emit_tap(dsrc_bytes_ap, nrows, rowlen_f32):
        lo = Alloc(nc, PERSIST)
        tbuf = lo.take("tapbuf", [128, 2048], F32)
        s_tl = b.stream("sync")
        s_tw = b.stream("gpsimd")
        dflat = dout.rearrange("a c -> (a c)")
        for i in range(0, nrows, 128):
            n = min(128, nrows - i)
            wv = b.dma("sync", tbuf[:n, :rowlen_f32], dsrc_bytes_ap[i:i + n, :],
                       s_tl, pre=[(s_tw, s_tw.n)])
            vv = b.vop(lambda v: v.engine_nop(), pre=[(s_tl, wv)])
            seg = dflat[i * rowlen_f32:(i + n) * rowlen_f32]
            b.dma("gpsimd", seg.rearrange("(p f) -> p f", f=rowlen_f32),
                  tbuf[:n, :rowlen_f32], s_tw, pre=[(b.v_ms, vv)])

    def finish():
        b.barrier()
        with nc.Block() as blk:
            for eng in B.ENGS:
                getattr(blk, eng)(lambda e, eng=eng: [f(e) for f in b.ops[eng]])
        return nc

    lnal = Alloc(nc, SBUF_TOP - 66000)
    ln_xb = lnal.take("lnxb", [128, KT * 512], BF16)
    ln_xq = [lnal.take(f"lnxq{i}", [128, 8 * 512], BF16) for i in range(2)]
    ln_r1 = lnal.take("lnr1", [1, 512], F32)
    ln_r2 = lnal.take("lnr2", [1, 512], F32)
    ln_rt = lnal.take("lnrt", [1, 512], F32)
    ln_rb = lnal.take("lnrb", [128, 512], F32)
    ln_mb = lnal.take("lnmb", [128, 512], F32)
    ln_tmp = lnal.take("lntmp", [128, 512], F32)
    ln_xn = [lnal.take(f"lnxn{i}", [128, 512], BF16) for i in range(4)]
    s_lnl = b.stream("gpsimd")
    s_lw = [b.stream("gpsimd") for _ in range(4)]
    ln_pace = b.sem()
    ln_st = {"xbf": 0, "sq0": 0, "sq1": 0, "xn": [0] * 4}

    def ln_load(dsrc, tc, seg, pre):
        p = list(pre)
        if seg is None or seg == 0:
            p.append((b.v_ms, ln_st["xbf"]))
        if seg is None:
            b.dma("gpsimd",
                  ln_xb[:].rearrange("p (dt c) -> p dt c", c=512),
                  dsrc[:, 512 * tc:512 * (tc + 1)]
                  .rearrange("(dt p) c -> p dt c", p=128),
                  s_lnl, pre=p)
        else:
            b.dma("gpsimd",
                  ln_xb[:, 2048 * seg:2048 * (seg + 1)]
                  .rearrange("p (dt c) -> p dt c", c=512),
                  dsrc[512 * seg:512 * (seg + 1), 512 * tc:512 * (tc + 1)]
                  .rearrange("(dt p) c -> p dt c", p=128),
                  s_lnl, pre=p)

    def ln_compute(tc, ld_pre=None):
        if ld_pre is None:
            ld_pre = [(s_lnl, s_lnl.n)]
        p0p = ln_pace.n
        sqv = []
        for c in range(4):
            if c == 0:
                pre = list(ld_pre)
                if ln_st["sq0"]:
                    pre.append((ln_pace, ln_st["sq0"]))
            elif c == 1:
                pre = [(b.rdy[7], ln_st["sq1"])] if ln_st["sq1"] else []
            elif c == 2:
                pre = [(ln_pace, p0p + 1)]
            else:
                pre = [(ln_pace, p0p + 2)]
            sqv.append(b.aop(lambda a, c=c: a.activation(
                ln_xq[c % 2][:], ln_xb[:, 4096 * c:4096 * (c + 1)], AF.Square),
                pre=pre))
        rv6 = b.group(6, [
            (lambda t, s, e, dt=dt: t.matmul(
                pb[6][:1, :], sb_ones[:], ln_xb[:, 512 * dt:512 * (dt + 1)],
                start=s, stop=e)) for dt in range(KT)],
            pre=list(ld_pre))
        wf7 = b.uses[7]
        b.uses[7] += 1
        b.rdy[7].n += 1
        rv7 = b.rdy[7].n

        def sq_run(t, sqv=tuple(sqv), wf7=wf7):
            if wf7 > 0:
                t.wait_ge(b.fre[7].h, wf7)
            for c in range(4):
                t.wait_ge(b.a_ms.h, sqv[c])
                for i in range(8):
                    dt = 8 * c + i
                    inst = t.matmul(pb[7][:1, :], sb_ones[:],
                                    ln_xq[c % 2][:, 512 * i:512 * (i + 1)],
                                    start=(dt == 0), stop=(dt == KT - 1))
                    if dt == KT - 1:
                        inst.then_inc(b.rdy[7].h, 1)
                    elif i == 7:
                        inst.then_inc(ln_pace.h, 1)
        b.emit("tensor", sq_run)
        ln_pace.n += 3
        ln_st["sq0"] = p0p + 3
        ln_st["sq1"] = rv7
        b.consume("vector", 6, rv6,
                  lambda v: v.tensor_scalar_mul(ln_r1[:], pb[6][:1, :], 1.0 / D))
        b.consume("vector", 7, rv7,
                  lambda v: v.tensor_scalar_mul(ln_r2[:], pb[7][:1, :], 1.0 / D))

        def vstat(v):
            v.tensor_tensor(ln_rt[:], ln_r1[:], ln_r1[:], ALU.mult)
            v.tensor_tensor(ln_r2[:], ln_r2[:], ln_rt[:], ALU.subtract)
            return v.tensor_scalar_add(ln_r2[:], ln_r2[:], EPS)
        vv = b.vop(vstat)
        av = b.aop(lambda a: a.activation(ln_r2[:], ln_r2[:], AF.Sqrt),
                   pre=[(b.v_ms, vv)])

        def vrec(v):
            v.reciprocal(ln_r2[:], ln_r2[:])
            return v.tensor_tensor(ln_r1[:], ln_r1[:], ln_r2[:], ALU.mult)
        vv = b.vop(vrec, pre=[(b.a_ms, av)])
        for bnk, srcr, dst in ((6, ln_r2, ln_rb), (7, ln_r1, ln_mb)):
            rvb = b.group(bnk, [
                lambda t, s, e, bnk=bnk, srcr=srcr: t.matmul(
                    pb[bnk][:], sb_ones1f[:], srcr[:], start=s, stop=e)],
                pre=[(b.v_ms, vv)])
            b.consume("vector", bnk, rvb,
                      lambda v, dst=dst, bnk=bnk: v.tensor_copy(dst[:], pb[bnk][:]))
        lastv = 0
        for dt in range(KT):
            sl = dt % 4

            def vnorm(v, dt=dt, sl=sl):
                v.tensor_tensor(ln_tmp[:], ln_xb[:, 512 * dt:512 * (dt + 1)],
                                ln_rb[:], ALU.mult)
                return v.tensor_tensor(ln_xn[sl][:], ln_tmp[:], ln_mb[:],
                                       ALU.subtract)
            pre = [(s_lw[sl], ln_st["xn"][sl])] if ln_st["xn"][sl] else []
            lastv = b.vop(vnorm, pre=pre)
            b.dma("gpsimd",
                  dxn[128 * dt:128 * (dt + 1), 512 * tc:512 * (tc + 1)],
                  ln_xn[sl][:], s_lw[sl], pre=[(b.v_ms, lastv)])
            ln_st["xn"][sl] = s_lw[sl].n
        ln_st["xbf"] = lastv

    p0 = Alloc(nc, PERSIST)
    sb_xh = [p0.take(f"p0xh{i}", [128, D], F32) for i in range(4)]
    sb_xts = [p0.take(f"p0xts{i}", [128, 512], F32) for i in range(2)]
    sb_qzg = p0.take("p0qzg", [KT, NG // 8], I32)
    sb_zu = p0.take("p0zu", [KT, 4096], U16)
    sb_zb = p0.take("p0zb", [KT, 4096], BF16)
    sb_qzd = p0.take("p0qzd", [MT, D // 8], I32)
    sb_zud = p0.take("p0zud", [MT, D], U16)
    sb_zbd = p0.take("p0zbd", [MT, D], BF16)

    s_px = b.stream("sync")
    s_pw = [b.stream("gpsimd") for _ in range(2)]
    xts_guard = [0, 0]
    xh_guard = [0]
    for tg in range(4):
        wv = [b.dma("sync", sb_xh[i][:],
                    dx[(4 * tg + i) * 128:(4 * tg + i + 1) * 128, :], s_px,
                    pre=([(b.t_ms, xh_guard[0])] if i == 0 else ()))
              for i in range(4)]
        for dt in range(KT):
            bnk = dt % 2
            wf = b.uses[bnk]
            b.uses[bnk] += 1
            b.rdy[bnk].n += 1
            rv = b.rdy[bnk].n

            def trun(t, dt=dt, bnk=bnk, wf=wf, w=wv[3]):
                t.wait_ge(s_px.h, w)
                if wf > 0:
                    t.wait_ge(b.fre[bnk].h, wf)
                for i in range(4):
                    inst = t.transpose(pb[bnk][:, 128 * i:128 * (i + 1)],
                                       sb_xh[i][:, 128 * dt:128 * (dt + 1)],
                                       sb_idf[:])
                    if i == 3:
                        inst.then_inc(b.rdy[bnk].h, 1)
            b.emit("tensor", trun)
            st = sb_xts[dt % 2]
            fv = b.consume("vector", bnk, rv,
                           lambda v, st=st, bnk=bnk: v.tensor_copy(st[:], pb[bnk][:]),
                           pre=[(s_pw[dt % 2], xts_guard[dt % 2])])
            b.dma("gpsimd", dxT[0][128 * dt:128 * (dt + 1), 512 * tg:512 * (tg + 1)],
                  st[:], s_pw[dt % 2], pre=[(b.fre[bnk], fv)])
            xts_guard[dt % 2] = s_pw[dt % 2].n
        xh_guard[0] = b.tmark()
        ln_load(dxT[0], tg, None, [(s_pw[0], s_pw[0].n), (s_pw[1], s_pw[1].n)])
        ln_compute(tg)

    s_zs = b.stream("sync")
    wq = b.dma("sync", sb_qzg[:], dqz_gu, s_zs)
    zb_guard = 0
    for ch in range(6):
        c0 = 4096 * ch
        w = min(4096, NG - c0)

        def zrun(v, c0=c0, w=w):
            qz = sb_qzg[:].bitcast(U16)
            for o in range(0, w, SW):
                u0 = ((c0 + o) // SW) * 64
                for i in range(4):
                    v.tensor_scalar(sb_zu[:, o + 64 * i:o + 64 * (i + 1)],
                                    qz[:, u0:u0 + 64], 4 * i, 15,
                                    ALU.logical_shift_right, ALU.bitwise_and)
            return v.tensor_copy(sb_zb[:, :w], sb_zu[:, :w])
        vv = b.vop(zrun, pre=[(s_zs, max(wq, zb_guard))])
        b.dma("sync", dzs_gu[:, c0:c0 + w], sb_zb[:, :w], s_zs, pre=[(b.v_ms, vv)])
        zb_guard = s_zs.n

    wq = b.dma("sync", sb_qzd[:], dqz_dn, s_zs)

    def zdrun(v):
        qz = sb_qzd[:].bitcast(U16)
        for o in range(0, D, SW):
            u0 = (o // SW) * 64
            for i in range(4):
                v.tensor_scalar(sb_zud[:, o + 64 * i:o + 64 * (i + 1)],
                                qz[:, u0:u0 + 64], 4 * i, 15,
                                ALU.logical_shift_right, ALU.bitwise_and)
        return v.tensor_copy(sb_zbd[:], sb_zud[:])
    vv = b.vop(zdrun, pre=[(s_zs, wq)])
    b.dma("sync", dzs_dn[:], sb_zbd[:], s_zs, pre=[(b.v_ms, vv)])

    b.barrier()

    if upto == "p0":
        emit_tap(dxT[0], D, T)
        return finish()
    if upto == "ln1":
        emit_tap(dxn.bitcast(F32), D, T // 2)
        return finish()

    p1 = Alloc(nc, PERSIST)
    sb_qw = [p1.take(f"dqqw{i}", [128, NG // 8], I32) for i in range(2)]
    sb_srow = [p1.take(f"dqsr{i}", [2, 8192], BF16) for i in range(2)]
    sb_sbc = [p1.take(f"dqsbc{i}", [128, 1024], BF16) for i in range(8)]
    sb_nib = [p1.take(f"dqnib{i}", [128, 512], U16) for i in range(2)]
    sb_nbb = p1.take("dqnbb", [128, 512], BF16)
    sb_wst = [p1.take(f"dqwst{i}", [128, 2048], BF16) for i in range(2)]

    s_qw = [b.stream("sync") for _ in range(2)]
    s_sr = [b.stream("gpsimd") for _ in range(2)]
    s_wc = [b.stream("sync") for _ in range(2)]
    DQ = dict(qw=sb_qw, srow=sb_srow, sbc=sb_sbc, nib=sb_nib, nbb=sb_nbb,
              wst=sb_wst, s_qw=s_qw, s_sr=s_sr, s_wc=s_wc, q_wc="sync",
              qw_guard=[0, 0], srow_guard=[(0, 0), (0, 0)],
              sbc_guard=[0] * 8, wst_guard=[0, 0], subctr=[0], banks=None)

    dnal = Alloc(nc, 168000)
    DQD = dict(qw=[dnal.take(f"dnqw{i}", [128, D // 8], I32) for i in range(2)],
               srow=[dnal.take(f"dnsr{i}", [2, 4096], BF16) for i in range(2)],
               sbc=[dnal.take(f"dnsbc{i}", [128, 1024], BF16) for i in range(8)],
               nib=[dnal.take(f"dnnib{i}", [128, 512], U16) for i in range(2)],
               nbb=dnal.take("dnnbb", [128, 512], BF16),
               wst=[dnal.take(f"dnwst{i}", [128, 2048], BF16) for i in range(2)],
               s_qw=[b.stream("sync") for _ in range(2)],
               s_sr=[b.stream("gpsimd") for _ in range(2)],
               s_wc=[b.stream("gpsimd") for _ in range(2)],
               q_wc="gpsimd",
               qw_guard=[0, 0], srow_guard=[(0, 0), (0, 0)],
               sbc_guard=[0] * 8, wst_guard=[0, 0], subctr=[0], banks=(6, 7))

    def emit_dequant(S, dqw, dcache, n_i32, n_pos, dsrows, dzrows, n_tiles,
                     dn_mode, g0=0, g1=None):
        if g1 is None:
            g1 = n_tiles
        qsel = (dqw.rearrange("(j r s) c -> j s r c", r=64, s=4) if dn_mode else None)
        sb_qw_, sb_srow_, sb_sbc_ = S["qw"], S["srow"], S["sbc"]
        sb_nib_, sb_nbb_, sb_wst_ = S["nib"], S["nbb"], S["wst"]
        s_qw_, s_sr_, s_wc_ = S["s_qw"], S["s_sr"], S["s_wc"]
        qw_guard, srow_guard = S["qw_guard"], S["srow_guard"]
        sbc_guard, wst_guard = S["sbc_guard"], S["wst_guard"]
        subctr = S["subctr"]
        qwv = S.setdefault("qwv", {})

        def qload(g):
            if g in qwv or g >= n_tiles:
                return
            qs = sb_qw_[g % 2]
            if dn_mode:
                j, q = g // 2, g % 2
                qwv[g] = b.dma("sync", qs[:, :n_i32],
                               qsel[j, 2 * q:2 * q + 2, :, :],
                               s_qw_[g % 2], pre=[(b.v_ms, qw_guard[g % 2])])
            else:
                qwv[g] = b.dma("sync", qs[:, :n_i32],
                               dqw[128 * g:128 * (g + 1), :],
                               s_qw_[g % 2], pre=[(b.v_ms, qw_guard[g % 2])])

        def srload(g, ch, w):
            chi = (ch // 2048) % 2
            sr = sb_srow_[chi]
            gb, gv = srow_guard[chi]
            nrow = 2 if dn_mode else 1
            gg = (g // 2) * 2 if dn_mode else g
            srv = sr[:nrow, :].rearrange("a (u two c) -> a u two c", two=2, c=512)
            wsv = b.dma("gpsimd", srv[:, :w // 512, 0, :],
                        dsrows[gg:gg + nrow, ch:ch + w]
                        .rearrange("a (u c) -> a u c", c=512),
                        s_sr_[chi], pre=[(b.rdy[gb], gv)])
            wzv = b.dma("gpsimd", srv[:, :w // 512, 1, :],
                        dzrows[gg:gg + nrow, ch:ch + w]
                        .rearrange("a (u c) -> a u c", c=512),
                        s_sr_[chi])
            return (wsv, wzv)

        for g in range(g0, g1):
            qload(g)
            qs = sb_qw_[g % 2]
            srw = {}
            if dn_mode:
                for ch in range(0, n_pos, 2048):
                    srw[ch] = srload(g, ch, min(2048, n_pos - ch))
            qload(g + 1)
            for ch in range(0, n_pos, 2048):
                w = min(2048, n_pos - ch)
                chi = (ch // 2048) % 2
                sr = sb_srow_[chi]
                nrow = 2 if dn_mode else 1
                wsv, wzv = srw[ch] if ch in srw else srload(g, ch, w)
                wstg = sb_wst_[chi]
                last_rv = (6, 0)
                for sub in range(0, w, 512):
                    si = subctr[0] % 8
                    if S["banks"] is not None:
                        bnk = 4 + 2 * (subctr[0] % 2)
                        bnkz = bnk + 1
                    else:
                        bnk = (2 * subctr[0]) % 8
                        bnkz = bnk + 1
                    subctr[0] += 1
                    sbc = sb_sbc_[si]
                    lhsb = sb_stair2[:] if dn_mode else sb_ones1b[:]
                    rv = b.group(bnk, [
                        (lambda t, s, e, sr=sr, sub=sub, bnk=bnk, lhsb=lhsb,
                         nrow=nrow:
                         t.matmul(pb[bnk][:], lhsb,
                                  sr[:nrow, 2 * sub:2 * sub + 512],
                                  start=s, stop=e))],
                        pre=[(s_sr_[chi], wzv)])
                    rvz = b.group(bnkz, [
                        (lambda t, s, e, sr=sr, sub=sub, bnkz=bnkz, lhsb=lhsb,
                         nrow=nrow:
                         t.matmul(pb[bnkz][:], lhsb,
                                  sr[:nrow, 2 * sub + 512:2 * sub + 1024],
                                  start=s, stop=e))])
                    last_rv = (bnkz, rvz)
                    fv = b.consume("scalar", bnk, rv,
                                   lambda a, sbc=sbc, bnk=bnk:
                                   a.activation(sbc[:, :512], pb[bnk][:],
                                                AF.Identity),
                                   pre=[(b.v_ms, sbc_guard[si])])
                    fvz = b.consume("scalar", bnkz, rvz,
                                    lambda a, sbc=sbc, bnkz=bnkz:
                                    a.activation(sbc[:, 512:], pb[bnkz][:],
                                                 AF.Identity))

                    nib = sb_nib_[subctr[0] % 2]

                    def drun(v, qs=qs, sbc=sbc, pos=ch + sub, sub=sub,
                             nib=nib, n_i32=n_i32, wstg=wstg):
                        qu = qs[:, :n_i32].bitcast(U16)
                        for jj in range(2):
                            u0 = ((pos + SW * jj) // SW) * 64
                            for i in range(4):
                                v.tensor_scalar(
                                    nib[:, SW * jj + 64 * i:SW * jj + 64 * (i + 1)],
                                    qu[:, u0:u0 + 64], 4 * i, 15,
                                    ALU.logical_shift_right, ALU.bitwise_and)
                        zbv = sbc[:].rearrange("p (u two c) -> p u two c",
                                               two=2, c=512)
                        v.tensor_tensor(sb_nbb_[:], nib[:], zbv[:, 0, 1, :],
                                        ALU.subtract)
                        return v.tensor_tensor(wstg[:, sub:sub + 512],
                                               sb_nbb_[:], zbv[:, 0, 0, :],
                                               ALU.mult)
                    vv = b.vop(drun, pre=[(s_qw_[g % 2], qwv[g]),
                                          (s_wc_[chi], wst_guard[chi]),
                                          (b.fre[bnk], fv), (b.fre[bnkz], fvz)])
                    sbc_guard[si] = vv
                srow_guard[chi] = last_rv
                vvw = b.v_ms.n
                if not dn_mode:
                    g4 = dcache.rearrange("(s p) (kt c) -> s p kt c", p=128, kt=KT)
                    for o in range(0, w, SW):
                        s_idx = (ch + o) // SW
                        b.dma(S["q_wc"], g4[s_idx, :, g, :],
                              wstg[:, o:o + SW], s_wc_[chi],
                              pre=([(b.v_ms, vvw)] if o == 0 else ()))
                else:
                    d4 = dcache.rearrange("(mt oc p) c -> mt oc p c", oc=8, p=128)
                    for o in range(0, w, 512):
                        b.dma(S["q_wc"], d4[g, (ch + o) // 512, :, :],
                              wstg[:, o:o + 512], s_wc_[chi],
                              pre=([(b.v_ms, vvw)] if o == 0 else ()))
                wst_guard[chi] = s_wc_[chi].n
            qw_guard[g % 2] = b.v_ms.n

    emit_dequant(DQ, dqw_gu, dguc, NG // 8, NG, ds_gu, dzs_gu, KT, False)
    b.barrier()

    s_gx = b.stream("sync")
    s_rg = [b.stream("sync") for _ in range(4)]
    s_hw = [b.stream("gpsimd") for _ in range(4)]
    s_hh = b.stream("sync")
    s_wd = [b.stream("scalar") for _ in range(8)]
    s_xr = b.stream("gpsimd")
    s_os = [b.stream("gpsimd") for _ in range(8)]

    def emit_ffn(dsrc, ddst, ln_dst=False, co=None):
        lo = Alloc(nc, PERSIST)
        xnh = lo.take("gxn", [128, KT * 1024], BF16)
        ring = [lo.take(f"grg{i}", [128, KT * SW], BF16) for i in range(4)]
        silub = [lo.take(f"gsl{i}", [128, 2048], BF16) for i in range(2)]
        hstg = [lo.take(f"ghs{i}", [128, 1024], BF16) for i in range(4)]

        ring_guard = [(0, 0)] * 4
        silu_guard = [(0, 0), (0, 0)]
        silu_fv = {}
        hmul_fv = {}
        hst_guard = [0, 0, 0, 0]
        rot = [0]

        for half in range(2):
            pre0 = ([(b.rdy[k], b.rdy[k].n) for k in range(8)] if half == 1 else [])
            xwv = [b.dma("sync", xnh[:, 1024 * k:1024 * (k + 1)],
                         dxn[128 * k:128 * (k + 1), HALF * half:HALF * (half + 1)],
                         s_gx, pre=(pre0 if k == 0 else ()))
                   for k in range(KT)]

            for jp in range(2 * NJ):
                if co and (half * 2 * NJ + jp) % 2 == 0:
                    co.pop(0)()
                j, side = jp // 2, jp % 2
                slot = ring[jp % 4]
                pos0 = side * INTER + SW * j
                gb, gv = ring_guard[jp % 4]
                s_idx = pos0 // SW
                wv = b.dma("sync", slot[:],
                           dguc[128 * s_idx:128 * (s_idx + 1), :],
                           s_rg[jp % 4], pre=[(b.rdy[gb], gv)])
                for q in range(2):
                    for tc in range(2):
                        bnk = rot[0]
                        rot[0] = (rot[0] + 1) % 4
                        mms = [(lambda t, s, e, k=k, q=q, tc=tc, slot=slot, bnk=bnk:
                                t.matmul(pb[bnk][:],
                                         slot[:, SW * k + 128 * q:SW * k + 128 * (q + 1)],
                                         xnh[:, 1024 * k + 512 * tc:1024 * k + 512 * (tc + 1)],
                                         start=s, stop=e))
                               for k in range(KT)]
                        rv = b.group(bnk, mms,
                                     pre=([(s_rg[jp % 4], wv), (s_gx, xwv[-1])]
                                          if (q == 0 and tc == 0) else ()))
                        ring_guard[jp % 4] = (bnk, rv)
                        if side == 0:
                            sl = silub[j % 2]
                            gb2, gv2 = silu_guard[j % 2]
                            sfv = b.consume("scalar", bnk, rv,
                                            lambda a, sl=sl, q=q, tc=tc, bnk=bnk:
                                            a.activation(
                                                sl[:, 1024 * q + 512 * tc:1024 * q + 512 * (tc + 1)],
                                                pb[bnk][:], AF.Silu),
                                            pre=([(b.fre[gb2], gv2)]
                                                 if (q == 0 and tc == 0) else ()))
                            silu_fv[(q, tc)] = (bnk, sfv)
                        else:
                            hs = hstg[2 * (j % 2) + q]
                            sl = silub[j % 2]

                            def hmul(v, hs=hs, sl=sl, q=q, tc=tc, bnk=bnk):
                                return v.tensor_tensor(
                                    hs[:, 512 * tc:512 * (tc + 1)], pb[bnk][:],
                                    sl[:, 1024 * q + 512 * tc:1024 * q + 512 * (tc + 1)],
                                    ALU.mult)
                            sb_, sv_ = silu_fv[(q, tc)]
                            fv = b.consume("vector", bnk, rv, hmul,
                                           pre=([(b.fre[sb_], sv_),
                                                 (s_hw[2 * (j % 2) + q],
                                                  hst_guard[2 * (j % 2) + q])]
                                                if tc == 0 else [(b.fre[sb_], sv_)]))
                            hmul_fv[(j % 2, q)] = (bnk, fv)
                            if tc == 1:
                                silu_guard[j % 2] = (bnk, fv)
                if side == 1:
                    dh4 = dh.rearrange("(mt tq p) c -> mt tq p c", tq=4, p=128)
                    for q in range(2):
                        hb, hv = hmul_fv[(j % 2, q)]
                        for tc in range(2):
                            b.dma("gpsimd",
                                  dh4[2 * j + q, 2 * half + tc, :, :],
                                  hstg[2 * (j % 2) + q][:, 512 * tc:512 * (tc + 1)],
                                  s_hw[2 * (j % 2) + q],
                                  pre=([(b.fre[hb], hv)] if tc == 0 else ()))
                        hst_guard[2 * (j % 2) + q] = s_hw[2 * (j % 2) + q].n

        while co:
            co.pop(0)()
        b.barrier()
        if K_UPTO_GATE[0]:
            return

        lo = Alloc(nc, PERSIST)
        hh = lo.take("dnh", [128, MT * 512], BF16)
        wdn = [lo.take(f"dnw{i}", [128, 512], BF16) for i in range(8)]
        xres = [lo.take(f"dnxr{i}", [128, 512], F32) for i in range(8)]
        ost = [lo.take(f"dnos{i}", [128, 512], F32) for i in range(8)]

        wdn_guard = [0] * 8
        xres_guard = [0] * 8
        ost_guard = [0] * 8
        dsel = ddnc.rearrange("(mt oc p) c -> mt oc p c", oc=8, p=128)
        dh4 = dh.rearrange("(mt tq p) c -> mt tq p c", tq=4, p=128)
        xsel = dsrc.rearrange("(c r s) t -> c s r t", r=64, s=4)
        osel = ddst.rearrange("(c r s) t -> c s r t", r=64, s=4)

        for tq in range(4):
            pre0 = [(b.rdy[k], b.rdy[k].n) for k in range(8)]
            hwv = [b.dma("sync", hh[:, 512 * mt:512 * (mt + 1)],
                         dh4[mt, tq, :, :],
                         s_hh, pre=(pre0 if mt == 0 else ()))
                   for mt in range(MT)]

            for og in range(8):
                base = 4 * (og % 2)
                banks = [base + i for i in range(4)]
                for ot in range(4):
                    bnk = banks[ot]
                    jj, q = 2 * og + ot // 2, ot % 2
                    b.dma("gpsimd", xres[bnk][:],
                          xsel[jj, 2 * q:2 * q + 2, :, 512 * tq:512 * (tq + 1)],
                          s_xr, pre=[(b.fre[bnk], xres_guard[bnk])])
                xr_done = s_xr.n
                open_wf = []
                rvs = {}
                for bnk in banks:
                    open_wf.append((bnk, b.uses[bnk]))
                    b.uses[bnk] += 1
                    b.rdy[bnk].n += 1
                    rvs[bnk] = b.rdy[bnk].n
                for mt in range(MT):
                    ws = wdn[mt % 8]
                    wv = b.dma("scalar", ws[:], dsel[mt, og, :, :],
                               s_wd[mt % 8], pre=[(b.t_ms, wdn_guard[mt % 8])])

                    def dmm(t, mt=mt, ws=ws, wv=wv, banks=tuple(banks),
                            open_wf=tuple(open_wf), hv=hwv[-1]):
                        t.wait_ge(s_wd[mt % 8].h, wv)
                        if mt == 0:
                            t.wait_ge(s_hh.h, hv)
                            for bnk, wf in open_wf:
                                if wf > 0:
                                    t.wait_ge(b.fre[bnk].h, wf)
                        for ot in range(4):
                            inst = t.matmul(
                                pb[banks[ot]][:], ws[:, 128 * ot:128 * (ot + 1)],
                                hh[:, 512 * mt:512 * (mt + 1)],
                                start=(mt == 0), stop=(mt == MT - 1))
                            if mt == MT - 1:
                                inst.then_inc(b.rdy[banks[ot]].h, 1)
                        if mt != MT - 1:
                            inst.then_inc(b.t_ms.h, 1)
                        else:
                            t.nop().then_inc(b.t_ms.h, 1)
                    b.t_ms.n += 1
                    b.emit("tensor", dmm)
                    wdn_guard[mt % 8] = b.t_ms.n
                for ot in range(4):
                    bnk = banks[ot]
                    jj, q = 2 * og + ot // 2, ot % 2
                    if "noxres" in dbg:
                        fv = b.consume("vector", bnk, rvs[bnk],
                                       lambda v, bnk=bnk: v.tensor_scalar_mul(
                                           ost[bnk][:], pb[bnk][:], 0.5),
                                       pre=[(s_xr, xr_done),
                                            (s_os[bnk], ost_guard[bnk])])
                    else:
                        fv = b.consume("vector", bnk, rvs[bnk],
                                       lambda v, bnk=bnk: v.scalar_tensor_tensor(
                                           ost[bnk][:], pb[bnk][:], 0.5, xres[bnk][:],
                                           ALU.mult, ALU.add),
                                       pre=[(s_xr, xr_done),
                                            (s_os[bnk], ost_guard[bnk])])
                    xres_guard[bnk] = fv
                    b.dma("gpsimd",
                          osel[jj, 2 * q:2 * q + 2, :, 512 * tq:512 * (tq + 1)],
                          ost[bnk][:], s_os[bnk], pre=[(b.fre[bnk], fv)])
                    ost_guard[bnk] = s_os[bnk].n
                if ln_dst:
                    ln_load(ddst, tq, og,
                            [(s_os[bk], ost_guard[bk]) for bk in banks])
            if ln_dst:
                ln_compute(tq)
        b.barrier()

    def emit_attn(dsrc, ddst):
        lo = Alloc(nc, PERSIST)
        abuf = lo.take("abuf", [128, max(n_mask, 1) * 128], BF16)
        A_END = lo.off
        w1s = [lo.take(f"aw1{i}", [128, KT * 128], BF16) for i in range(4)]
        w2t = lo.take("aw2", [128, KT * 512], BF16)
        w2bs = [lo.take(f"aw2b{i}", [128, 4096], F32) for i in range(4)]
        amask = [lo.take(f"amk{i}", [128, 128], F32) for i in range(2)]

        nz_of_chunk = []
        for tcb in range(8):
            ss = {}
            for tt in range(4 * tcb, 4 * tcb + 4):
                for (s_t, slot) in nz_of_t[tt]:
                    ss.setdefault(s_t, []).append((tt, slot))
            nz_of_chunk.append(sorted(ss.items()))

        s_w2 = [b.stream("sync") for _ in range(2)]
        s_w1 = [b.stream("sync") for _ in range(4)]
        s_mk = [b.stream("sync") for _ in range(2)]
        w1_guard = [0] * 4
        w2_guard = [0]
        w2b_guard = [0]
        mk_guard = [("v", 0, 0), ("v", 0, 0)]
        arot = [0]
        mki = [0]
        for tcb in range(8):
            w2s = w2t
            wv2b = [b.dma("sync", w2bs[a][:],
                          dw2[512 * tcb + 128 * a:512 * tcb + 128 * (a + 1), :],
                          s_w2[a % 2],
                          pre=([(b.t_ms, w2b_guard[0])] if a == 0 else ()))
                    for a in range(4)]
            ldw = [(s_w2[0], s_w2[0].n), (s_w2[1], s_w2[1].n)]
            evf = {}
            for dt in range(KT):
                bnk = 6 + (dt % 2)
                wf = b.uses[bnk]
                b.uses[bnk] += 1
                b.rdy[bnk].n += 1
                rv = b.rdy[bnk].n

                def w2run(t, dt=dt, bnk=bnk, wf=wf, ldw=tuple(ldw)):
                    if dt == 0:
                        for s_, v_ in ldw:
                            t.wait_ge(s_.h, v_)
                    if wf > 0:
                        t.wait_ge(b.fre[bnk].h, wf)
                    for a in range(4):
                        inst = t.transpose(pb[bnk][:, 128 * a:128 * (a + 1)],
                                           w2bs[a][:, 128 * dt:128 * (dt + 1)],
                                           sb_idf[:])
                        if a == 3:
                            inst.then_inc(b.rdy[bnk].h, 1)
                b.emit("tensor", w2run)
                evf[bnk] = b.consume(
                    "scalar", bnk, rv,
                    lambda a_, dt=dt, bnk=bnk: a_.activation(
                        w2s[:, 512 * dt:512 * (dt + 1)], pb[bnk][:], AF.Identity),
                    pre=([(b.t_ms, w2_guard[0])] if dt == 0 else ()))
            w2b_guard[0] = b.tmark()
            si = 0
            for (s_t, uses) in nz_of_chunk[tcb]:
                w1b = w1s[si % 4]
                wv1 = b.dma("sync", w1b[:],
                            dw1c[128 * s_t:128 * (s_t + 1), :],
                            s_w1[si % 4],
                            pre=[(b.t_ms, w1_guard[si % 4]),
                                 (s_w2bf, w1c_done)])
                bnk = arot[0]
                arot[0] = (arot[0] + 1) % 6
                mms = [(lambda t, s, e, k=k, w1b=w1b, w2s=w2s, bnk=bnk:
                        t.matmul(pb[bnk][:], w1b[:, 128 * k:128 * (k + 1)],
                                 w2s[:, 512 * k:512 * (k + 1)],
                                 start=s, stop=e)) for k in range(KT)]
                rv = b.group(bnk, mms,
                             pre=[(s_w1[si % 4], wv1)] +
                                 ([(b.fre[6], evf[6]), (b.fre[7], evf[7])]
                                  if si == 0 else []))
                tm = b.tmark()
                w1_guard[si % 4] = tm
                w2_guard[0] = tm
                nuses = len(uses)
                for ui, (tt, slot) in enumerate(uses):
                    mi = mki[0] % 2
                    mk = amask[mi]
                    kind, gb3, gv3 = mk_guard[mi]
                    wmk = b.dma("sync", mk[:], dmask[128 * slot:128 * (slot + 1), :],
                                s_mk[mi],
                                pre=[(b.v_ms, gv3) if kind == "v"
                                     else (b.fre[gb3], gv3)])
                    mki[0] += 1
                    qq = tt - 4 * tcb

                    def amul(v, mk=mk, qq=qq, slot=slot, bnk=bnk):
                        return v.tensor_tensor(
                            abuf[:, 128 * slot:128 * (slot + 1)],
                            pb[bnk][:, 128 * qq:128 * (qq + 1)], mk[:], ALU.mult)
                    if ui == nuses - 1:
                        fv = b.consume("vector", bnk, rv, amul,
                                       pre=[(s_mk[mi], wmk)])
                        mk_guard[mi] = ("f", bnk, fv)
                    else:
                        vv = b.vop(amul, pre=[(s_mk[mi], wmk),
                                              (b.rdy[bnk], rv)])
                        mk_guard[mi] = ("v", 0, vv)
                si += 1
        b.barrier()

        lo = Alloc(nc, A_END)
        xn2 = [lo.take(f"yxn{i}", [128, KT * 512], BF16) for i in range(2)]
        gsb = [lo.take(f"ygs{i}", [128, 512], F32) for i in range(4)]
        x1t = [lo.take(f"yx1{i}", [128, 512], F32) for i in range(4)]
        x2t = [lo.take(f"yx2{i}", [128, 512], F32) for i in range(4)]

        s_x2l = b.stream("sync")
        s_x1l = [b.stream("sync") for _ in range(4)]
        s_x2w = [b.stream("gpsimd") for _ in range(4)]
        yrot = [0]
        gs_guard = [0] * 4
        x1_guard = [0] * 4
        x2_guard = [0] * 4
        xc_guard = [0] * 4
        xn2_guard = [0, 0]
        for tk in range(4):
            sl2 = tk % 2
            xw = b.dma("sync", xn2[sl2][:].rearrange("p (k c) -> p k c", c=512),
                       dxn[:, 512 * tk:512 * (tk + 1)]
                       .rearrange("(k p) c -> p k c", p=128),
                       s_x2l, pre=[(b.t_ms, xn2_guard[sl2])])
            for J in range(32):
                pairs = nz_of_t[J]
                assert pairs, "empty attn row-tile unsupported"
                bnk = yrot[0]
                yrot[0] = (yrot[0] + 1) % 6
                mms = [(lambda t, s, e, st=st, slot=slot, bnk=bnk, sl2=sl2:
                        t.matmul(pb[bnk][:],
                                 abuf[:, 128 * slot:128 * (slot + 1)],
                                 xn2[sl2][:, 512 * st:512 * (st + 1)],
                                 start=s, stop=e))
                       for (st, slot) in pairs]
                rv = b.group(bnk, mms, pre=([(s_x2l, xw)] if J == 0 else []))
                gi = J % 4
                gs = gsb[gi]
                fvg = b.consume("scalar", bnk, rv,
                                lambda a, gs=gs, J=J, bnk=bnk: a.activation(
                                    gs[:], pb[bnk][:], AF.Gelu,
                                    bias=sb_b2[:, J:J + 1]),
                                pre=[(b.v_ms, gs_guard[gi])])
                wx1 = b.dma("sync", x1t[gi][:],
                            dsrc[128 * J:128 * (J + 1), 512 * tk:512 * (tk + 1)],
                            s_x1l[gi], pre=[(b.v_ms, x1_guard[gi])])

                def yadd(v, gs=gs, gi=gi):
                    return v.tensor_tensor(x2t[gi][:], x1t[gi][:], gs[:], ALU.add)
                vv = b.vop(yadd, pre=[(b.fre[bnk], fvg), (s_x1l[gi], wx1),
                                      (s_x2w[gi], x2_guard[gi]),
                                      (b.a_ms, xc_guard[gi])])
                gs_guard[gi] = vv
                x1_guard[gi] = vv
                b.dma("gpsimd",
                      ddst[128 * J:128 * (J + 1), 512 * tk:512 * (tk + 1)],
                      x2t[gi][:], s_x2w[gi], pre=[(b.v_ms, vv)])
                x2_guard[gi] = s_x2w[gi].n
                cast_last = b.aop(lambda a_, J=J, gi=gi: a_.activation(
                    ln_xb[:, 512 * J:512 * (J + 1)], x2t[gi][:], AF.Identity),
                    pre=[(b.v_ms, vv)])
                xc_guard[gi] = cast_last
            xn2_guard[sl2] = b.tmark()
            ln_compute(tk, ld_pre=[(b.a_ms, cast_last)])
        b.barrier()

    def emit_final():
        lo = Alloc(nc, PERSIST)
        fst = [lo.take(f"ffs{i}", [128, KT * 128], F32) for i in range(2)]
        ostg = [lo.take(f"fos{i}", [128, D], F32) for i in range(2)]
        s_fl = [b.stream("sync") for _ in range(2)]
        s_fo = [b.stream("gpsimd") for _ in range(2)]
        f_guard = [0, 0]
        o_guard = [0, 0]
        for tt in range(16):
            fs = fst[tt % 2]
            wv = b.dma("sync", fs[:].rearrange("p (dt c) -> p dt c", dt=KT),
                       dxT[3][:, 128 * tt:128 * (tt + 1)]
                       .rearrange("(dt p) c -> p dt c", p=128),
                       s_fl[tt % 2], pre=[(b.t_ms, f_guard[tt % 2])])
            og = ostg[tt % 2]
            fvs = {}
            for bg in range(8):
                bnk = bg % 2 + 2
                wf = b.uses[bnk]
                b.uses[bnk] += 1
                b.rdy[bnk].n += 1
                rv = b.rdy[bnk].n

                def trun(t, bg=bg, bnk=bnk, wf=wf, wv=wv, fs=fs, tt=tt):
                    t.wait_ge(s_fl[tt % 2].h, wv)
                    if wf > 0:
                        t.wait_ge(b.fre[bnk].h, wf)
                    for i in range(4):
                        inst = t.transpose(
                            pb[bnk][:, 128 * i:128 * (i + 1)],
                            fs[:, 128 * (4 * bg + i):128 * (4 * bg + i + 1)],
                            sb_idf[:])
                        if i == 3:
                            inst.then_inc(b.rdy[bnk].h, 1)
                b.emit("tensor", trun)
                fvs[bnk] = b.consume(
                    "vector", bnk, rv,
                    lambda v, og=og, bg=bg, bnk=bnk:
                    v.tensor_copy(og[:, 512 * bg:512 * (bg + 1)], pb[bnk][:]),
                    pre=([(s_fo[tt % 2], o_guard[tt % 2])] if bg == 0 else ()))
            f_guard[tt % 2] = b.tmark()
            b.dma("gpsimd", dout[128 * tt:128 * (tt + 1), :], og[:],
                  s_fo[tt % 2], pre=[(b.fre[2], fvs[2]), (b.fre[3], fvs[3])])
            o_guard[tt % 2] = s_fo[tt % 2].n

    if upto == "p1":
        emit_tap(ddnc.bitcast(F32), 2048, 2048)
        return finish()
    if upto == "gate":
        K_UPTO_GATE[0] = True
    dn_parts = [
        (lambda k=k: emit_dequant(
            DQD, dqw_dn, ddnc, D // 8, D, ds_dn, dzs_dn, MT, True, k, k + 1))
        for k in range(MT)]
    emit_ffn(dxT[0], dxT[1], ln_dst=True, co=dn_parts)
    if upto == "gate":
        emit_tap(dh.bitcast(F32), 4096, 1024)
        return finish()
    if upto == "ffn1":
        emit_tap(dxT[1], D, T)
        return finish()
    emit_attn(dxT[1], dxT[2])
    if upto == "attn":
        emit_tap(dxT[2], D, T)
        return finish()
    emit_ffn(dxT[2], dxT[3])
    emit_final()
    b.barrier()

    with nc.Block() as blk:
        for eng in B.ENGS:
            getattr(blk, eng)(lambda e, eng=eng: [f(e) for f in b.ops[eng]])
    return nc



def _host_prep(inputs):
    mask = np.asarray(inputs["L0_attn_sparse_mask"], dtype=np.float32)
    tiles = mask.reshape(32, 128, 32, 128)
    nzmap = (tiles != 0).any(axis=(1, 3))
    nz_of_t = [[] for _ in range(32)]
    packed = []
    slot = 0
    for tj in range(32):
        for si in range(32):
            if nzmap[si, tj]:
                packed.append(np.ascontiguousarray(tiles[si, :, tj, :]))
                nz_of_t[tj].append((si, slot))
                slot += 1
    mask_nz = (np.concatenate(packed, axis=0) if packed
               else np.zeros((128, 128), np.float32))
    n_mask = max(slot, 1)

    perm_gu = np.concatenate([_pos2dense(INTER), INTER + _pos2dense(INTER)])
    perm_dn = _pos2dense(D)
    s_gu = np.ascontiguousarray(
        np.asarray(inputs["L0_ffn_scales_gu"], np.float32)[:, perm_gu])
    s_dn = np.ascontiguousarray(
        np.asarray(inputs["L0_ffn_scales_dn"], np.float32)[:, perm_dn])

    ind_gu = np.zeros((128, KT * KT), dtype=bfdt)
    for g in range(KT):
        ind_gu[:, KT * g + g] = 1.0
    ind_dn = np.zeros((128, NJ * MT), dtype=bfdt)
    p = np.arange(128)
    for j in range(NJ):
        ind_dn[p, MT * j + 2 * j + (p % 64) // 32] = 1.0

    common = dict(
        w1=np.ascontiguousarray(np.asarray(inputs["L0_attn_w1"], np.float32)),
        w2=np.ascontiguousarray(np.asarray(inputs["L0_attn_w2"], np.float32)),
        b2=np.ascontiguousarray(np.asarray(inputs["L0_attn_b2"], np.float32)),
        mask_nz=np.ascontiguousarray(mask_nz),
        qw_gu=np.ascontiguousarray(np.asarray(inputs["L0_ffn_qweight_gu"], np.int32)),
        qz_gu=np.ascontiguousarray(np.asarray(inputs["L0_ffn_qzeros_gu"], np.int32)),
        s_gu=s_gu,
        qw_dn=np.ascontiguousarray(np.asarray(inputs["L0_ffn_qweight_dn"], np.int32)),
        qz_dn=np.ascontiguousarray(np.asarray(inputs["L0_ffn_qzeros_dn"], np.int32)),
        s_dn=s_dn,
        identf=np.eye(128, dtype=np.float32),
        stair2=np.stack([((np.arange(128) % 64) // 32 == k).astype(bfdt)
                         for k in range(2)]),
        ones_col=np.ones((128, 1), dtype=bfdt),
        ones_row_b=np.ones((1, 128), dtype=bfdt),
        ones_row_f=np.ones((1, 128), dtype=np.float32),
        ind_gu=ind_gu, ind_dn=ind_dn,
    )
    return nz_of_t, n_mask, common


def kernel(**inputs):
    nz_of_t, n_mask, common = _host_prep(inputs)
    x = np.asarray(inputs["L0_attn_x"], np.float32).reshape(SEQ, D)
    nc = build_program(nz_of_t, n_mask)
    in_maps = []
    for c in range(NCORES):
        m = dict(common)
        m["x"] = np.ascontiguousarray(x[T * c:T * (c + 1), :])
        in_maps.append(m)
    res = run_bass_kernel_spmd(nc, in_maps, core_ids=list(range(NCORES)))
    out = np.concatenate([res.results[c]["out"] for c in range(NCORES)], axis=0)
    return out.reshape(1, SEQ, D).astype(np.float32)
